# version 06
# baseline (speedup 1.0000x reference)
import contextlib

import numpy as np
import ml_dtypes

import concourse.bass as bass
import concourse.mybir as mybir
from concourse.bass_utils import run_bass_kernel_spmd

F32 = mybir.dt.float32
BF16 = mybir.dt.bfloat16
AF = mybir.ActivationFunctionType
ALU = mybir.AluOpType
AX = mybir.AxisListType.X

NEG = -9.0e15
P = 128
LNEPS = 1.0e-5
ACTEPS = 1.0e-5

COMPUTE_OPS = {
    "tensor_scalar", "tensor_tensor", "scalar_tensor_tensor", "tensor_copy",
    "tensor_add", "tensor_sub", "tensor_mul", "tensor_max", "reduce_sum",
    "reduce_max", "reciprocal", "memset", "activation", "iota", "transpose",
    "matmul", "tensor_single_scalar",
}

ENG_COL = {"sync": 0, "vector": 1, "scalar": 2, "gpsimd": 3, "tensor": 4}


class TEng:

    def __init__(self, e, sem, nbox, fence_ap):
        object.__setattr__(self, "_e", e)
        object.__setattr__(self, "_sem", sem)
        object.__setattr__(self, "_n", nbox)
        object.__setattr__(self, "_fence", fence_ap)

    @property
    def n(self):
        return self._n[0]

    def __getattr__(self, nm):
        f = getattr(self._e, nm)
        if nm in COMPUTE_OPS:
            def w(*a, **k):
                if self._n[0]:
                    self._e.wait_ge(self._sem, self._n[0])
                inst = f(*a, **k)
                inst.then_inc(self._sem, 1)
                self._n[0] += 1
                return inst
            return w
        return f

    def fence(self):
        if self._n[0]:
            self._e.wait_ge(self._sem, self._n[0])
        self._e.memset(self._fence, 0.0).then_inc(self._sem, 1)
        self._n[0] += 1


class Cfg:
    def __init__(self, C=2048, F=8192, H=16, NOWN=1024):
        self.C, self.F, self.H, self.NOWN = C, F, H, NOWN
        self.D = P
        assert C == H * self.D and NOWN % P == 0 and F % 512 == 0 and C % 512 == 0
        self.CC = C // P
        self.FC = F // P
        self.NT = NOWN // P
        self.TALL = 2 * NOWN
        self.MT = self.TALL // P
        self.QOFF = self.MT - self.NT
        self.HG = min(4, H)
        self.NG = H // self.HG
        self.SCALE = float(self.D) ** -0.5
        self.KL = [self.QOFF + i + 1 for i in range(self.NT)]
        self.soff = [0] * self.NT
        for i in range(1, self.NT):
            self.soff[i] = self.soff[i - 1] + self.KL[i - 1] * P
        self.SLAB_S = self.soff[-1] + self.KL[-1] * P


class GSem:
    def __init__(self, h):
        self.h = h
        self.n = 0


class SemView:

    def __init__(self, g):
        self.g = g
        self.base = g.n

    @property
    def h(self):
        return self.g.h

    def inc(self, k=16):
        self.g.n += k

    def val(self, rel):
        return self.base + rel


class Builder:
    def __init__(self, cfg: Cfg):
        self.cfg = cfg
        self.nc = bass.Bass("TRN2", target_bir_lowering=False)
        self.stack = contextlib.ExitStack()
        self._semi = 0
        self.fnc = None
        self._gsems = {}
        self._tickets = {}

    def sem(self, tag=""):
        self._semi += 1
        return self.stack.enter_context(self.nc.semaphore(f"s{self._semi}{tag}"))

    def gsem(self, key):
        if key not in self._gsems:
            self._gsems[key] = GSem(self.sem(key))
        return SemView(self._gsems[key])

    def ticket(self, eng):
        if eng not in self._tickets:
            self._tickets[eng] = (self.sem("tk" + eng[0]), [0])
        return self._tickets[eng]

    def TE(self, e, eng):
        sem, nbox = self.ticket(eng)
        return TEng(e, sem, nbox,
                    self.fnc[0:1, ENG_COL[eng]:ENG_COL[eng] + 1])

    def sbuf(self, name, shape, dt):
        return self.stack.enter_context(self.nc.sbuf_tensor(name, shape, dt))

    def chain(self, name, steps):
        prev = [None]
        with self.nc.Block() as block:
            for eng, fn in steps:
                sem, nbox = self.ticket(eng)
                n_before = nbox[0]

                def sect(e, fn=fn, sem=sem, nbox=nbox, pv=prev[0], eng=eng):
                    T = TEng(e, sem, nbox,
                             self.fnc[0:1, ENG_COL[eng]:ENG_COL[eng] + 1])
                    if pv is not None and pv[0] is not sem:
                        e.wait_ge(pv[0], pv[1])
                    fn(T)
                getattr(block, eng)(sect)
                if nbox[0] > n_before:
                    prev[0] = (sem, nbox[0])


def build(cfg: Cfg):
    B = Builder(cfg)
    nc = B.nc
    C, F, H, NOWN = cfg.C, cfg.F, cfg.H, cfg.NOWN
    CC, FC, NT, MT, TALL = cfg.CC, cfg.FC, cfg.NT, cfg.MT, cfg.TALL
    HG, NG, QOFF = cfg.HG, cfg.NG, cfg.QOFF

    x_all = nc.dram_tensor("x_all", [TALL, C], F32, kind="ExternalInput")
    w_qkv = nc.dram_tensor("w_qkv", [3 * C, C], F32, kind="ExternalInput")
    w_proj = nc.dram_tensor("w_proj", [C, C], F32, kind="ExternalInput")
    w1 = nc.dram_tensor("w1", [F, C], F32, kind="ExternalInput")
    b1 = nc.dram_tensor("b1", [1, F], F32, kind="ExternalInput")
    w2 = nc.dram_tensor("w2", [C, F], F32, kind="ExternalInput")
    b2 = nc.dram_tensor("b2", [1, C], F32, kind="ExternalInput")
    masks = nc.dram_tensor("masks", [P, cfg.SLAB_S], BF16, kind="ExternalInput")
    ident_b = nc.dram_tensor("ident_b", [P, P], BF16, kind="ExternalInput")
    ident_f = nc.dram_tensor("ident_f", [P, P], F32, kind="ExternalInput")
    out = nc.dram_tensor("out", [NOWN, C], F32, kind="ExternalOutput")
    _tap = "ExternalOutput" if globals().get("_DEBUG_TAPS") else "Internal"
    xqa_d = nc.dram_tensor("xqa_d", [TALL, C], BF16, kind=_tap)
    xqh_d = nc.dram_tensor("xqh_d", [NOWN, F], BF16, kind=_tap)
    h_d = nc.dram_tensor("h_d", [NOWN, F], BF16, kind=_tap)

    B.fnc = B.sbuf("fnc", [1, 8], F32)
    idb = B.sbuf("idb", [P, P], BF16)
    idf = B.sbuf("idf", [P, P], F32)
    onesrow = B.sbuf("onesrow", [1, P], F32)
    wsc = B.sbuf("wsc", [P, 8], F32)
    aff_a = B.sbuf("aff_a", [P, NT], F32)
    aff_b = B.sbuf("aff_b", [P, NT], F32)
    dq1 = B.sbuf("dq1", [P, NT], F32)
    isc1 = B.sbuf("isc1", [P, NT], F32)
    sr1 = B.sbuf("sr1", [1, NT * P], F32)
    dq2 = B.sbuf("dq2", [P, NT], F32)
    isc2 = B.sbuf("isc2", [P, NT], F32)
    sr2 = B.sbuf("sr2", [1, NT * P], F32)
    dqv = B.sbuf("dqv", [P, MT], F32)
    dqp = B.sbuf("dqp", [P, NT], F32)
    OSL1 = F // 512
    hsum = B.sbuf("hsum", [P, NT * OSL1], F32)
    hsq = B.sbuf("hsq", [P, NT * OSL1], F32)
    hmx = B.sbuf("hmx", [P, NT * OSL1], F32)
    hmn = B.sbuf("hmn", [P, NT * OSL1], F32)

    with nc.Block() as block:
        ld0 = B.sem("p0")

        @block.sync
        def _(s):
            s.dma_start(out=idb[:, :], in_=ident_b[:, :]).then_inc(ld0, 16)
            s.dma_start(out=idf[:, :], in_=ident_f[:, :]).then_inc(ld0, 16)
            s.wait_ge(ld0, 32)

        @block.vector
        def _(v):
            v.memset(onesrow[:, :], 1.0)

    def weight_scale(tag, w_dram, rows, cols, eps, scol):
        RT = rows // P
        ld = [B.gsem("gldA"), B.gsem("gldB")]
        vts, vnb = B.ticket("vector")
        vbase = vnb[0]
        with (
            nc.sbuf_tensor(tag + "wb", [P, 2 * cols], F32) as wb,
            nc.sbuf_tensor(tag + "ab", [P, cols], F32) as ab,
            nc.sbuf_tensor(tag + "cs", [P, RT], F32) as cs,
        ):
            with nc.Block() as block:
                @block.gpsimd
                def _(g):
                    for t in range(RT):
                        if t >= 2:
                            g.wait_ge(vts, vbase + 2 * (t - 1))
                        g.dma_start(out=wb[:, (t % 2) * cols:(t % 2 + 1) * cols],
                                    in_=w_dram[t * P:(t + 1) * P, :]
                                    ).then_inc(ld[t % 2].h, 16)
                        ld[t % 2].inc(16)

                @block.vector
                def _(v):
                    T = B.TE(v, "vector")
                    for t in range(RT):
                        v.wait_ge(ld[t % 2].h, ld[t % 2].val(16 * (t // 2 + 1)))
                        src = wb[:, (t % 2) * cols:(t % 2 + 1) * cols]
                        T.scalar_tensor_tensor(ab[:, :], src, -1.0, src,
                                               ALU.mult, ALU.max)
                        T.tensor_scalar(ab[:, :], ab[:, :], 1.0, None,
                                        ALU.mult, ALU.add,
                                        accum_out=cs[:, t:t + 1])
                    assert T.n == vbase + 2 * RT

            with (
                nc.sbuf_tensor(tag + "sc", [P, 1], F32) as sc,
                nc.sbuf_tensor(tag + "r1", [1, 2], F32) as r1,
                nc.psum_tensor(tag + "pt", [1, P], F32) as pt,
                nc.psum_tensor(tag + "pb", [P, 2], F32) as pb,
            ):
                def s1(T):
                    T.reduce_sum(sc[:, 0:1], cs[:, :], axis=AX)

                def s2(T):
                    T.transpose(pt[0:1, :], sc[:, 0:1], idf[:, :])

                def s3(T):
                    T.reduce_sum(r1[0:1, 0:1], pt[0:1, :], axis=AX)
                    T.tensor_scalar(r1[0:1, 0:1], r1[0:1, 0:1],
                                    float(1.0 / (rows * cols)), float(eps),
                                    ALU.mult, ALU.max)
                    T.reciprocal(r1[0:1, 1:2], r1[0:1, 0:1])

                def s4(T):
                    T.matmul(pb[:, 0:2], onesrow[0:1, :], r1[0:1, 0:2],
                             start=True, stop=True)

                def s5(T):
                    T.tensor_copy(wsc[:, scol:scol + 2], pb[:, 0:2])

                B.chain(tag + "t", [("vector", s1), ("tensor", s2),
                                    ("vector", s3), ("tensor", s4),
                                    ("vector", s5)])

    def q_steps(src_ap, w, alpha, beta, xq_ap, scr):
        return [
            ("vector", lambda T: T.tensor_scalar(
                scr["v"][:, 0:w], src_ap, alpha, beta, ALU.mult, ALU.add)),
            ("scalar", lambda T: T.activation(
                scr["sg"][:, 0:w], scr["v"][:, 0:w], AF.Sign,
                bias=0.0, scale=1.0)),
            ("vector", lambda T: T.scalar_tensor_tensor(
                scr["av"][:, 0:w], scr["v"][:, 0:w], -1.0, scr["v"][:, 0:w],
                ALU.mult, ALU.max)),
            ("vector", lambda T: T.tensor_scalar(
                scr["ub"][:, 0:w], scr["av"][:, 0:w], 128.0, None, ALU.add)),
            ("vector", lambda T: T.tensor_scalar(
                scr["ub"][:, 0:w], scr["ub"][:, 0:w], 128.0, None,
                ALU.subtract)),
            ("vector", lambda T: T.tensor_tensor(
                xq_ap, scr["sg"][:, 0:w], scr["ub"][:, 0:w], ALU.mult)),
        ]

    def ln_steps(src_ap, w, sb_, sq, cm, cr, t0, t1):
        def fin(T):
            T.tensor_scalar(sb_[:, cm:cm + 1], sb_[:, t0:t0 + 1], 1.0 / w,
                            None, ALU.mult)
            T.tensor_scalar(sb_[:, t1:t1 + 1], sb_[:, t1:t1 + 1], 1.0 / w,
                            None, ALU.mult)
            T.tensor_tensor(sb_[:, t0:t0 + 1], sb_[:, cm:cm + 1],
                            sb_[:, cm:cm + 1], ALU.mult)
            T.tensor_tensor(sb_[:, t1:t1 + 1], sb_[:, t1:t1 + 1],
                            sb_[:, t0:t0 + 1], ALU.subtract)
            T.tensor_scalar(sb_[:, t1:t1 + 1], sb_[:, t1:t1 + 1], 1.0, LNEPS,
                            ALU.mult, ALU.add)
        return [
            ("vector", lambda T: T.tensor_scalar(
                sq[:, 0:w], src_ap, 1.0, None, ALU.mult, ALU.add,
                accum_out=sb_[:, t0:t0 + 1])),
            ("scalar", lambda T: T.activation(
                sq[:, 0:w], src_ap, AF.Square, bias=0.0, scale=1.0,
                accum_out=sb_[:, t1:t1 + 1])),
            ("vector", fin),
            ("scalar", lambda T: T.activation(
                sb_[:, t0:t0 + 1], sb_[:, t1:t1 + 1], AF.Sqrt,
                bias=0.0, scale=1.0)),
            ("vector", lambda T: T.reciprocal(sb_[:, cr:cr + 1],
                                              sb_[:, t0:t0 + 1])),
        ]

    def qscale_steps(src_ap, w, sb_, sq, cm, cr, cs_, t0, t1):
        def fin(T):
            T.tensor_tensor(sb_[:, t0:t0 + 1], sb_[:, t0:t0 + 1],
                            sb_[:, cm:cm + 1], ALU.subtract)
            T.tensor_tensor(sb_[:, t1:t1 + 1], sb_[:, t1:t1 + 1],
                            sb_[:, cm:cm + 1], ALU.add)
            T.tensor_tensor(sb_[:, t0:t0 + 1], sb_[:, t0:t0 + 1],
                            sb_[:, t1:t1 + 1], ALU.max)
            T.tensor_tensor(sb_[:, t0:t0 + 1], sb_[:, t0:t0 + 1],
                            sb_[:, cr:cr + 1], ALU.mult)
            T.tensor_scalar(sb_[:, t0:t0 + 1], sb_[:, t0:t0 + 1], 1.0,
                            ACTEPS, ALU.mult, ALU.max)
            T.reciprocal(sb_[:, t1:t1 + 1], sb_[:, t0:t0 + 1])
            T.tensor_scalar(sb_[:, cs_:cs_ + 1], sb_[:, t1:t1 + 1], 127.0,
                            None, ALU.mult)
            T.tensor_scalar(sb_[:, t0:t0 + 1], sb_[:, t0:t0 + 1],
                            1.0 / 127.0, None, ALU.mult)
        return [
            ("vector", lambda T: T.tensor_scalar(
                sq[:, 0:w], src_ap, 1.0, None, ALU.mult, ALU.max,
                accum_out=sb_[:, t0:t0 + 1])),
            ("vector", lambda T: T.tensor_scalar(
                sq[:, 0:w], src_ap, -1.0, None, ALU.mult, ALU.max,
                accum_out=sb_[:, t1:t1 + 1])),
            ("vector", fin),
        ]

    def wload(eng, wr, slot, W, r0, c0, sem):
        for q in range(4):
            eng.dma_start(
                out=wr[:, slot * 512 + q * P:slot * 512 + (q + 1) * P],
                in_=W[r0 + q * P:r0 + (q + 1) * P, c0:c0 + P]
            ).then_inc(sem.h, 16)
            sem.inc(16)

    def wtrans(eng, wq_, dcol, wt, slot, sem):
        for q in range(4):
            eng.dma_start(
                out=wq_[:, dcol + q * P:dcol + (q + 1) * P],
                in_=wt[:, slot * 512 + q * P:slot * 512 + (q + 1) * P],
                transpose=True).then_inc(sem.h, 16)
            sem.inc(16)

    def tern3(T, src, dst, scr_a, rcol):
        T.tensor_scalar(scr_a, src, wsc[:, rcol:rcol + 1], 0.5,
                        ALU.mult, ALU.is_gt)
        T.tensor_scalar(dst, src, wsc[:, rcol:rcol + 1], -0.5,
                        ALU.mult, ALU.is_lt)
        T.tensor_tensor(dst, scr_a, dst, ALU.subtract)

    def phase_a1(xqf1T):
        with contextlib.ExitStack() as S:
            t = lambda n, sh, d: S.enter_context(nc.sbuf_tensor(n, sh, d))
            xb = t("a1x", [P, 2 * C], F32)
            xnb = t("a1xn", [P, C], F32)
            sq = t("a1sq", [P, C], F32)
            sb_ = t("a1st", [P, 12], F32)
            scr = {"v": t("a1v", [P, C], F32), "sg": t("a1sg", [P, C], BF16),
                   "av": t("a1av", [P, C], F32), "ub": t("a1ub", [P, C], BF16)}
            xqb = t("a1xq", [P, C], BF16)
            pt = S.enter_context(nc.psum_tensor("a1pt", [P, 8 * P], BF16))
            ps = S.enter_context(nc.psum_tensor("a1ps", [1, 2 * P], F32))
            dmas = [B.gsem("dmA"), B.gsem("dmB")]
            steps = []

            def mk_dma(i):
                def f(T):
                    T._e.dma_start(
                        out=xb[:, (i % 2) * C:(i % 2 + 1) * C],
                        in_=x_all[NOWN + i * P:NOWN + (i + 1) * P, :]
                    ).then_inc(dmas[i % 2].h, 16)
                    dmas[i % 2].inc(16)
                return f

            steps.append(("sync", mk_dma(0)))
            for i in range(NT):
                if i + 1 < NT:
                    steps.append(("sync", mk_dma(i + 1)))
                steps.append(("vector",
                              lambda T, s_=dmas[i % 2], n=16 * (i // 2 + 1):
                              T._e.wait_ge(s_.h, s_.val(n))))
                x_t = xb[:, (i % 2) * C:(i % 2 + 1) * C]
                steps += ln_steps(x_t, C, sb_, sq, 0, 1, 8, 9)

                def mkxn(T, x_t=x_t):
                    T.tensor_tensor(sb_[:, 8:9], sb_[:, 0:1], sb_[:, 1:2],
                                    ALU.mult)
                    T.tensor_scalar(sb_[:, 8:9], sb_[:, 8:9], -1.0, None,
                                    ALU.mult)
                    T.tensor_scalar(xnb[:, :], x_t, sb_[:, 1:2], sb_[:, 8:9],
                                    ALU.mult, ALU.add)
                steps.append(("vector", mkxn))
                steps += ln_steps(xnb[:, :], C, sb_, sq, 2, 3, 8, 9)
                steps += qscale_steps(xnb[:, :], C, sb_, sq, 2, 3, 4, 8, 9)

                def slabs(T, i=i):
                    T.tensor_tensor(aff_a[:, i:i + 1], sb_[:, 3:4], sb_[:, 4:5],
                                    ALU.mult)
                    T.tensor_tensor(sb_[:, 9:10], sb_[:, 2:3],
                                    aff_a[:, i:i + 1], ALU.mult)
                    T.tensor_scalar(aff_b[:, i:i + 1], sb_[:, 9:10], -1.0,
                                    None, ALU.mult)
                    T.tensor_tensor(dq1[:, i:i + 1], sb_[:, 8:9], wsc[:, 0:1],
                                    ALU.mult)
                    T.tensor_tensor(isc1[:, i:i + 1], sb_[:, 4:5], wsc[:, 1:2],
                                    ALU.mult)
                steps.append(("vector", slabs))
                steps += q_steps(xnb[:, :], C, aff_a[:, i:i + 1],
                                 aff_b[:, i:i + 1], xqb[:, :], scr)
                for g0 in range(0, CC, 8):
                    nb = min(8, CC - g0)

                    def tr(T, g0=g0, nb=nb):
                        for k in range(nb):
                            T.transpose(pt[:, k * P:(k + 1) * P],
                                        xqb[:, (g0 + k) * P:(g0 + k + 1) * P],
                                        idb[:, :])
                    steps.append(("tensor", tr))

                    def ev(T, i=i, g0=g0, nb=nb):
                        T.tensor_copy(
                            bass.AP(xqf1T, g0 * NOWN + i * P,
                                    [[CC * NOWN, P], [NOWN, nb], [1, P]]),
                            pt[:, 0:nb * P])
                    steps.append(("vector", ev))
            for i in range(NT):
                steps.append(("tensor", lambda T, i=i: T.transpose(
                    ps[0:1, (i % 2) * P:(i % 2 + 1) * P],
                    isc1[:, i:i + 1], idf[:, :])))
                steps.append(("vector", lambda T, i=i: T.tensor_copy(
                    sr1[0:1, i * P:(i + 1) * P],
                    ps[0:1, (i % 2) * P:(i % 2 + 1) * P])))
            B.chain("pa1", steps)

    def phase_f1(xqf1T):
        OSL = F // 512
        NK = OSL * NT
        vt_tern_plan = {}
        vt_stat_plan = {}
        cnt = 0
        for o in range(OSL + 2):
            if o >= 2:
                for i in range(NT):
                    cnt += 3
                    vt_stat_plan[(o - 2) * NT + i] = cnt
            if o < OSL:
                for cc in range(CC):
                    cnt += 3
                    vt_tern_plan[o * CC + cc] = cnt
        vt_total = cnt
        with contextlib.ExitStack() as S:
            t = lambda n, sh, d: S.enter_context(nc.sbuf_tensor(n, sh, d))
            wb = t("f1wb", [P, 2 * 512], F32)
            wt = t("f1wt", [P, 2 * 512], BF16)
            asc = t("f1as", [P, 512], BF16)
            wq = t("f1wq", [P, 2 * CC * 512], BF16)
            hb = t("f1h", [P, 4 * 512], BF16)
            sqs = t("f1sq", [P, 4 * 512], F32)
            b1s = t("b1s", [1, F], F32)
            pp = S.enter_context(nc.psum_tensor("f1p", [P, 2 * 512], F32))
            ld = [B.gsem("ldA"), B.gsem("ldB")]
            tq = [B.gsem("tqA"), B.gsem("tqB")]
            so = [B.gsem(f"gso{q}") for q in range(4)]
            mm = B.gsem("mm")
            vts, vnb = B.ticket("vector")
            ats, anb = B.ticket("scalar")
            vbase, abase = vnb[0], anb[0]
            b1l = B.gsem("bld")
            NK3 = OSL * CC
            with nc.Block() as block:
                @block.sync
                def _(s):
                    s.dma_start(out=b1s[:, :], in_=b1[:, :]).then_inc(b1l.h, 16)
                    b1l.inc(16)
                    for k3 in range(min(2, NK3)):
                        o, cc = divmod(k3, CC)
                        wload(s, wb, k3 % 2, w1, o * 512, cc * P, ld[k3 % 2])
                    for k3 in range(NK3):
                        o, cc = divmod(k3, CC)
                        s.wait_ge(vts, vbase + vt_tern_plan[k3])
                        if cc == 0 and o >= 2:
                            s.wait_ge(mm.h, mm.val((o - 1) * NT))
                        wtrans(s, wq, ((o % 2) * CC + cc) * 512, wt,
                               k3 % 2, tq[k3 % 2])
                        if k3 + 2 < NK3:
                            o2, c2 = divmod(k3 + 2, CC)
                            wload(s, wb, k3 % 2, w1, o2 * 512, c2 * P,
                                  ld[k3 % 2])

                @block.vector
                def _(v):
                    T = B.TE(v, "vector")
                    for o in range(OSL + 2):
                        if o >= 2:
                            for i in range(NT):
                                k = (o - 2) * NT + i
                                v.wait_ge(ats, abase + 2 * k + 2)
                                hsl = hb[:, (k % 4) * 512:(k % 4 + 1) * 512]
                                slot = i * OSL + (o - 2)
                                T.tensor_scalar(sqs[:, 0:512], hsl, 1.0, None,
                                                ALU.mult, ALU.add,
                                                accum_out=hsum[:, slot:slot + 1])
                                T.tensor_scalar(sqs[:, 512:1024], hsl, 1.0,
                                                None, ALU.mult, ALU.max,
                                                accum_out=hmx[:, slot:slot + 1])
                                T.tensor_scalar(sqs[:, 1024:1536], hsl, -1.0,
                                                None, ALU.mult, ALU.max,
                                                accum_out=hmn[:, slot:slot + 1])
                                assert T.n == vbase + vt_stat_plan[k]
                        if o < OSL:
                            for cc in range(CC):
                                k3 = o * CC + cc
                                v.wait_ge(ld[k3 % 2].h,
                                          ld[k3 % 2].val(64 * (k3 // 2 + 1)))
                                if k3 >= 2:
                                    v.wait_ge(tq[k3 % 2].h,
                                              tq[k3 % 2].val(64 * (k3 // 2)))
                                tern3(T,
                                      wb[:, (k3 % 2) * 512:(k3 % 2 + 1) * 512],
                                      wt[:, (k3 % 2) * 512:(k3 % 2 + 1) * 512],
                                      asc[:, :], 1)
                                assert T.n == vbase + vt_tern_plan[k3]
                    assert T.n == vbase + vt_total

                @block.tensor
                def _(te):
                    te.wait_ge(b1l.h, b1l.val(16))
                    for o in range(OSL):
                        ke, ko = o * CC + CC - 2, o * CC + CC - 1
                        te.wait_ge(tq[0].h, tq[0].val(64 * (ke // 2 + 1)))
                        te.wait_ge(tq[1].h, tq[1].val(64 * (ko // 2 + 1)))
                        for i in range(NT):
                            k = o * NT + i
                            if k >= 2:
                                te.wait_ge(ats, abase + 2 * (k - 2) + 1)
                            psl = pp[:, (k % 2) * 512:(k % 2 + 1) * 512]
                            for cc in range(CC):
                                te.matmul(
                                    psl,
                                    xqf1T[:, cc * NOWN + i * P:
                                          cc * NOWN + (i + 1) * P],
                                    wq[:, ((o % 2) * CC + cc) * 512:
                                       ((o % 2) * CC + cc + 1) * 512],
                                    start=(cc == 0), stop=False)
                            te.matmul(psl, sr1[0:1, i * P:(i + 1) * P],
                                      b1s[0:1, o * 512:(o + 1) * 512],
                                      start=False, stop=True).then_inc(mm.h, 1)
                            mm.inc(1)

                @block.scalar
                def _(s):
                    T = B.TE(s, "scalar")
                    for k in range(NK):
                        o, i = divmod(k, NT)
                        s.wait_ge(mm.h, mm.val(k + 1))
                        if k >= 4:
                            s.wait_ge(so[k % 4].h, so[k % 4].val(16 * (k // 4)))
                            s.wait_ge(vts, vbase + vt_stat_plan[k - 4])
                        psl = pp[:, (k % 2) * 512:(k % 2 + 1) * 512]
                        hsl = hb[:, (k % 4) * 512:(k % 4 + 1) * 512]
                        T.activation(hsl, psl, AF.Gelu, bias=0.0,
                                     scale=dq1[:, i:i + 1])
                        T.activation(sqs[:, 1536:2048], hsl, AF.Square,
                                     bias=0.0, scale=1.0,
                                     accum_out=hsq[:, i * OSL + o:
                                                   i * OSL + o + 1])
                    assert T.n == abase + 2 * NK

                @block.gpsimd
                def _(g):
                    for k in range(NK):
                        o, i = divmod(k, NT)
                        g.wait_ge(vts, vbase + vt_stat_plan[k])
                        g.dma_start(out=h_d[i * P:(i + 1) * P,
                                            o * 512:(o + 1) * 512],
                                    in_=hb[:, (k % 4) * 512:(k % 4 + 1) * 512]
                                    ).then_inc(so[k % 4].h, 16)
                        so[k % 4].inc(16)
                    for sl in range(4):
                        g.wait_ge(so[sl].h,
                                  so[sl].val(16 * ((NK - 1 - sl) // 4 + 1)))

    def phase_h():
        OSL = F // 512
        with contextlib.ExitStack() as S:
            t = lambda n, sh, d: S.enter_context(nc.sbuf_tensor(n, sh, d))
            hb = t("phh", [P, 2 * F], BF16)
            sb_ = t("phst", [P, 12], F32)
            scr = {"v": t("phv", [P, F], F32), "sg": t("phsg", [P, F], BF16),
                   "av": t("phav", [P, F], F32), "ub": t("phub", [P, F], BF16)}
            ps = S.enter_context(nc.psum_tensor("phps", [1, 2 * P], F32))
            dmas = [B.gsem("dmA"), B.gsem("dmB")]
            sto = [B.gsem("stA"), B.gsem("stB")]
            steps = []

            def mk_dma(i):
                def f(T):
                    if i >= 2:
                        T._e.wait_ge(sto[i % 2].h,
                                     sto[i % 2].val(16 * ((i - 2) // 2 + 1)))
                    T._e.dma_start(out=hb[:, (i % 2) * F:(i % 2 + 1) * F],
                                   in_=h_d[i * P:(i + 1) * P, :]
                                   ).then_inc(dmas[i % 2].h, 16)
                    dmas[i % 2].inc(16)
                return f

            steps.append(("sync", mk_dma(0)))
            for i in range(NT):
                if i + 1 < NT:
                    steps.append(("sync", mk_dma(i + 1)))
                steps.append(("vector",
                              lambda T, s_=dmas[i % 2], n=16 * (i // 2 + 1):
                              T._e.wait_ge(s_.h, s_.val(n))))
                h_t = hb[:, (i % 2) * F:(i % 2 + 1) * F]

                def agg(T, i=i):
                    T.reduce_sum(sb_[:, 8:9], hsum[:, i * OSL:(i + 1) * OSL],
                                 axis=AX)
                    T.reduce_sum(sb_[:, 9:10], hsq[:, i * OSL:(i + 1) * OSL],
                                 axis=AX)
                    T.reduce_max(sb_[:, 10:11], hmx[:, i * OSL:(i + 1) * OSL],
                                 axis=AX)
                    T.reduce_max(sb_[:, 11:12], hmn[:, i * OSL:(i + 1) * OSL],
                                 axis=AX)
                    T.tensor_scalar(sb_[:, 0:1], sb_[:, 8:9], 1.0 / F, None,
                                    ALU.mult)
                    T.tensor_scalar(sb_[:, 9:10], sb_[:, 9:10], 1.0 / F, None,
                                    ALU.mult)
                    T.tensor_tensor(sb_[:, 8:9], sb_[:, 0:1], sb_[:, 0:1],
                                    ALU.mult)
                    T.tensor_tensor(sb_[:, 9:10], sb_[:, 9:10], sb_[:, 8:9],
                                    ALU.subtract)
                    T.tensor_scalar(sb_[:, 9:10], sb_[:, 9:10], 1.0, LNEPS,
                                    ALU.mult, ALU.add)
                steps.append(("vector", agg))
                steps.append(("scalar", lambda T: T.activation(
                    sb_[:, 8:9], sb_[:, 9:10], AF.Sqrt, bias=0.0, scale=1.0)))

                def fin(T, i=i):
                    T.reciprocal(sb_[:, 1:2], sb_[:, 8:9])
                    T.tensor_tensor(sb_[:, 10:11], sb_[:, 10:11], sb_[:, 0:1],
                                    ALU.subtract)
                    T.tensor_tensor(sb_[:, 11:12], sb_[:, 11:12], sb_[:, 0:1],
                                    ALU.add)
                    T.tensor_tensor(sb_[:, 10:11], sb_[:, 10:11],
                                    sb_[:, 11:12], ALU.max)
                    T.tensor_tensor(sb_[:, 10:11], sb_[:, 10:11], sb_[:, 1:2],
                                    ALU.mult)
                    T.tensor_scalar(sb_[:, 10:11], sb_[:, 10:11], 1.0, ACTEPS,
                                    ALU.mult, ALU.max)
                    T.reciprocal(sb_[:, 11:12], sb_[:, 10:11])
                    T.tensor_scalar(sb_[:, 4:5], sb_[:, 11:12], 127.0, None,
                                    ALU.mult)
                    T.tensor_scalar(sb_[:, 10:11], sb_[:, 10:11], 1.0 / 127.0,
                                    None, ALU.mult)
                    T.tensor_tensor(sb_[:, 2:3], sb_[:, 1:2], sb_[:, 4:5],
                                    ALU.mult)
                    T.tensor_tensor(sb_[:, 3:4], sb_[:, 0:1], sb_[:, 2:3],
                                    ALU.mult)
                    T.tensor_scalar(sb_[:, 3:4], sb_[:, 3:4], -1.0, None,
                                    ALU.mult)
                    T.tensor_tensor(dq2[:, i:i + 1], sb_[:, 10:11],
                                    wsc[:, 2:3], ALU.mult)
                    T.tensor_tensor(isc2[:, i:i + 1], sb_[:, 4:5], wsc[:, 3:4],
                                    ALU.mult)
                steps.append(("vector", fin))
                steps += q_steps(h_t, F, sb_[:, 2:3], sb_[:, 3:4], h_t, scr)

                def sto_f(T, i=i):
                    T._e.dma_start(out=xqh_d[i * P:(i + 1) * P, :],
                                   in_=hb[:, (i % 2) * F:(i % 2 + 1) * F]
                                   ).then_inc(sto[i % 2].h, 16)
                    sto[i % 2].inc(16)
                steps.append(("scalar", sto_f))
            def wait_all_sto(T):
                for sl in range(2):
                    T._e.wait_ge(sto[sl].h,
                                 sto[sl].val(16 * ((NT - 1 - sl) // 2 + 1)))
            steps.append(("scalar", wait_all_sto))
            for i in range(NT):
                steps.append(("tensor", lambda T, i=i: T.transpose(
                    ps[0:1, (i % 2) * P:(i % 2 + 1) * P],
                    isc2[:, i:i + 1], idf[:, :])))
                steps.append(("vector", lambda T, i=i: T.tensor_copy(
                    sr2[0:1, i * P:(i + 1) * P],
                    ps[0:1, (i % 2) * P:(i % 2 + 1) * P])))
            B.chain("ph", steps)

    def phase_f2(ffn_acc):
        OSL = C // 512
        NCH = OSL * FC

        def mmv(k):
            o, fc = divmod(k, FC)
            return (o * (FC + 1) + fc + 1) * NT

        vt_tern_plan = {}
        vt_evac_plan = {}
        cnt = 0
        for o in range(OSL):
            for fc in range(FC):
                if fc == 2 and o >= 1:
                    cnt += NT
                    vt_evac_plan[o - 1] = cnt
                cnt += 3
                vt_tern_plan[o * FC + fc] = cnt
        cnt += NT
        vt_evac_plan[OSL - 1] = cnt
        vt_total = cnt

        with contextlib.ExitStack() as S:
            t = lambda n, sh, d: S.enter_context(nc.sbuf_tensor(n, sh, d))
            wb = t("f2wb", [P, 2 * 512], F32)
            wt = t("f2wt", [P, 2 * 512], BF16)
            wq = t("f2wq", [P, 2 * 512], BF16)
            asc = t("f2as", [P, 512], BF16)
            xb = t("f2xb", [P, 2 * NOWN], BF16)
            b2s = t("b2s", [1, C], F32)
            pp = S.enter_context(nc.psum_tensor("f2p", [P, 8 * 512], F32))
            ld = [B.gsem("ldA"), B.gsem("ldB")]
            tq = [B.gsem("tqA"), B.gsem("tqB")]
            xl = [B.gsem("xlA"), B.gsem("xlB")]
            mm = B.gsem("mm")
            vts, vnb = B.ticket("vector")
            vbase = vnb[0]
            b2l = B.gsem("bld")
            with nc.Block() as block:
                @block.sync
                def _(s):
                    s.dma_start(out=b2s[:, :], in_=b2[:, :]).then_inc(b2l.h, 16)
                    b2l.inc(16)
                    for k in range(min(2, NCH)):
                        o, fc = divmod(k, FC)
                        wload(s, wb, k % 2, w2, o * 512, fc * P, ld[k % 2])
                    for k in range(NCH):
                        o, fc = divmod(k, FC)
                        if k >= 2:
                            s.wait_ge(mm.h, mm.val(mmv(k - 2)))
                        s.dma_start(
                            out=xb[:, (k % 2) * NOWN:(k % 2 + 1) * NOWN],
                            in_=xqh_d[0:NOWN, fc * P:(fc + 1) * P],
                            transpose=True).then_inc(xl[k % 2].h, 16)
                        xl[k % 2].inc(16)
                        s.wait_ge(vts, vbase + vt_tern_plan[k])
                        wtrans(s, wq, (k % 2) * 512, wt, k % 2, tq[k % 2])
                        if k + 2 < NCH:
                            o2, f2c = divmod(k + 2, FC)
                            wload(s, wb, k % 2, w2, o2 * 512, f2c * P,
                                  ld[k % 2])

                @block.vector
                def _(v):
                    T = B.TE(v, "vector")

                    def evac(o):
                        v.wait_ge(mm.h, mm.val((o + 1) * (FC + 1) * NT))
                        for i in range(NT):
                            T.tensor_scalar(
                                ffn_acc[:, i * C + o * 512:
                                        i * C + (o + 1) * 512],
                                pp[:, i * 512:(i + 1) * 512], dq2[:, i:i + 1],
                                None, ALU.mult)
                    for o in range(OSL):
                        for fc in range(FC):
                            k = o * FC + fc
                            if fc == 2 and o >= 1:
                                evac(o - 1)
                                assert T.n == vbase + vt_evac_plan[o - 1]
                            v.wait_ge(ld[k % 2].h,
                                      ld[k % 2].val(64 * (k // 2 + 1)))
                            if k >= 2:
                                v.wait_ge(tq[k % 2].h,
                                          tq[k % 2].val(64 * (k // 2)))
                            tern3(T, wb[:, (k % 2) * 512:(k % 2 + 1) * 512],
                                  wt[:, (k % 2) * 512:(k % 2 + 1) * 512],
                                  asc[:, :], 3)
                            assert T.n == vbase + vt_tern_plan[k]
                    evac(OSL - 1)
                    assert T.n == vbase + vt_total

                @block.tensor
                def _(te):
                    te.wait_ge(b2l.h, b2l.val(16))
                    for o in range(OSL):
                        if o >= 1:
                            te.wait_ge(vts, vbase + vt_evac_plan[o - 1])
                        for fc in range(FC):
                            k = o * FC + fc
                            te.wait_ge(tq[k % 2].h,
                                       tq[k % 2].val(64 * (k // 2 + 1)))
                            te.wait_ge(xl[k % 2].h,
                                       xl[k % 2].val(16 * (k // 2 + 1)))
                            for i in range(NT):
                                mi = te.matmul(pp[:, i * 512:(i + 1) * 512],
                                               xb[:, (k % 2) * NOWN + i * P:
                                                  (k % 2) * NOWN + (i + 1) * P],
                                               wq[:, (k % 2) * 512:
                                                  (k % 2 + 1) * 512],
                                               start=(fc == 0), stop=False)
                                if i == NT - 1:
                                    mi.then_inc(mm.h, NT)
                                    mm.inc(NT)
                        for i in range(NT):
                            mi = te.matmul(pp[:, i * 512:(i + 1) * 512],
                                           sr2[0:1, i * P:(i + 1) * P],
                                           b2s[0:1, o * 512:(o + 1) * 512],
                                           start=False, stop=True)
                            if i == NT - 1:
                                mi.then_inc(mm.h, NT)
                                mm.inc(NT)

    def phase_a2(bck, bcq):
        with contextlib.ExitStack() as S:
            t = lambda n, sh, d: S.enter_context(nc.sbuf_tensor(n, sh, d))
            xb = t("a2x", [P, 2 * C], F32)
            sq = t("a2sq", [P, C], F32)
            sb_ = t("a2st", [P, 12], F32)
            scr = {"v": t("a2v", [P, C], F32), "sg": t("a2sg", [P, C], BF16),
                   "av": t("a2av", [P, C], F32), "ub": t("a2ub", [P, C], BF16)}
            xqb = t("a2xq", [P, 2 * C], BF16)
            svr = t("a2sv", [1, MT * P], F32)
            ps = S.enter_context(nc.psum_tensor("a2ps", [1, 2 * P], F32))
            pb = S.enter_context(nc.psum_tensor("a2pb", [P, 2 * P], F32))
            dmas = [B.gsem("dmA"), B.gsem("dmB")]
            sto = [B.gsem("stA"), B.gsem("stB")]
            steps = []

            def mk_dma(t_):
                def f(T):
                    T._e.dma_start(out=xb[:, (t_ % 2) * C:(t_ % 2 + 1) * C],
                                   in_=x_all[t_ * P:(t_ + 1) * P, :]
                                   ).then_inc(dmas[t_ % 2].h, 16)
                    dmas[t_ % 2].inc(16)
                return f

            steps.append(("sync", mk_dma(0)))
            for t_ in range(MT):
                if t_ + 1 < MT:
                    steps.append(("sync", mk_dma(t_ + 1)))
                steps.append(("vector",
                              lambda T, s_=dmas[t_ % 2], n=16 * (t_ // 2 + 1):
                              T._e.wait_ge(s_.h, s_.val(n))))
                x_t = xb[:, (t_ % 2) * C:(t_ % 2 + 1) * C]
                steps += ln_steps(x_t, C, sb_, sq, 0, 1, 8, 9)
                steps += qscale_steps(x_t, C, sb_, sq, 0, 1, 4, 8, 9)

                def slabs(T, t_=t_):
                    T.tensor_tensor(sb_[:, 2:3], sb_[:, 1:2], sb_[:, 4:5],
                                    ALU.mult)
                    T.tensor_tensor(sb_[:, 3:4], sb_[:, 0:1], sb_[:, 2:3],
                                    ALU.mult)
                    T.tensor_scalar(sb_[:, 3:4], sb_[:, 3:4], -1.0, None,
                                    ALU.mult)
                    T.tensor_tensor(dqv[:, t_:t_ + 1], sb_[:, 8:9],
                                    wsc[:, 4:5], ALU.mult)
                steps.append(("vector", slabs))
                if t_ >= 2:
                    steps.append(("vector",
                                  lambda T, s_=sto[t_ % 2],
                                  n=16 * ((t_ - 2) // 2 + 1):
                                  T._e.wait_ge(s_.h, s_.val(n))))
                steps += q_steps(x_t, C, sb_[:, 2:3], sb_[:, 3:4],
                                 xqb[:, (t_ % 2) * C:(t_ % 2 + 1) * C], scr)

                def sto_f(T, t_=t_):
                    T._e.dma_start(out=xqa_d[t_ * P:(t_ + 1) * P, :],
                                   in_=xqb[:, (t_ % 2) * C:(t_ % 2 + 1) * C]
                                   ).then_inc(sto[t_ % 2].h, 16)
                    sto[t_ % 2].inc(16)
                steps.append(("scalar", sto_f))
            def wait_all_sto(T):
                for sl in range(2):
                    T._e.wait_ge(sto[sl].h,
                                 sto[sl].val(16 * ((MT - 1 - sl) // 2 + 1)))
            steps.append(("scalar", wait_all_sto))
            for t_ in range(MT):
                steps.append(("tensor", lambda T, t_=t_: T.transpose(
                    ps[0:1, (t_ % 2) * P:(t_ % 2 + 1) * P],
                    dqv[:, t_:t_ + 1], idf[:, :])))
                steps.append(("vector", lambda T, t_=t_: T.tensor_copy(
                    svr[0:1, t_ * P:(t_ + 1) * P],
                    ps[0:1, (t_ % 2) * P:(t_ % 2 + 1) * P])))
            for t_ in range(MT):
                def bc(T, t_=t_):
                    T.matmul(pb[:, (t_ % 2) * P:(t_ % 2 + 1) * P],
                             onesrow[0:1, :], svr[0:1, t_ * P:(t_ + 1) * P],
                             start=True, stop=True)
                steps.append(("tensor", bc))

                def bcev(T, t_=t_):
                    T.tensor_copy(bck[:, t_ * P:(t_ + 1) * P],
                                  pb[:, (t_ % 2) * P:(t_ % 2 + 1) * P])
                    if t_ >= QOFF:
                        T.tensor_scalar(
                            bcq[:, (t_ - QOFF) * P:(t_ - QOFF + 1) * P],
                            pb[:, (t_ % 2) * P:(t_ % 2 + 1) * P],
                            float(cfg.SCALE), None, ALU.mult)
                steps.append(("vector", bcev))
            B.chain("pa2", steps)

    def phase_att(attn_tm, bck, bcq):
        SLAB = cfg.SLAB_S
        KL, soff = cfg.KL, cfg.soff
        MAXKV = MT * P
        NP = 5
        with contextlib.ExitStack() as S:
            t = lambda n, sh, d: S.enter_context(nc.sbuf_tensor(n, sh, d))
            xsl = t("atxs", [P, 2 * NOWN], BF16)
            wvb = t("atwv", [P, 2 * 512], F32)
            wvt = t("atwt", [P, 2 * 512], BF16)
            wvq = t("atwvq", [P, 2 * 512], BF16)
            ascv = t("atas", [P, 512], BF16)
            vbuf = t("atvb", [P, MT * HG * 129], BF16)
            qtm = t("atqtm", [P, NT * HG * P], BF16)
            ktm = t("atktm", [P, MT * HG * P], BF16)
            qfm = t("atqfm", [P, NOWN], BF16)
            kfm = t("atkfm", [P, TALL], BF16)
            pT = t("atpT", [P, 2 * MT * P], BF16)
            msk = t("atms", [P, SLAB], BF16)
            sst = t("atss", [P, MAXKV], F32)
            pS = t("atps", [P, 2 * MAXKV], BF16)
            ssc = t("atsc", [P, 512], F32)
            rms = t("atrm", [P, 8], F32)
            nrm = t("atnr", [P, 2], F32)
            rcd = t("atrcd", [P, 2], F32)
            mld = B.gsem("mld")
            with nc.Block() as block:
                @block.sync
                def _(s):
                    NSL = (SLAB + 8191) // 8192
                    for k in range(NSL):
                        a, b_ = k * 8192, min(SLAB, (k + 1) * 8192)
                        s.dma_start(out=msk[:, a:b_], in_=masks[:, a:b_]
                                    ).then_inc(mld.h, 16)
                        mld.inc(16)
                    s.wait_ge(mld.h, mld.val(16 * NSL))

                @block.vector
                def _(v):
                    v.memset(bass.AP(vbuf, P, [[MT * HG * 129, P],
                                               [129, MT * HG], [1, 1]]), 1.0)

            for g in range(NG):
                qoff, koff, voff = g * HG * P, C + g * HG * P, 2 * C + g * HG * P
                passes = [("q", qoff, QOFF, MT),
                          ("kA", koff, 0, MT // 2), ("kB", koff, MT // 2, MT),
                          ("vA", voff, 0, MT // 2), ("vB", voff, MT // 2, MT)]
                xl = [B.gsem("xlA"), B.gsem("xlB")]
                wl = [B.gsem("wlA"), B.gsem("wlB")]
                tq = [B.gsem("tqA"), B.gsem("tqB")]
                mm = B.gsem("mm")
                vts, vnb = B.ticket("vector")
                vbase = vnb[0]

                def ev_ops(pn):
                    nm, off, t0, t1 = passes[pn]
                    return (t1 - t0) * (HG if nm.startswith("v") else 1)

                vt_tern_plan = {}
                vt_evac_plan = {}
                cnt = 0
                for pn in range(NP):
                    if pn >= 1:
                        cnt += ev_ops(pn - 1)
                        vt_evac_plan[pn - 1] = cnt
                    for cc in range(CC):
                        cnt += 3
                        vt_tern_plan[pn * CC + cc] = cnt
                cnt += ev_ops(NP - 1)
                vt_evac_plan[NP - 1] = cnt
                vt_total = cnt

                with nc.psum_tensor(f"atp{g}", [P, 8 * 512], F32) as pqv, \
                        nc.Block() as block:
                    @block.sync
                    def _(s):
                        for pn, (nm, off, t0, t1) in enumerate(passes):
                            for cc in range(CC):
                                k = pn * CC + cc
                                if k >= 2:
                                    s.wait_ge(mm.h, mm.val(k - 1))
                                s.dma_start(
                                    out=xsl[:, (k % 2) * NOWN:
                                            (k % 2) * NOWN + (t1 - t0) * P],
                                    in_=xqa_d[t0 * P:t1 * P,
                                              cc * P:(cc + 1) * P],
                                    transpose=True).then_inc(xl[k % 2].h, 16)
                                xl[k % 2].inc(16)
                                s.wait_ge(vts, vbase + vt_tern_plan[k])
                                wtrans(s, wvq, (k % 2) * 512, wvt, k % 2,
                                       tq[k % 2])

                    @block.scalar
                    def _(s):
                        NKQ = NP * CC
                        for k in range(min(2, NKQ)):
                            pn, cc = divmod(k, CC)
                            wload(s, wvb, k % 2, w_qkv, passes[pn][1],
                                  cc * P, wl[k % 2])
                        for k in range(NKQ):
                            if k + 2 < NKQ:
                                s.wait_ge(vts, vbase + vt_tern_plan[k])
                                pn2, cc2 = divmod(k + 2, CC)
                                wload(s, wvb, k % 2, w_qkv, passes[pn2][1],
                                      cc2 * P, wl[k % 2])

                    @block.vector
                    def _(v):
                        T = B.TE(v, "vector")

                        def evac(pn):
                            nm, off, t0, t1 = passes[pn]
                            v.wait_ge(mm.h, mm.val((pn + 1) * CC))
                            for mt in range(t0, t1):
                                psl_ = pqv[:, (mt - t0) * 512:
                                           (mt - t0 + 1) * 512]
                                if nm.startswith("v"):
                                    for hh in range(HG):
                                        T.tensor_scalar(
                                            vbuf[:, mt * HG * 129 + hh * 129:
                                                 mt * HG * 129 + hh * 129 + P],
                                            psl_[:, hh * P:(hh + 1) * P],
                                            dqv[:, mt:mt + 1], None, ALU.mult)
                                elif nm.startswith("k"):
                                    T.tensor_copy(
                                        ktm[:, mt * HG * P:(mt + 1) * HG * P],
                                        psl_)
                                else:
                                    T.tensor_copy(
                                        qtm[:, (mt - t0) * HG * P:
                                            (mt - t0 + 1) * HG * P], psl_)

                        for pn in range(NP):
                            if pn >= 1:
                                evac(pn - 1)
                                assert T.n == vbase + vt_evac_plan[pn - 1]
                            for cc in range(CC):
                                k = pn * CC + cc
                                v.wait_ge(wl[k % 2].h,
                                          wl[k % 2].val(64 * (k // 2 + 1)))
                                if k >= 2:
                                    v.wait_ge(tq[k % 2].h,
                                              tq[k % 2].val(64 * (k // 2)))
                                tern3(T,
                                      wvb[:, (k % 2) * 512:(k % 2 + 1) * 512],
                                      wvt[:, (k % 2) * 512:(k % 2 + 1) * 512],
                                      ascv[:, :], 5)
                                assert T.n == vbase + vt_tern_plan[k]
                        evac(NP - 1)
                        assert T.n == vbase + vt_total

                    @block.tensor
                    def _(te):
                        for pn, (nm, off, t0, t1) in enumerate(passes):
                            if pn >= 1:
                                te.wait_ge(vts, vbase + vt_evac_plan[pn - 1])
                            for cc in range(CC):
                                k = pn * CC + cc
                                te.wait_ge(tq[k % 2].h,
                                           tq[k % 2].val(64 * (k // 2 + 1)))
                                te.wait_ge(xl[k % 2].h,
                                           xl[k % 2].val(16 * (k // 2 + 1)))
                                for mt in range(t0, t1):
                                    mi = te.matmul(
                                        pqv[:, (mt - t0) * 512:
                                            (mt - t0 + 1) * 512],
                                        xsl[:, (k % 2) * NOWN + (mt - t0) * P:
                                            (k % 2) * NOWN +
                                            (mt - t0 + 1) * P],
                                        wvq[:, (k % 2) * 512:
                                            (k % 2 + 1) * 512],
                                        start=(cc == 0), stop=(cc == CC - 1))
                                    if mt == t1 - 1:
                                        mi.then_inc(mm.h, 1)
                                        mm.inc(1)

                nq, nk = (NT + 7) // 8, (MT + 7) // 8
                VQ0 = nq + nk
                NCH = [(KL[i] * P + 511) // 512 for i in range(NT)]
                TBn = [(KL[i] + 7) // 8 for i in range(NT)]
                vst_end = [0] * NT
                vpte_end = [0] * NT
                vpav_end = [0] * NT
                cnt2 = VQ0
                for i in range(NT):
                    cnt2 += 2 * NCH[i] + 2
                    vst_end[i] = cnt2
                    cnt2 += TBn[i]
                    vpte_end[i] = cnt2
                    cnt2 += 2
                    vpav_end[i] = cnt2
                VTOT = cnt2
                mm_sc_s = [0] * NT
                mm_av_e = [0] * NT
                m2 = 0
                for i in range(NT):
                    mm_sc_s[i] = m2
                    m2 += NCH[i] + 1
                    mm_av_e[i] = m2
                evac_vop = [b + 1 for b in range(VQ0)]
                tse_end = [0] * NT
                tb_c = VQ0
                for i in range(NT):
                    for tx in range(TBn[i]):
                        evac_vop.append(vst_end[i] + tx + 1)
                    tb_c += TBn[i]
                    tse_end[i] = tb_c

                with nc.psum_tensor(f"asc{g}", [P, 4 * 512], F32) as psS, \
                        nc.psum_tensor(f"aav{g}", [P, 2 * 512], F32) as pav, \
                        nc.psum_tensor(f"atr{g}", [P, 16 * P], BF16) as ptr:
                    for hh in range(HG):
                        h = g * HG + hh
                        tsem = B.gsem("tsem")
                        mmh = B.gsem("mmh")
                        vts2, vnb2 = B.ticket("vector")
                        ats2, anb2 = B.ticket("scalar")
                        v2b, a2b = vnb2[0], anb2[0]
                        with nc.Block() as block:
                            @block.tensor
                            def _(te):
                                tb = 0
                                for src_, n_ in ((qtm, NT), (ktm, MT)):
                                    for i in range(n_):
                                        if tb >= 2 and i % 8 == 0:
                                            te.wait_ge(vts2,
                                                       v2b + evac_vop[tb - 2])
                                        ti = te.transpose(
                                            ptr[:, (tb % 2) * 8 * P +
                                                (i % 8) * P:
                                                (tb % 2) * 8 * P +
                                                (i % 8 + 1) * P],
                                            src_[:, i * HG * P + hh * P:
                                                 i * HG * P + (hh + 1) * P],
                                            idb[:, :])
                                        if i % 8 == 7 or i == n_ - 1:
                                            ti.then_inc(tsem.h, 1)
                                            tsem.inc(1)
                                            tb += 1
                                for i in range(NT):
                                    kvlen = KL[i] * P
                                    te.wait_ge(vts2, v2b + (vst_end[i - 1]
                                                            if i else VQ0))
                                    for c in range(NCH[i]):
                                        wc = min(512, kvlen - c * 512)
                                        te.matmul(
                                            psS[:, c * 512:c * 512 + wc],
                                            qfm[:, i * P:(i + 1) * P],
                                            kfm[:, c * 512:c * 512 + wc],
                                            start=True, stop=True
                                            ).then_inc(mmh.h, 1)
                                        mmh.inc(1)
                                    te.wait_ge(ats2, a2b + i + 1)
                                    for j in range(KL[i]):
                                        if tb >= 2 and j % 8 == 0:
                                            te.wait_ge(vts2,
                                                       v2b + evac_vop[tb - 2])
                                        ti = te.transpose(
                                            ptr[:, (tb % 2) * 8 * P +
                                                (j % 8) * P:
                                                (tb % 2) * 8 * P +
                                                (j % 8 + 1) * P],
                                            pS[:, (i % 2) * MAXKV + j * P:
                                               (i % 2) * MAXKV + (j + 1) * P],
                                            idb[:, :])
                                        if j % 8 == 7 or j == KL[i] - 1:
                                            ti.then_inc(tsem.h, 1)
                                            tsem.inc(1)
                                            tb += 1
                                    te.wait_ge(vts2, v2b + vpte_end[i])
                                    if i >= 2:
                                        te.wait_ge(vts2, v2b + vpav_end[i - 2])
                                    for j in range(KL[i]):
                                        mmi = te.matmul(
                                            pav[:, (i % 2) * 512:
                                                (i % 2) * 512 + 129],
                                            pT[:, (i % 2) * MAXKV + j * P:
                                               (i % 2) * MAXKV + (j + 1) * P],
                                            vbuf[:, j * HG * 129 + hh * 129:
                                                 j * HG * 129 +
                                                 (hh + 1) * 129],
                                            start=(j == 0),
                                            stop=(j == KL[i] - 1))
                                        if j == KL[i] - 1:
                                            mmi.then_inc(mmh.h, 1)
                                            mmh.inc(1)

                            @block.vector
                            def _(v):
                                T = B.TE(v, "vector")
                                tb = 0
                                for bq in range(nq):
                                    v.wait_ge(tsem.h, tsem.val(tb + 1))
                                    nb = min(8, NT - bq * 8)
                                    T.tensor_tensor(
                                        qfm[:, bq * 8 * P:bq * 8 * P + nb * P],
                                        ptr[:, (tb % 2) * 8 * P:
                                            (tb % 2) * 8 * P + nb * P],
                                        bcq[:, bq * 8 * P:bq * 8 * P + nb * P],
                                        ALU.mult)
                                    tb += 1
                                for bk in range(nk):
                                    v.wait_ge(tsem.h, tsem.val(tb + 1))
                                    nb = min(8, MT - bk * 8)
                                    T.tensor_tensor(
                                        kfm[:, bk * 8 * P:bk * 8 * P + nb * P],
                                        ptr[:, (tb % 2) * 8 * P:
                                            (tb % 2) * 8 * P + nb * P],
                                        bck[:, bk * 8 * P:bk * 8 * P + nb * P],
                                        ALU.mult)
                                    tb += 1
                                assert T.n == v2b + VQ0
                                for i in range(NT):
                                    kvlen = KL[i] * P
                                    for c in range(NCH[i]):
                                        wc = min(512, kvlen - c * 512)
                                        v.wait_ge(mmh.h,
                                                  mmh.val(mm_sc_s[i] + c + 1))
                                        if c == 0 and i >= 1:
                                            v.wait_ge(ats2, a2b + i)
                                        T.scalar_tensor_tensor(
                                            sst[:, c * 512:c * 512 + wc],
                                            psS[:, c * 512:c * 512 + wc],
                                            1.0,
                                            msk[:, soff[i] + c * 512:
                                                soff[i] + c * 512 + wc],
                                            ALU.mult, ALU.add)
                                        T.tensor_scalar(
                                            ssc[:, 0:wc],
                                            sst[:, c * 512:c * 512 + wc],
                                            1.0, None, ALU.mult, ALU.max,
                                            accum_out=rms[:, c:c + 1])
                                    T.reduce_max(nrm[:, 0:1],
                                                 rms[:, 0:NCH[i]], axis=AX)
                                    T.tensor_scalar(nrm[:, 1:2], nrm[:, 0:1],
                                                    -1.0, None, ALU.mult)
                                    assert T.n == v2b + vst_end[i]
                                    for tx in range(TBn[i]):
                                        v.wait_ge(tsem.h,
                                                  tsem.val(tse_end[i] -
                                                           TBn[i] + tx + 1))
                                        if tx == 0 and i >= 2:
                                            v.wait_ge(mmh.h,
                                                      mmh.val(mm_av_e[i - 2]))
                                        nb = min(8, KL[i] - tx * 8)
                                        tbg = tse_end[i] - TBn[i] + tx
                                        T.tensor_copy(
                                            pT[:, (i % 2) * MAXKV + tx * 8 * P:
                                               (i % 2) * MAXKV +
                                               tx * 8 * P + nb * P],
                                            ptr[:, (tbg % 2) * 8 * P:
                                                (tbg % 2) * 8 * P + nb * P])
                                    assert T.n == v2b + vpte_end[i]
                                    v.wait_ge(mmh.h, mmh.val(mm_av_e[i]))
                                    T.reciprocal(
                                        rcd[:, 0:1],
                                        pav[:, (i % 2) * 512 + P:
                                            (i % 2) * 512 + 129])
                                    T.tensor_scalar(
                                        attn_tm[:, i * C + h * P:
                                                i * C + (h + 1) * P],
                                        pav[:, (i % 2) * 512:
                                            (i % 2) * 512 + P],
                                        rcd[:, 0:1], None, ALU.mult)
                                    assert T.n == v2b + vpav_end[i]
                                assert T.n == v2b + VTOT

                            @block.scalar
                            def _(s):
                                T = B.TE(s, "scalar")
                                for i in range(NT):
                                    kvlen = KL[i] * P
                                    s.wait_ge(vts2, v2b + vst_end[i])
                                    if i >= 2:
                                        s.wait_ge(tsem.h,
                                                  tsem.val(tse_end[i - 2]))
                                    T.activation(
                                        pS[:, (i % 2) * MAXKV:
                                           (i % 2) * MAXKV + kvlen],
                                        sst[:, 0:kvlen], AF.Exp,
                                        bias=nrm[:, 1:2], scale=1.0)
                                assert T.n == a2b + NT

    def phase_qp(ffn_acc, attn_tm):
        OSL = C // 512
        with contextlib.ExitStack() as S:
            t = lambda n, sh, d: S.enter_context(nc.sbuf_tensor(n, sh, d))
            xqpT = t("xqpT", [P, CC * NOWN], BF16)
            sq = t("qpsq", [P, C], F32)
            sb_ = t("qpst", [P, 12], F32)
            scr = {"v": t("qpv", [P, C], F32), "sg": t("qpsg", [P, C], BF16),
                   "av": t("qpav", [P, C], F32), "ub": t("qpub", [P, C], BF16)}
            xqb = t("qpxq", [P, C], BF16)
            S2 = contextlib.ExitStack()
            ptq = S2.enter_context(nc.psum_tensor("qppt", [P, 8 * P], BF16))
            steps = []
            for i in range(NT):
                a_t = attn_tm[:, i * C:(i + 1) * C]

                def mk0(T):
                    T.memset(sb_[:, 0:1], 0.0)
                    T.memset(sb_[:, 1:2], 1.0)
                steps.append(("vector", mk0))
                steps += qscale_steps(a_t, C, sb_, sq, 0, 1, 4, 8, 9)

                def slabs(T, i=i):
                    T.tensor_copy(sb_[:, 2:3], sb_[:, 4:5])
                    T.memset(sb_[:, 3:4], 0.0)
                    T.tensor_tensor(dqp[:, i:i + 1], sb_[:, 8:9], wsc[:, 6:7],
                                    ALU.mult)
                steps.append(("vector", slabs))
                steps += q_steps(a_t, C, sb_[:, 2:3], sb_[:, 3:4], xqb[:, :],
                                 scr)
                for g0 in range(0, CC, 8):
                    nb = min(8, CC - g0)

                    def tr(T, g0=g0, nb=nb):
                        for kx in range(nb):
                            T.transpose(ptq[:, kx * P:(kx + 1) * P],
                                        xqb[:, (g0 + kx) * P:
                                            (g0 + kx + 1) * P],
                                        idb[:, :])
                    steps.append(("tensor", tr))

                    def evt(T, i=i, g0=g0, nb=nb):
                        T.tensor_copy(
                            bass.AP(xqpT, g0 * NOWN + i * P,
                                    [[CC * NOWN, P], [NOWN, nb], [1, P]]),
                            ptq[:, 0:nb * P])
                    steps.append(("vector", evt))
            B.chain("pqpq", steps)
            S2.close()

            wb = t("qpw", [P, 2 * 512], F32)
            wt = t("qpwt", [P, 2 * 512], BF16)
            wq = t("qpwq", [P, 2 * 512], BF16)
            asc = t("qpas", [P, 512], BF16)
            xrb = t("qpxr", [P, 4 * 512], F32)
            ob = t("qpo", [P, 4 * 512], F32)
            pp = S.enter_context(nc.psum_tensor("qpp", [P, 8 * 512], F32))
            ld = [B.gsem("ldA"), B.gsem("ldB")]
            tq = [B.gsem("tqA"), B.gsem("tqB")]
            xr = [B.gsem(f"xg{q}") for q in range(4)]
            so = [B.gsem(f"so{q}") for q in range(4)]
            mm = B.gsem("mm")
            vts, vnb = B.ticket("vector")
            vbase = vnb[0]
            NCH = OSL * CC
            vt_tern_plan = {}
            vt_evac_plan = {}
            cnt = 0
            for o in range(OSL):
                for cc in range(CC):
                    if cc == 2 and o >= 1:
                        cnt += 2 * NT
                        vt_evac_plan[o - 1] = cnt
                    cnt += 3
                    vt_tern_plan[o * CC + cc] = cnt
            cnt += 2 * NT
            vt_evac_plan[OSL - 1] = cnt
            vt_total = cnt

            with nc.Block() as block:
                @block.sync
                def _(s):
                    for k in range(min(2, NCH)):
                        o, cc = divmod(k, CC)
                        wload(s, wb, k % 2, w_proj, o * 512, cc * P,
                              ld[k % 2])
                    for k in range(NCH):
                        s.wait_ge(vts, vbase + vt_tern_plan[k])
                        if k >= 2:
                            s.wait_ge(mm.h, mm.val((k - 1) * NT))
                        wtrans(s, wq, (k % 2) * 512, wt, k % 2, tq[k % 2])
                        if k + 2 < NCH:
                            o2, c2 = divmod(k + 2, CC)
                            wload(s, wb, k % 2, w_proj, o2 * 512, c2 * P,
                                  ld[k % 2])

                @block.gpsimd
                def _(g):
                    for o in range(OSL):
                        for i in range(NT):
                            k2 = o * NT + i
                            if k2 >= 4:
                                op, oi = divmod(k2 - 4, NT)
                                g.wait_ge(vts, vbase + vt_evac_plan[op]
                                          - 2 * NT + 2 * (oi + 1))
                            g.dma_start(
                                out=xrb[:, (k2 % 4) * 512:(k2 % 4 + 1) * 512],
                                in_=x_all[NOWN + i * P:NOWN + (i + 1) * P,
                                          o * 512:(o + 1) * 512]
                                ).then_inc(xr[k2 % 4].h, 16)
                            xr[k2 % 4].inc(16)

                @block.vector
                def _(v):
                    T = B.TE(v, "vector")

                    def evacs(o):
                        v.wait_ge(mm.h, mm.val((o + 1) * CC * NT))
                        for i in range(NT):
                            k2 = o * NT + i
                            v.wait_ge(xr[k2 % 4].h,
                                      xr[k2 % 4].val(16 * (k2 // 4 + 1)))
                            if k2 >= 4:
                                v.wait_ge(so[k2 % 4].h,
                                          so[k2 % 4].val(16 * (k2 // 4)))
                            psl_ = pp[:, i * 512:(i + 1) * 512]
                            osl_ = ob[:, (k2 % 4) * 512:(k2 % 4 + 1) * 512]
                            T.scalar_tensor_tensor(
                                osl_, psl_, dqp[:, i:i + 1],
                                ffn_acc[:, i * C + o * 512:
                                        i * C + (o + 1) * 512],
                                ALU.mult, ALU.add)
                            T.tensor_tensor(
                                osl_, osl_,
                                xrb[:, (k2 % 4) * 512:(k2 % 4 + 1) * 512],
                                ALU.add)

                    for o in range(OSL):
                        for cc in range(CC):
                            k = o * CC + cc
                            if cc == 2 and o >= 1:
                                evacs(o - 1)
                                assert T.n == vbase + vt_evac_plan[o - 1]
                            v.wait_ge(ld[k % 2].h,
                                      ld[k % 2].val(64 * (k // 2 + 1)))
                            if k >= 2:
                                v.wait_ge(tq[k % 2].h,
                                          tq[k % 2].val(64 * (k // 2)))
                            tern3(T, wb[:, (k % 2) * 512:(k % 2 + 1) * 512],
                                  wt[:, (k % 2) * 512:(k % 2 + 1) * 512],
                                  asc[:, :], 7)
                            assert T.n == vbase + vt_tern_plan[k]
                    evacs(OSL - 1)
                    assert T.n == vbase + vt_total

                @block.tensor
                def _(te):
                    for o in range(OSL):
                        if o >= 1:
                            te.wait_ge(vts, vbase + vt_evac_plan[o - 1])
                        for cc in range(CC):
                            k = o * CC + cc
                            te.wait_ge(tq[k % 2].h,
                                       tq[k % 2].val(64 * (k // 2 + 1)))
                            for i in range(NT):
                                mi = te.matmul(
                                    pp[:, i * 512:(i + 1) * 512],
                                    xqpT[:, cc * NOWN + i * P:
                                         cc * NOWN + (i + 1) * P],
                                    wq[:, (k % 2) * 512:(k % 2 + 1) * 512],
                                    start=(cc == 0), stop=(cc == CC - 1))
                                if i == NT - 1:
                                    mi.then_inc(mm.h, NT)
                                    mm.inc(NT)

                @block.scalar
                def _(s):
                    for o in range(OSL):
                        for i in range(NT):
                            k2 = o * NT + i
                            s.wait_ge(vts, vbase + vt_evac_plan[o]
                                      - 2 * NT + 2 * (i + 1))
                            s.dma_start(out=out[i * P:(i + 1) * P,
                                                o * 512:(o + 1) * 512],
                                        in_=ob[:, (k2 % 4) * 512:
                                               (k2 % 4 + 1) * 512]
                                        ).then_inc(so[k2 % 4].h, 16)
                            so[k2 % 4].inc(16)
                    NK2 = OSL * NT
                    for sl in range(4):
                        s.wait_ge(so[sl].h,
                                  so[sl].val(16 * ((NK2 - 1 - sl) // 4 + 1)))

    LIM = globals().get("_PHASE_LIMIT", 99)
    MASK = globals().get("_PHASE_MASK", None)
    k_ = [0]

    def go(fn, *a):
        if (MASK is not None and k_[0] in MASK) or \
                (MASK is None and k_[0] < LIM):
            fn(*a)
        k_[0] += 1

    with nc.sbuf_tensor("xqf1T", [P, CC * NOWN], BF16) as xqf1T:
        go(weight_scale, "pw1", w1, F, C, 1e-8, 0)
        go(phase_a1, xqf1T)
        go(phase_f1, xqf1T)
    with (
        nc.sbuf_tensor("ffn_acc", [P, NT * C], BF16) as ffn_acc,
        nc.sbuf_tensor("attn_tm", [P, NT * C], BF16) as attn_tm,
    ):
        go(weight_scale, "pw2", w2, C, F, 1e-8, 2)
        go(phase_h)
        go(phase_f2, ffn_acc)
        go(weight_scale, "pwq", w_qkv, 3 * C, C, 1e-5, 4)
        with (
            nc.sbuf_tensor("bck", [P, TALL], F32) as bck,
            nc.sbuf_tensor("bcq", [P, NOWN], F32) as bcq,
        ):
            go(phase_a2, bck, bcq)
            go(phase_att, attn_tm, bck, bcq)
        if globals().get("_DEBUG_TAPS"):
            atm_d = nc.dram_tensor("atm_d", [P, NT * C], BF16,
                                   kind="ExternalOutput")
            fac_d = nc.dram_tensor("fac_d", [P, NT * C], BF16,
                                   kind="ExternalOutput")
            with nc.Block() as block:
                ts_ = B.gsem("tapx")

                @block.sync
                def _(s):
                    s.dma_start(out=atm_d[:, :], in_=attn_tm[:, :]
                                ).then_inc(ts_.h, 16)
                    ts_.inc(16)
                    s.dma_start(out=fac_d[:, :], in_=ffn_acc[:, :]
                                ).then_inc(ts_.h, 16)
                    ts_.inc(16)
                    s.wait_ge(ts_.h, ts_.val(32))
        go(weight_scale, "pwp", w_proj, C, C, 1e-5, 6)
        go(phase_qp, ffn_acc, attn_tm)

    if MASK is not None or LIM < 99:
        with nc.sbuf_tensor("fin", [P, C], F32) as fin, nc.Block() as block:
            fs = B.gsem("fin")

            @block.sync
            def _(s):
                for i in range(NT):
                    s.dma_start(out=fin[:, :],
                                in_=x_all[NOWN + i * P:NOWN + (i + 1) * P, :]
                                ).then_inc(fs.h, 16)
                    fs.inc(16)
                    s.wait_ge(fs.h, fs.val(32 * i + 16))
                    s.dma_start(out=out[i * P:(i + 1) * P, :], in_=fin[:, :]
                                ).then_inc(fs.h, 16)
                    fs.inc(16)
                    s.wait_ge(fs.h, fs.val(32 * i + 32))

    B.stack.close()
    return nc




def make_masks(cfg: Cfg, half: int) -> np.ndarray:
    m = np.zeros((P, cfg.SLAB_S), dtype=np.float32)
    for i in range(cfg.NT):
        kv = np.arange(cfg.KL[i] * P)[None, :]
        gkv = np.where(kv < cfg.NOWN, (1 - half) * cfg.NOWN + kv,
                       half * cfg.NOWN + (kv - cfg.NOWN))
        gq = half * cfg.NOWN + i * P + np.arange(P)[:, None]
        blk = np.where(gkv <= gq, 0.0, NEG).astype(np.float32)
        m[:, cfg.soff[i]:cfg.soff[i] + cfg.KL[i] * P] = blk
    return m.astype(ml_dtypes.bfloat16)


def stage_inputs(cfg: Cfg, x, wqkv, wproj, w1v, b1v, w2v, b2v):
    idb = np.eye(P).astype(ml_dtypes.bfloat16)
    idf = np.eye(P).astype(np.float32)
    in_maps = []
    nb = x.shape[0]
    for core in range(2 * nb):
        b, half = core // 2, core % 2
        partner = x[b, (1 - half) * cfg.NOWN:(2 - half) * cfg.NOWN]
        own = x[b, half * cfg.NOWN:(half + 1) * cfg.NOWN]
        x_all = np.ascontiguousarray(
            np.concatenate([partner, own], axis=0)).astype(np.float32)
        in_maps.append({
            "x_all": x_all,
            "w_qkv": wqkv, "w_proj": wproj,
            "w1": w1v, "b1": b1v.reshape(1, -1),
            "w2": w2v, "b2": b2v.reshape(1, -1),
            "masks": make_masks(cfg, half),
            "ident_b": idb, "ident_f": idf,
        })
    return in_maps


def kernel(L0_attn_x, L0_attn_w_qkv, L0_attn_w_proj, L0_ffn_w1, L0_ffn_b1,
           L0_ffn_w2, L0_ffn_b2):
    cfg = Cfg(C=2048, F=8192, H=16, NOWN=1024)
    x = np.ascontiguousarray(np.asarray(L0_attn_x, dtype=np.float32))
    nc = build(cfg)
    in_maps = stage_inputs(cfg, x,
                           np.asarray(L0_attn_w_qkv, dtype=np.float32),
                           np.asarray(L0_attn_w_proj, dtype=np.float32),
                           np.asarray(L0_ffn_w1, dtype=np.float32),
                           np.asarray(L0_ffn_b1, dtype=np.float32),
                           np.asarray(L0_ffn_w2, dtype=np.float32),
                           np.asarray(L0_ffn_b2, dtype=np.float32))
    res = run_bass_kernel_spmd(nc, in_maps, core_ids=list(range(len(in_maps))))
    B_, S, C = x.shape
    outv = np.empty((B_, S, C), dtype=np.float32)
    for core, r in enumerate(res.results):
        b, half = core // 2, core % 2
        outv[b, half * cfg.NOWN:(half + 1) * cfg.NOWN] = r["out"]
    return outv


# version 07
# speedup vs baseline: 1.0312x; 1.0312x over previous
import contextlib

import numpy as np
import ml_dtypes

import concourse.bass as bass
import concourse.mybir as mybir
from concourse.bass_utils import run_bass_kernel_spmd

F32 = mybir.dt.float32
BF16 = mybir.dt.bfloat16
I32 = mybir.dt.int32
AF = mybir.ActivationFunctionType
ALU = mybir.AluOpType
AX = mybir.AxisListType.X

NEG = -9.0e15
P = 128
LNEPS = 1.0e-5
ACTEPS = 1.0e-5

COMPUTE_OPS = {
    "tensor_scalar", "tensor_tensor", "scalar_tensor_tensor", "tensor_copy",
    "tensor_add", "tensor_sub", "tensor_mul", "tensor_max", "reduce_sum",
    "reduce_max", "reciprocal", "memset", "activation", "iota", "transpose",
    "matmul", "tensor_single_scalar",
}

ENG_COL = {"sync": 0, "vector": 1, "scalar": 2, "gpsimd": 3, "tensor": 4}


class TEng:

    def __init__(self, e, sem, nbox, fence_ap):
        object.__setattr__(self, "_e", e)
        object.__setattr__(self, "_sem", sem)
        object.__setattr__(self, "_n", nbox)
        object.__setattr__(self, "_fence", fence_ap)

    @property
    def n(self):
        return self._n[0]

    def __getattr__(self, nm):
        f = getattr(self._e, nm)
        if nm in COMPUTE_OPS:
            def w(*a, **k):
                if self._n[0]:
                    self._e.wait_ge(self._sem, self._n[0])
                inst = f(*a, **k)
                inst.then_inc(self._sem, 1)
                self._n[0] += 1
                return inst
            return w
        return f

    def fence(self):
        if self._n[0]:
            self._e.wait_ge(self._sem, self._n[0])
        self._e.memset(self._fence, 0.0).then_inc(self._sem, 1)
        self._n[0] += 1


class Cfg:
    def __init__(self, C=2048, F=8192, H=16, NOWN=1024):
        self.C, self.F, self.H, self.NOWN = C, F, H, NOWN
        self.D = P
        assert C == H * self.D and NOWN % P == 0 and F % 512 == 0 and C % 512 == 0
        self.CC = C // P
        self.FC = F // P
        self.NT = NOWN // P
        self.TALL = 2 * NOWN
        self.MT = self.TALL // P
        self.QOFF = self.MT - self.NT
        self.HG = min(4, H)
        self.NG = H // self.HG
        self.SCALE = float(self.D) ** -0.5
        self.KL = [self.QOFF + i + 1 for i in range(self.NT)]
        self.soff = [0] * self.NT
        for i in range(1, self.NT):
            self.soff[i] = self.soff[i - 1] + self.KL[i - 1] * P
        self.SLAB_S = self.soff[-1] + self.KL[-1] * P


class GSem:
    def __init__(self, h):
        self.h = h
        self.n = 0


class SemView:

    def __init__(self, g):
        self.g = g
        self.base = g.n

    @property
    def h(self):
        return self.g.h

    def inc(self, k=16):
        self.g.n += k

    def val(self, rel):
        return self.base + rel


class Builder:
    def __init__(self, cfg: Cfg):
        self.cfg = cfg
        self.nc = bass.Bass("TRN2", target_bir_lowering=False)
        self.stack = contextlib.ExitStack()
        self._semi = 0
        self.fnc = None
        self._gsems = {}
        self._tickets = {}

    def sem(self, tag=""):
        self._semi += 1
        return self.stack.enter_context(self.nc.semaphore(f"s{self._semi}{tag}"))

    def gsem(self, key):
        if key not in self._gsems:
            self._gsems[key] = GSem(self.sem(key))
        return SemView(self._gsems[key])

    def ticket(self, eng):
        if eng not in self._tickets:
            self._tickets[eng] = (self.sem("tk" + eng[0]), [0])
        return self._tickets[eng]

    def TE(self, e, eng):
        sem, nbox = self.ticket(eng)
        return TEng(e, sem, nbox,
                    self.fnc[0:1, ENG_COL[eng]:ENG_COL[eng] + 1])

    def sbuf(self, name, shape, dt):
        return self.stack.enter_context(self.nc.sbuf_tensor(name, shape, dt))

    def chain(self, name, steps):
        prev = [None]
        with self.nc.Block() as block:
            for eng, fn in steps:
                sem, nbox = self.ticket(eng)
                n_before = nbox[0]

                def sect(e, fn=fn, sem=sem, nbox=nbox, pv=prev[0], eng=eng):
                    T = TEng(e, sem, nbox,
                             self.fnc[0:1, ENG_COL[eng]:ENG_COL[eng] + 1])
                    if pv is not None and pv[0] is not sem:
                        e.wait_ge(pv[0], pv[1])
                    fn(T)
                getattr(block, eng)(sect)
                if nbox[0] > n_before:
                    prev[0] = (sem, nbox[0])


def build(cfg: Cfg):
    B = Builder(cfg)
    nc = B.nc
    C, F, H, NOWN = cfg.C, cfg.F, cfg.H, cfg.NOWN
    CC, FC, NT, MT, TALL = cfg.CC, cfg.FC, cfg.NT, cfg.MT, cfg.TALL
    HG, NG, QOFF = cfg.HG, cfg.NG, cfg.QOFF

    x_all = nc.dram_tensor("x_all", [TALL, C], F32, kind="ExternalInput")
    w_qkv = nc.dram_tensor("w_qkv", [3 * C, C], F32, kind="ExternalInput")
    w_proj = nc.dram_tensor("w_proj", [C, C], F32, kind="ExternalInput")
    w1 = nc.dram_tensor("w1", [F, C], F32, kind="ExternalInput")
    b1 = nc.dram_tensor("b1", [1, F], F32, kind="ExternalInput")
    w2 = nc.dram_tensor("w2", [C, F], F32, kind="ExternalInput")
    b2 = nc.dram_tensor("b2", [1, C], F32, kind="ExternalInput")
    masks = nc.dram_tensor("masks", [P, cfg.SLAB_S], BF16, kind="ExternalInput")
    ident_b = nc.dram_tensor("ident_b", [P, P], BF16, kind="ExternalInput")
    ident_f = nc.dram_tensor("ident_f", [P, P], F32, kind="ExternalInput")
    out = nc.dram_tensor("out", [NOWN, C], F32, kind="ExternalOutput")
    _tap = "ExternalOutput" if globals().get("_DEBUG_TAPS") else "Internal"
    xqa_d = nc.dram_tensor("xqa_d", [TALL, C], BF16, kind=_tap)
    xqh_d = nc.dram_tensor("xqh_d", [NOWN, F], BF16, kind=_tap)
    h_d = nc.dram_tensor("h_d", [NOWN, F], BF16, kind=_tap)

    B.fnc = B.sbuf("fnc", [1, 8], F32)
    idb = B.sbuf("idb", [P, P], BF16)
    idf = B.sbuf("idf", [P, P], F32)
    onesrow = B.sbuf("onesrow", [1, P], F32)
    wsc = B.sbuf("wsc", [P, 8], F32)
    aff_a = B.sbuf("aff_a", [P, NT], F32)
    aff_b = B.sbuf("aff_b", [P, NT], F32)
    dq1 = B.sbuf("dq1", [P, NT], F32)
    isc1 = B.sbuf("isc1", [P, NT], F32)
    sr1 = B.sbuf("sr1", [1, NT * P], F32)
    dq2 = B.sbuf("dq2", [P, NT], F32)
    isc2 = B.sbuf("isc2", [P, NT], F32)
    sr2 = B.sbuf("sr2", [1, NT * P], F32)
    dqv = B.sbuf("dqv", [P, MT], F32)
    dqp = B.sbuf("dqp", [P, NT], F32)
    OSL1 = F // 512
    hsum = B.sbuf("hsum", [P, NT * OSL1], F32)
    hsq = B.sbuf("hsq", [P, NT * OSL1], F32)
    hmx = B.sbuf("hmx", [P, NT * OSL1], F32)
    hmn = B.sbuf("hmn", [P, NT * OSL1], F32)

    with nc.Block() as block:
        ld0 = B.sem("p0")

        @block.sync
        def _(s):
            s.dma_start(out=idb[:, :], in_=ident_b[:, :]).then_inc(ld0, 16)
            s.dma_start(out=idf[:, :], in_=ident_f[:, :]).then_inc(ld0, 16)
            s.wait_ge(ld0, 32)

        @block.vector
        def _(v):
            v.memset(onesrow[:, :], 1.0)

    def weight_scale(tag, w_dram, rows, cols, eps, scol):
        RT = rows // P
        ld = [B.gsem("gldA"), B.gsem("gldB")]
        vts, vnb = B.ticket("vector")
        vbase = vnb[0]
        with (
            nc.sbuf_tensor(tag + "wb", [P, 2 * cols], F32) as wb,
            nc.sbuf_tensor(tag + "ab", [P, cols], F32) as ab,
            nc.sbuf_tensor(tag + "cs", [P, RT], F32) as cs,
        ):
            with nc.Block() as block:
                @block.gpsimd
                def _(g):
                    for t in range(RT):
                        if t >= 2:
                            g.wait_ge(vts, vbase + 2 * (t - 1))
                        g.dma_start(out=wb[:, (t % 2) * cols:(t % 2 + 1) * cols],
                                    in_=w_dram[t * P:(t + 1) * P, :]
                                    ).then_inc(ld[t % 2].h, 16)
                        ld[t % 2].inc(16)

                @block.vector
                def _(v):
                    T = B.TE(v, "vector")
                    for t in range(RT):
                        v.wait_ge(ld[t % 2].h, ld[t % 2].val(16 * (t // 2 + 1)))
                        src = wb[:, (t % 2) * cols:(t % 2 + 1) * cols]
                        T.scalar_tensor_tensor(ab[:, :], src, -1.0, src,
                                               ALU.mult, ALU.max)
                        T.tensor_scalar(ab[:, :], ab[:, :], 1.0, None,
                                        ALU.mult, ALU.add,
                                        accum_out=cs[:, t:t + 1])
                    assert T.n == vbase + 2 * RT

            with (
                nc.sbuf_tensor(tag + "sc", [P, 1], F32) as sc,
                nc.sbuf_tensor(tag + "r1", [1, 2], F32) as r1,
                nc.psum_tensor(tag + "pt", [1, P], F32) as pt,
                nc.psum_tensor(tag + "pb", [P, 2], F32) as pb,
            ):
                def s1(T):
                    T.reduce_sum(sc[:, 0:1], cs[:, :], axis=AX)

                def s2(T):
                    T.transpose(pt[0:1, :], sc[:, 0:1], idf[:, :])

                def s3(T):
                    T.reduce_sum(r1[0:1, 0:1], pt[0:1, :], axis=AX)
                    T.tensor_scalar(r1[0:1, 0:1], r1[0:1, 0:1],
                                    float(1.0 / (rows * cols)), float(eps),
                                    ALU.mult, ALU.max)
                    T.reciprocal(r1[0:1, 1:2], r1[0:1, 0:1])

                def s4(T):
                    T.matmul(pb[:, 0:2], onesrow[0:1, :], r1[0:1, 0:2],
                             start=True, stop=True)

                def s5(T):
                    T.tensor_copy(wsc[:, scol:scol + 2], pb[:, 0:2])

                B.chain(tag + "t", [("vector", s1), ("tensor", s2),
                                    ("vector", s3), ("tensor", s4),
                                    ("vector", s5)])

    def q_steps(src_ap, w, alpha, beta, xq_ap, scr):
        return [
            ("vector", lambda T: T.tensor_scalar(
                scr["v"][:, 0:w], src_ap, alpha, beta, ALU.mult, ALU.add)),
            ("vector", lambda T: T.tensor_copy(
                scr["i"][:, 0:w], scr["v"][:, 0:w])),
            ("vector", lambda T: T.tensor_copy(
                xq_ap, scr["i"][:, 0:w])),
        ]

    def ln_steps(src_ap, w, sb_, sq, cm, cr, t0, t1):
        def fin(T):
            T.tensor_scalar(sb_[:, cm:cm + 1], sb_[:, t0:t0 + 1], 1.0 / w,
                            None, ALU.mult)
            T.tensor_scalar(sb_[:, t1:t1 + 1], sb_[:, t1:t1 + 1], 1.0 / w,
                            None, ALU.mult)
            T.tensor_tensor(sb_[:, t0:t0 + 1], sb_[:, cm:cm + 1],
                            sb_[:, cm:cm + 1], ALU.mult)
            T.tensor_tensor(sb_[:, t1:t1 + 1], sb_[:, t1:t1 + 1],
                            sb_[:, t0:t0 + 1], ALU.subtract)
            T.tensor_scalar(sb_[:, t1:t1 + 1], sb_[:, t1:t1 + 1], 1.0, LNEPS,
                            ALU.mult, ALU.add)
        return [
            ("vector", lambda T: T.tensor_scalar(
                sq[:, 0:w], src_ap, 1.0, None, ALU.mult, ALU.add,
                accum_out=sb_[:, t0:t0 + 1])),
            ("scalar", lambda T: T.activation(
                sq[:, 0:w], src_ap, AF.Square, bias=0.0, scale=1.0,
                accum_out=sb_[:, t1:t1 + 1])),
            ("vector", fin),
            ("scalar", lambda T: T.activation(
                sb_[:, t0:t0 + 1], sb_[:, t1:t1 + 1], AF.Sqrt,
                bias=0.0, scale=1.0)),
            ("vector", lambda T: T.reciprocal(sb_[:, cr:cr + 1],
                                              sb_[:, t0:t0 + 1])),
        ]

    def qscale_steps(src_ap, w, sb_, sq, cm, cr, cs_, t0, t1):
        def fin(T):
            T.tensor_tensor(sb_[:, t0:t0 + 1], sb_[:, t0:t0 + 1],
                            sb_[:, cm:cm + 1], ALU.subtract)
            T.tensor_tensor(sb_[:, t1:t1 + 1], sb_[:, t1:t1 + 1],
                            sb_[:, cm:cm + 1], ALU.add)
            T.tensor_tensor(sb_[:, t0:t0 + 1], sb_[:, t0:t0 + 1],
                            sb_[:, t1:t1 + 1], ALU.max)
            T.tensor_tensor(sb_[:, t0:t0 + 1], sb_[:, t0:t0 + 1],
                            sb_[:, cr:cr + 1], ALU.mult)
            T.tensor_scalar(sb_[:, t0:t0 + 1], sb_[:, t0:t0 + 1], 1.0,
                            ACTEPS, ALU.mult, ALU.max)
            T.reciprocal(sb_[:, t1:t1 + 1], sb_[:, t0:t0 + 1])
            T.tensor_scalar(sb_[:, cs_:cs_ + 1], sb_[:, t1:t1 + 1], 127.0,
                            None, ALU.mult)
            T.tensor_scalar(sb_[:, t0:t0 + 1], sb_[:, t0:t0 + 1],
                            1.0 / 127.0, None, ALU.mult)
        return [
            ("vector", lambda T: T.tensor_scalar(
                sq[:, 0:w], src_ap, 1.0, None, ALU.mult, ALU.max,
                accum_out=sb_[:, t0:t0 + 1])),
            ("vector", lambda T: T.tensor_scalar(
                sq[:, 0:w], src_ap, -1.0, None, ALU.mult, ALU.max,
                accum_out=sb_[:, t1:t1 + 1])),
            ("vector", fin),
        ]

    def wload(eng, wr, slot, W, r0, c0, sem):
        for q in range(4):
            eng.dma_start(
                out=wr[:, slot * 512 + q * P:slot * 512 + (q + 1) * P],
                in_=W[r0 + q * P:r0 + (q + 1) * P, c0:c0 + P]
            ).then_inc(sem.h, 16)
            sem.inc(16)

    def wtrans(eng, wq_, dcol, wt, slot, sem):
        for q in range(4):
            eng.dma_start(
                out=wq_[:, dcol + q * P:dcol + (q + 1) * P],
                in_=wt[:, slot * 512 + q * P:slot * 512 + (q + 1) * P],
                transpose=True).then_inc(sem.h, 16)
            sem.inc(16)

    def tern3(T, src, dst, scr_a, rcol):
        T.tensor_scalar(scr_a, src, wsc[:, rcol:rcol + 1], 0.5,
                        ALU.mult, ALU.is_gt)
        T.tensor_scalar(dst, src, wsc[:, rcol:rcol + 1], -0.5,
                        ALU.mult, ALU.is_lt)
        T.tensor_tensor(dst, scr_a, dst, ALU.subtract)

    def phase_a1(xqf1T):
        with contextlib.ExitStack() as S:
            t = lambda n, sh, d: S.enter_context(nc.sbuf_tensor(n, sh, d))
            xb = t("a1x", [P, 2 * C], F32)
            xnb = t("a1xn", [P, C], F32)
            sq = t("a1sq", [P, C], F32)
            sb_ = t("a1st", [P, 12], F32)
            scr = {"v": t("a1v", [P, C], F32), "i": t("a1i", [P, C], I32)}
            xqb = t("a1xq", [P, C], BF16)
            pt = S.enter_context(nc.psum_tensor("a1pt", [P, 8 * P], BF16))
            ps = S.enter_context(nc.psum_tensor("a1ps", [1, 2 * P], F32))
            dmas = [B.gsem("dmA"), B.gsem("dmB")]
            steps = []

            def mk_dma(i):
                def f(T):
                    T._e.dma_start(
                        out=xb[:, (i % 2) * C:(i % 2 + 1) * C],
                        in_=x_all[NOWN + i * P:NOWN + (i + 1) * P, :]
                    ).then_inc(dmas[i % 2].h, 16)
                    dmas[i % 2].inc(16)
                return f

            steps.append(("sync", mk_dma(0)))
            for i in range(NT):
                if i + 1 < NT:
                    steps.append(("sync", mk_dma(i + 1)))
                steps.append(("vector",
                              lambda T, s_=dmas[i % 2], n=16 * (i // 2 + 1):
                              T._e.wait_ge(s_.h, s_.val(n))))
                x_t = xb[:, (i % 2) * C:(i % 2 + 1) * C]
                steps += ln_steps(x_t, C, sb_, sq, 0, 1, 8, 9)

                def mkxn(T, x_t=x_t):
                    T.tensor_tensor(sb_[:, 8:9], sb_[:, 0:1], sb_[:, 1:2],
                                    ALU.mult)
                    T.tensor_scalar(sb_[:, 8:9], sb_[:, 8:9], -1.0, None,
                                    ALU.mult)
                    T.tensor_scalar(xnb[:, :], x_t, sb_[:, 1:2], sb_[:, 8:9],
                                    ALU.mult, ALU.add)
                steps.append(("vector", mkxn))
                steps += ln_steps(xnb[:, :], C, sb_, sq, 2, 3, 8, 9)
                steps += qscale_steps(xnb[:, :], C, sb_, sq, 2, 3, 4, 8, 9)

                def slabs(T, i=i):
                    T.tensor_tensor(aff_a[:, i:i + 1], sb_[:, 3:4], sb_[:, 4:5],
                                    ALU.mult)
                    T.tensor_tensor(sb_[:, 9:10], sb_[:, 2:3],
                                    aff_a[:, i:i + 1], ALU.mult)
                    T.tensor_scalar(aff_b[:, i:i + 1], sb_[:, 9:10], -1.0,
                                    None, ALU.mult)
                    T.tensor_tensor(dq1[:, i:i + 1], sb_[:, 8:9], wsc[:, 0:1],
                                    ALU.mult)
                    T.tensor_tensor(isc1[:, i:i + 1], sb_[:, 4:5], wsc[:, 1:2],
                                    ALU.mult)
                steps.append(("vector", slabs))
                steps += q_steps(xnb[:, :], C, aff_a[:, i:i + 1],
                                 aff_b[:, i:i + 1], xqb[:, :], scr)
                for g0 in range(0, CC, 8):
                    nb = min(8, CC - g0)

                    def tr(T, g0=g0, nb=nb):
                        for k in range(nb):
                            T.transpose(pt[:, k * P:(k + 1) * P],
                                        xqb[:, (g0 + k) * P:(g0 + k + 1) * P],
                                        idb[:, :])
                    steps.append(("tensor", tr))

                    def ev(T, i=i, g0=g0, nb=nb):
                        T.tensor_copy(
                            bass.AP(xqf1T, g0 * NOWN + i * P,
                                    [[CC * NOWN, P], [NOWN, nb], [1, P]]),
                            pt[:, 0:nb * P])
                    steps.append(("vector", ev))
            for i in range(NT):
                steps.append(("tensor", lambda T, i=i: T.transpose(
                    ps[0:1, (i % 2) * P:(i % 2 + 1) * P],
                    isc1[:, i:i + 1], idf[:, :])))
                steps.append(("vector", lambda T, i=i: T.tensor_copy(
                    sr1[0:1, i * P:(i + 1) * P],
                    ps[0:1, (i % 2) * P:(i % 2 + 1) * P])))
            B.chain("pa1", steps)

    def phase_f1(xqf1T):
        OSL = F // 512
        NK = OSL * NT
        vt_tern_plan = {}
        vt_stat_plan = {}
        cnt = 0
        for o in range(OSL + 2):
            if o >= 2:
                for i in range(NT):
                    cnt += 3
                    vt_stat_plan[(o - 2) * NT + i] = cnt
            if o < OSL:
                for cc in range(CC):
                    cnt += 3
                    vt_tern_plan[o * CC + cc] = cnt
        vt_total = cnt
        with contextlib.ExitStack() as S:
            t = lambda n, sh, d: S.enter_context(nc.sbuf_tensor(n, sh, d))
            wb = t("f1wb", [P, 2 * 512], F32)
            wt = t("f1wt", [P, 2 * 512], BF16)
            asc = t("f1as", [P, 512], BF16)
            wq = t("f1wq", [P, 2 * CC * 512], BF16)
            hb = t("f1h", [P, 4 * 512], BF16)
            sqs = t("f1sq", [P, 4 * 512], F32)
            b1s = t("b1s", [1, F], F32)
            pp = S.enter_context(nc.psum_tensor("f1p", [P, 2 * 512], F32))
            ld = [B.gsem("ldA"), B.gsem("ldB")]
            tq = [B.gsem("tqA"), B.gsem("tqB")]
            so = [B.gsem(f"gso{q}") for q in range(4)]
            mm = B.gsem("mm")
            vts, vnb = B.ticket("vector")
            ats, anb = B.ticket("scalar")
            vbase, abase = vnb[0], anb[0]
            b1l = B.gsem("bld")
            NK3 = OSL * CC
            with nc.Block() as block:
                @block.sync
                def _(s):
                    s.dma_start(out=b1s[:, :], in_=b1[:, :]).then_inc(b1l.h, 16)
                    b1l.inc(16)
                    for k3 in range(min(2, NK3)):
                        o, cc = divmod(k3, CC)
                        wload(s, wb, k3 % 2, w1, o * 512, cc * P, ld[k3 % 2])
                    for k3 in range(NK3):
                        o, cc = divmod(k3, CC)
                        s.wait_ge(vts, vbase + vt_tern_plan[k3])
                        if cc == 0 and o >= 2:
                            s.wait_ge(mm.h, mm.val((o - 1) * NT))
                        wtrans(s, wq, ((o % 2) * CC + cc) * 512, wt,
                               k3 % 2, tq[k3 % 2])
                        if k3 + 2 < NK3:
                            o2, c2 = divmod(k3 + 2, CC)
                            wload(s, wb, k3 % 2, w1, o2 * 512, c2 * P,
                                  ld[k3 % 2])

                @block.vector
                def _(v):
                    T = B.TE(v, "vector")
                    for o in range(OSL + 2):
                        if o >= 2:
                            for i in range(NT):
                                k = (o - 2) * NT + i
                                v.wait_ge(ats, abase + 2 * k + 2)
                                hsl = hb[:, (k % 4) * 512:(k % 4 + 1) * 512]
                                slot = i * OSL + (o - 2)
                                T.tensor_scalar(sqs[:, 0:512], hsl, 1.0, None,
                                                ALU.mult, ALU.add,
                                                accum_out=hsum[:, slot:slot + 1])
                                T.tensor_scalar(sqs[:, 512:1024], hsl, 1.0,
                                                None, ALU.mult, ALU.max,
                                                accum_out=hmx[:, slot:slot + 1])
                                T.tensor_scalar(sqs[:, 1024:1536], hsl, -1.0,
                                                None, ALU.mult, ALU.max,
                                                accum_out=hmn[:, slot:slot + 1])
                                assert T.n == vbase + vt_stat_plan[k]
                        if o < OSL:
                            for cc in range(CC):
                                k3 = o * CC + cc
                                v.wait_ge(ld[k3 % 2].h,
                                          ld[k3 % 2].val(64 * (k3 // 2 + 1)))
                                if k3 >= 2:
                                    v.wait_ge(tq[k3 % 2].h,
                                              tq[k3 % 2].val(64 * (k3 // 2)))
                                tern3(T,
                                      wb[:, (k3 % 2) * 512:(k3 % 2 + 1) * 512],
                                      wt[:, (k3 % 2) * 512:(k3 % 2 + 1) * 512],
                                      asc[:, :], 1)
                                assert T.n == vbase + vt_tern_plan[k3]
                    assert T.n == vbase + vt_total

                @block.tensor
                def _(te):
                    te.wait_ge(b1l.h, b1l.val(16))
                    for o in range(OSL):
                        ke, ko = o * CC + CC - 2, o * CC + CC - 1
                        te.wait_ge(tq[0].h, tq[0].val(64 * (ke // 2 + 1)))
                        te.wait_ge(tq[1].h, tq[1].val(64 * (ko // 2 + 1)))
                        for i in range(NT):
                            k = o * NT + i
                            if k >= 2:
                                te.wait_ge(ats, abase + 2 * (k - 2) + 1)
                            psl = pp[:, (k % 2) * 512:(k % 2 + 1) * 512]
                            for cc in range(CC):
                                te.matmul(
                                    psl,
                                    xqf1T[:, cc * NOWN + i * P:
                                          cc * NOWN + (i + 1) * P],
                                    wq[:, ((o % 2) * CC + cc) * 512:
                                       ((o % 2) * CC + cc + 1) * 512],
                                    start=(cc == 0), stop=False)
                            te.matmul(psl, sr1[0:1, i * P:(i + 1) * P],
                                      b1s[0:1, o * 512:(o + 1) * 512],
                                      start=False, stop=True).then_inc(mm.h, 1)
                            mm.inc(1)

                @block.scalar
                def _(s):
                    T = B.TE(s, "scalar")
                    for k in range(NK):
                        o, i = divmod(k, NT)
                        s.wait_ge(mm.h, mm.val(k + 1))
                        if k >= 4:
                            s.wait_ge(so[k % 4].h, so[k % 4].val(16 * (k // 4)))
                            s.wait_ge(vts, vbase + vt_stat_plan[k - 4])
                        psl = pp[:, (k % 2) * 512:(k % 2 + 1) * 512]
                        hsl = hb[:, (k % 4) * 512:(k % 4 + 1) * 512]
                        T.activation(hsl, psl, AF.Gelu, bias=0.0,
                                     scale=dq1[:, i:i + 1])
                        T.activation(sqs[:, 1536:2048], hsl, AF.Square,
                                     bias=0.0, scale=1.0,
                                     accum_out=hsq[:, i * OSL + o:
                                                   i * OSL + o + 1])
                    assert T.n == abase + 2 * NK

                @block.gpsimd
                def _(g):
                    for k in range(NK):
                        o, i = divmod(k, NT)
                        g.wait_ge(vts, vbase + vt_stat_plan[k])
                        g.dma_start(out=h_d[i * P:(i + 1) * P,
                                            o * 512:(o + 1) * 512],
                                    in_=hb[:, (k % 4) * 512:(k % 4 + 1) * 512]
                                    ).then_inc(so[k % 4].h, 16)
                        so[k % 4].inc(16)
                    for sl in range(4):
                        g.wait_ge(so[sl].h,
                                  so[sl].val(16 * ((NK - 1 - sl) // 4 + 1)))

    def phase_h():
        OSL = F // 512
        with contextlib.ExitStack() as S:
            t = lambda n, sh, d: S.enter_context(nc.sbuf_tensor(n, sh, d))
            hb = t("phh", [P, 2 * F], BF16)
            sb_ = t("phst", [P, 12], F32)
            scr = {"v": t("phv", [P, F], F32), "i": t("phi", [P, F], I32)}
            ps = S.enter_context(nc.psum_tensor("phps", [1, 2 * P], F32))
            dmas = [B.gsem("dmA"), B.gsem("dmB")]
            sto = [B.gsem("stA"), B.gsem("stB")]
            steps = []

            def mk_dma(i):
                def f(T):
                    if i >= 2:
                        T._e.wait_ge(sto[i % 2].h,
                                     sto[i % 2].val(16 * ((i - 2) // 2 + 1)))
                    T._e.dma_start(out=hb[:, (i % 2) * F:(i % 2 + 1) * F],
                                   in_=h_d[i * P:(i + 1) * P, :]
                                   ).then_inc(dmas[i % 2].h, 16)
                    dmas[i % 2].inc(16)
                return f

            steps.append(("sync", mk_dma(0)))
            for i in range(NT):
                if i + 1 < NT:
                    steps.append(("sync", mk_dma(i + 1)))
                steps.append(("vector",
                              lambda T, s_=dmas[i % 2], n=16 * (i // 2 + 1):
                              T._e.wait_ge(s_.h, s_.val(n))))
                h_t = hb[:, (i % 2) * F:(i % 2 + 1) * F]

                def agg(T, i=i):
                    T.reduce_sum(sb_[:, 8:9], hsum[:, i * OSL:(i + 1) * OSL],
                                 axis=AX)
                    T.reduce_sum(sb_[:, 9:10], hsq[:, i * OSL:(i + 1) * OSL],
                                 axis=AX)
                    T.reduce_max(sb_[:, 10:11], hmx[:, i * OSL:(i + 1) * OSL],
                                 axis=AX)
                    T.reduce_max(sb_[:, 11:12], hmn[:, i * OSL:(i + 1) * OSL],
                                 axis=AX)
                    T.tensor_scalar(sb_[:, 0:1], sb_[:, 8:9], 1.0 / F, None,
                                    ALU.mult)
                    T.tensor_scalar(sb_[:, 9:10], sb_[:, 9:10], 1.0 / F, None,
                                    ALU.mult)
                    T.tensor_tensor(sb_[:, 8:9], sb_[:, 0:1], sb_[:, 0:1],
                                    ALU.mult)
                    T.tensor_tensor(sb_[:, 9:10], sb_[:, 9:10], sb_[:, 8:9],
                                    ALU.subtract)
                    T.tensor_scalar(sb_[:, 9:10], sb_[:, 9:10], 1.0, LNEPS,
                                    ALU.mult, ALU.add)
                steps.append(("vector", agg))
                steps.append(("scalar", lambda T: T.activation(
                    sb_[:, 8:9], sb_[:, 9:10], AF.Sqrt, bias=0.0, scale=1.0)))

                def fin(T, i=i):
                    T.reciprocal(sb_[:, 1:2], sb_[:, 8:9])
                    T.tensor_tensor(sb_[:, 10:11], sb_[:, 10:11], sb_[:, 0:1],
                                    ALU.subtract)
                    T.tensor_tensor(sb_[:, 11:12], sb_[:, 11:12], sb_[:, 0:1],
                                    ALU.add)
                    T.tensor_tensor(sb_[:, 10:11], sb_[:, 10:11],
                                    sb_[:, 11:12], ALU.max)
                    T.tensor_tensor(sb_[:, 10:11], sb_[:, 10:11], sb_[:, 1:2],
                                    ALU.mult)
                    T.tensor_scalar(sb_[:, 10:11], sb_[:, 10:11], 1.0, ACTEPS,
                                    ALU.mult, ALU.max)
                    T.reciprocal(sb_[:, 11:12], sb_[:, 10:11])
                    T.tensor_scalar(sb_[:, 4:5], sb_[:, 11:12], 127.0, None,
                                    ALU.mult)
                    T.tensor_scalar(sb_[:, 10:11], sb_[:, 10:11], 1.0 / 127.0,
                                    None, ALU.mult)
                    T.tensor_tensor(sb_[:, 2:3], sb_[:, 1:2], sb_[:, 4:5],
                                    ALU.mult)
                    T.tensor_tensor(sb_[:, 3:4], sb_[:, 0:1], sb_[:, 2:3],
                                    ALU.mult)
                    T.tensor_scalar(sb_[:, 3:4], sb_[:, 3:4], -1.0, None,
                                    ALU.mult)
                    T.tensor_tensor(dq2[:, i:i + 1], sb_[:, 10:11],
                                    wsc[:, 2:3], ALU.mult)
                    T.tensor_tensor(isc2[:, i:i + 1], sb_[:, 4:5], wsc[:, 3:4],
                                    ALU.mult)
                steps.append(("vector", fin))
                steps += q_steps(h_t, F, sb_[:, 2:3], sb_[:, 3:4], h_t, scr)

                def sto_f(T, i=i):
                    T._e.dma_start(out=xqh_d[i * P:(i + 1) * P, :],
                                   in_=hb[:, (i % 2) * F:(i % 2 + 1) * F]
                                   ).then_inc(sto[i % 2].h, 16)
                    sto[i % 2].inc(16)
                steps.append(("scalar", sto_f))
            def wait_all_sto(T):
                for sl in range(2):
                    T._e.wait_ge(sto[sl].h,
                                 sto[sl].val(16 * ((NT - 1 - sl) // 2 + 1)))
            steps.append(("scalar", wait_all_sto))
            for i in range(NT):
                steps.append(("tensor", lambda T, i=i: T.transpose(
                    ps[0:1, (i % 2) * P:(i % 2 + 1) * P],
                    isc2[:, i:i + 1], idf[:, :])))
                steps.append(("vector", lambda T, i=i: T.tensor_copy(
                    sr2[0:1, i * P:(i + 1) * P],
                    ps[0:1, (i % 2) * P:(i % 2 + 1) * P])))
            B.chain("ph", steps)

    def phase_f2(ffn_acc):
        OSL = C // 512
        NCH = OSL * FC

        def mmv(k):
            o, fc = divmod(k, FC)
            return (o * (FC + 1) + fc + 1) * NT

        vt_tern_plan = {}
        vt_evac_plan = {}
        cnt = 0
        for o in range(OSL):
            for fc in range(FC):
                if fc == 2 and o >= 1:
                    cnt += NT
                    vt_evac_plan[o - 1] = cnt
                cnt += 3
                vt_tern_plan[o * FC + fc] = cnt
        cnt += NT
        vt_evac_plan[OSL - 1] = cnt
        vt_total = cnt

        with contextlib.ExitStack() as S:
            t = lambda n, sh, d: S.enter_context(nc.sbuf_tensor(n, sh, d))
            wb = t("f2wb", [P, 2 * 512], F32)
            wt = t("f2wt", [P, 2 * 512], BF16)
            wq = t("f2wq", [P, 2 * 512], BF16)
            asc = t("f2as", [P, 512], BF16)
            xb = t("f2xb", [P, 2 * NOWN], BF16)
            b2s = t("b2s", [1, C], F32)
            pp = S.enter_context(nc.psum_tensor("f2p", [P, 8 * 512], F32))
            ld = [B.gsem("ldA"), B.gsem("ldB")]
            tq = [B.gsem("tqA"), B.gsem("tqB")]
            xl = [B.gsem("xlA"), B.gsem("xlB")]
            mm = B.gsem("mm")
            vts, vnb = B.ticket("vector")
            vbase = vnb[0]
            b2l = B.gsem("bld")
            with nc.Block() as block:
                @block.sync
                def _(s):
                    s.dma_start(out=b2s[:, :], in_=b2[:, :]).then_inc(b2l.h, 16)
                    b2l.inc(16)
                    for k in range(min(2, NCH)):
                        o, fc = divmod(k, FC)
                        wload(s, wb, k % 2, w2, o * 512, fc * P, ld[k % 2])
                    for k in range(NCH):
                        o, fc = divmod(k, FC)
                        if k >= 2:
                            s.wait_ge(mm.h, mm.val(mmv(k - 2)))
                        s.dma_start(
                            out=xb[:, (k % 2) * NOWN:(k % 2 + 1) * NOWN],
                            in_=xqh_d[0:NOWN, fc * P:(fc + 1) * P],
                            transpose=True).then_inc(xl[k % 2].h, 16)
                        xl[k % 2].inc(16)
                        s.wait_ge(vts, vbase + vt_tern_plan[k])
                        wtrans(s, wq, (k % 2) * 512, wt, k % 2, tq[k % 2])
                        if k + 2 < NCH:
                            o2, f2c = divmod(k + 2, FC)
                            wload(s, wb, k % 2, w2, o2 * 512, f2c * P,
                                  ld[k % 2])

                @block.vector
                def _(v):
                    T = B.TE(v, "vector")

                    def evac(o):
                        v.wait_ge(mm.h, mm.val((o + 1) * (FC + 1) * NT))
                        for i in range(NT):
                            T.tensor_scalar(
                                ffn_acc[:, i * C + o * 512:
                                        i * C + (o + 1) * 512],
                                pp[:, i * 512:(i + 1) * 512], dq2[:, i:i + 1],
                                None, ALU.mult)
                    for o in range(OSL):
                        for fc in range(FC):
                            k = o * FC + fc
                            if fc == 2 and o >= 1:
                                evac(o - 1)
                                assert T.n == vbase + vt_evac_plan[o - 1]
                            v.wait_ge(ld[k % 2].h,
                                      ld[k % 2].val(64 * (k // 2 + 1)))
                            if k >= 2:
                                v.wait_ge(tq[k % 2].h,
                                          tq[k % 2].val(64 * (k // 2)))
                            tern3(T, wb[:, (k % 2) * 512:(k % 2 + 1) * 512],
                                  wt[:, (k % 2) * 512:(k % 2 + 1) * 512],
                                  asc[:, :], 3)
                            assert T.n == vbase + vt_tern_plan[k]
                    evac(OSL - 1)
                    assert T.n == vbase + vt_total

                @block.tensor
                def _(te):
                    te.wait_ge(b2l.h, b2l.val(16))
                    for o in range(OSL):
                        if o >= 1:
                            te.wait_ge(vts, vbase + vt_evac_plan[o - 1])
                        for fc in range(FC):
                            k = o * FC + fc
                            te.wait_ge(tq[k % 2].h,
                                       tq[k % 2].val(64 * (k // 2 + 1)))
                            te.wait_ge(xl[k % 2].h,
                                       xl[k % 2].val(16 * (k // 2 + 1)))
                            for i in range(NT):
                                mi = te.matmul(pp[:, i * 512:(i + 1) * 512],
                                               xb[:, (k % 2) * NOWN + i * P:
                                                  (k % 2) * NOWN + (i + 1) * P],
                                               wq[:, (k % 2) * 512:
                                                  (k % 2 + 1) * 512],
                                               start=(fc == 0), stop=False)
                                if i == NT - 1:
                                    mi.then_inc(mm.h, NT)
                                    mm.inc(NT)
                        for i in range(NT):
                            mi = te.matmul(pp[:, i * 512:(i + 1) * 512],
                                           sr2[0:1, i * P:(i + 1) * P],
                                           b2s[0:1, o * 512:(o + 1) * 512],
                                           start=False, stop=True)
                            if i == NT - 1:
                                mi.then_inc(mm.h, NT)
                                mm.inc(NT)

    def phase_a2(bck, bcq):
        with contextlib.ExitStack() as S:
            t = lambda n, sh, d: S.enter_context(nc.sbuf_tensor(n, sh, d))
            xb = t("a2x", [P, 2 * C], F32)
            sq = t("a2sq", [P, C], F32)
            sb_ = t("a2st", [P, 12], F32)
            scr = {"v": t("a2v", [P, C], F32), "i": t("a2i", [P, C], I32)}
            xqb = t("a2xq", [P, 2 * C], BF16)
            svr = t("a2sv", [1, MT * P], F32)
            ps = S.enter_context(nc.psum_tensor("a2ps", [1, 2 * P], F32))
            pb = S.enter_context(nc.psum_tensor("a2pb", [P, 2 * P], F32))
            dmas = [B.gsem("dmA"), B.gsem("dmB")]
            sto = [B.gsem("stA"), B.gsem("stB")]
            steps = []

            def mk_dma(t_):
                def f(T):
                    T._e.dma_start(out=xb[:, (t_ % 2) * C:(t_ % 2 + 1) * C],
                                   in_=x_all[t_ * P:(t_ + 1) * P, :]
                                   ).then_inc(dmas[t_ % 2].h, 16)
                    dmas[t_ % 2].inc(16)
                return f

            steps.append(("sync", mk_dma(0)))
            for t_ in range(MT):
                if t_ + 1 < MT:
                    steps.append(("sync", mk_dma(t_ + 1)))
                steps.append(("vector",
                              lambda T, s_=dmas[t_ % 2], n=16 * (t_ // 2 + 1):
                              T._e.wait_ge(s_.h, s_.val(n))))
                x_t = xb[:, (t_ % 2) * C:(t_ % 2 + 1) * C]
                steps += ln_steps(x_t, C, sb_, sq, 0, 1, 8, 9)
                steps += qscale_steps(x_t, C, sb_, sq, 0, 1, 4, 8, 9)

                def slabs(T, t_=t_):
                    T.tensor_tensor(sb_[:, 2:3], sb_[:, 1:2], sb_[:, 4:5],
                                    ALU.mult)
                    T.tensor_tensor(sb_[:, 3:4], sb_[:, 0:1], sb_[:, 2:3],
                                    ALU.mult)
                    T.tensor_scalar(sb_[:, 3:4], sb_[:, 3:4], -1.0, None,
                                    ALU.mult)
                    T.tensor_tensor(dqv[:, t_:t_ + 1], sb_[:, 8:9],
                                    wsc[:, 4:5], ALU.mult)
                steps.append(("vector", slabs))
                if t_ >= 2:
                    steps.append(("vector",
                                  lambda T, s_=sto[t_ % 2],
                                  n=16 * ((t_ - 2) // 2 + 1):
                                  T._e.wait_ge(s_.h, s_.val(n))))
                steps += q_steps(x_t, C, sb_[:, 2:3], sb_[:, 3:4],
                                 xqb[:, (t_ % 2) * C:(t_ % 2 + 1) * C], scr)

                def sto_f(T, t_=t_):
                    T._e.dma_start(out=xqa_d[t_ * P:(t_ + 1) * P, :],
                                   in_=xqb[:, (t_ % 2) * C:(t_ % 2 + 1) * C]
                                   ).then_inc(sto[t_ % 2].h, 16)
                    sto[t_ % 2].inc(16)
                steps.append(("scalar", sto_f))
            def wait_all_sto(T):
                for sl in range(2):
                    T._e.wait_ge(sto[sl].h,
                                 sto[sl].val(16 * ((MT - 1 - sl) // 2 + 1)))
            steps.append(("scalar", wait_all_sto))
            for t_ in range(MT):
                steps.append(("tensor", lambda T, t_=t_: T.transpose(
                    ps[0:1, (t_ % 2) * P:(t_ % 2 + 1) * P],
                    dqv[:, t_:t_ + 1], idf[:, :])))
                steps.append(("vector", lambda T, t_=t_: T.tensor_copy(
                    svr[0:1, t_ * P:(t_ + 1) * P],
                    ps[0:1, (t_ % 2) * P:(t_ % 2 + 1) * P])))
            for t_ in range(MT):
                def bc(T, t_=t_):
                    T.matmul(pb[:, (t_ % 2) * P:(t_ % 2 + 1) * P],
                             onesrow[0:1, :], svr[0:1, t_ * P:(t_ + 1) * P],
                             start=True, stop=True)
                steps.append(("tensor", bc))

                def bcev(T, t_=t_):
                    T.tensor_copy(bck[:, t_ * P:(t_ + 1) * P],
                                  pb[:, (t_ % 2) * P:(t_ % 2 + 1) * P])
                    if t_ >= QOFF:
                        T.tensor_scalar(
                            bcq[:, (t_ - QOFF) * P:(t_ - QOFF + 1) * P],
                            pb[:, (t_ % 2) * P:(t_ % 2 + 1) * P],
                            float(cfg.SCALE), None, ALU.mult)
                steps.append(("vector", bcev))
            B.chain("pa2", steps)

    def phase_att(attn_tm, bck, bcq):
        SLAB = cfg.SLAB_S
        KL, soff = cfg.KL, cfg.soff
        MAXKV = MT * P
        NP = 5
        with contextlib.ExitStack() as S:
            t = lambda n, sh, d: S.enter_context(nc.sbuf_tensor(n, sh, d))
            xsl = t("atxs", [P, 2 * NOWN], BF16)
            wvb = t("atwv", [P, 2 * 512], F32)
            wvt = t("atwt", [P, 2 * 512], BF16)
            wvq = t("atwvq", [P, 2 * 512], BF16)
            ascv = t("atas", [P, 512], BF16)
            vbuf = t("atvb", [P, MT * HG * 129], BF16)
            qtm = t("atqtm", [P, NT * HG * P], BF16)
            ktm = t("atktm", [P, MT * HG * P], BF16)
            qfm = t("atqfm", [P, NOWN], BF16)
            kfm = t("atkfm", [P, TALL], BF16)
            pT = t("atpT", [P, 2 * MT * P], BF16)
            msk = t("atms", [P, SLAB], BF16)
            sst = t("atss", [P, MAXKV], F32)
            pS = t("atps", [P, 2 * MAXKV], BF16)
            ssc = t("atsc", [P, 512], F32)
            rms = t("atrm", [P, 8], F32)
            nrm = t("atnr", [P, 2], F32)
            rcd = t("atrcd", [P, 2], F32)
            mld = B.gsem("mld")
            with nc.Block() as block:
                @block.sync
                def _(s):
                    NSL = (SLAB + 8191) // 8192
                    for k in range(NSL):
                        a, b_ = k * 8192, min(SLAB, (k + 1) * 8192)
                        s.dma_start(out=msk[:, a:b_], in_=masks[:, a:b_]
                                    ).then_inc(mld.h, 16)
                        mld.inc(16)
                    s.wait_ge(mld.h, mld.val(16 * NSL))

                @block.vector
                def _(v):
                    v.memset(bass.AP(vbuf, P, [[MT * HG * 129, P],
                                               [129, MT * HG], [1, 1]]), 1.0)

            for g in range(NG):
                qoff, koff, voff = g * HG * P, C + g * HG * P, 2 * C + g * HG * P
                passes = [("q", qoff, QOFF, MT),
                          ("kA", koff, 0, MT // 2), ("kB", koff, MT // 2, MT),
                          ("vA", voff, 0, MT // 2), ("vB", voff, MT // 2, MT)]
                xl = [B.gsem("xlA"), B.gsem("xlB")]
                wl = [B.gsem("wlA"), B.gsem("wlB")]
                tq = [B.gsem("tqA"), B.gsem("tqB")]
                mm = B.gsem("mm")
                vts, vnb = B.ticket("vector")
                vbase = vnb[0]

                def ev_ops(pn):
                    nm, off, t0, t1 = passes[pn]
                    return (t1 - t0) * (HG if nm.startswith("v") else 1)

                vt_tern_plan = {}
                vt_evac_plan = {}
                cnt = 0
                for pn in range(NP):
                    if pn >= 1:
                        cnt += ev_ops(pn - 1)
                        vt_evac_plan[pn - 1] = cnt
                    for cc in range(CC):
                        cnt += 3
                        vt_tern_plan[pn * CC + cc] = cnt
                cnt += ev_ops(NP - 1)
                vt_evac_plan[NP - 1] = cnt
                vt_total = cnt

                with nc.psum_tensor(f"atp{g}", [P, 8 * 512], F32) as pqv, \
                        nc.Block() as block:
                    @block.sync
                    def _(s):
                        for pn, (nm, off, t0, t1) in enumerate(passes):
                            for cc in range(CC):
                                k = pn * CC + cc
                                if k >= 2:
                                    s.wait_ge(mm.h, mm.val(k - 1))
                                s.dma_start(
                                    out=xsl[:, (k % 2) * NOWN:
                                            (k % 2) * NOWN + (t1 - t0) * P],
                                    in_=xqa_d[t0 * P:t1 * P,
                                              cc * P:(cc + 1) * P],
                                    transpose=True).then_inc(xl[k % 2].h, 16)
                                xl[k % 2].inc(16)
                                s.wait_ge(vts, vbase + vt_tern_plan[k])
                                wtrans(s, wvq, (k % 2) * 512, wvt, k % 2,
                                       tq[k % 2])

                    @block.scalar
                    def _(s):
                        NKQ = NP * CC
                        for k in range(min(2, NKQ)):
                            pn, cc = divmod(k, CC)
                            wload(s, wvb, k % 2, w_qkv, passes[pn][1],
                                  cc * P, wl[k % 2])
                        for k in range(NKQ):
                            if k + 2 < NKQ:
                                s.wait_ge(vts, vbase + vt_tern_plan[k])
                                pn2, cc2 = divmod(k + 2, CC)
                                wload(s, wvb, k % 2, w_qkv, passes[pn2][1],
                                      cc2 * P, wl[k % 2])

                    @block.vector
                    def _(v):
                        T = B.TE(v, "vector")

                        def evac(pn):
                            nm, off, t0, t1 = passes[pn]
                            v.wait_ge(mm.h, mm.val((pn + 1) * CC))
                            for mt in range(t0, t1):
                                psl_ = pqv[:, (mt - t0) * 512:
                                           (mt - t0 + 1) * 512]
                                if nm.startswith("v"):
                                    for hh in range(HG):
                                        T.tensor_scalar(
                                            vbuf[:, mt * HG * 129 + hh * 129:
                                                 mt * HG * 129 + hh * 129 + P],
                                            psl_[:, hh * P:(hh + 1) * P],
                                            dqv[:, mt:mt + 1], None, ALU.mult)
                                elif nm.startswith("k"):
                                    T.tensor_copy(
                                        ktm[:, mt * HG * P:(mt + 1) * HG * P],
                                        psl_)
                                else:
                                    T.tensor_copy(
                                        qtm[:, (mt - t0) * HG * P:
                                            (mt - t0 + 1) * HG * P], psl_)

                        for pn in range(NP):
                            if pn >= 1:
                                evac(pn - 1)
                                assert T.n == vbase + vt_evac_plan[pn - 1]
                            for cc in range(CC):
                                k = pn * CC + cc
                                v.wait_ge(wl[k % 2].h,
                                          wl[k % 2].val(64 * (k // 2 + 1)))
                                if k >= 2:
                                    v.wait_ge(tq[k % 2].h,
                                              tq[k % 2].val(64 * (k // 2)))
                                tern3(T,
                                      wvb[:, (k % 2) * 512:(k % 2 + 1) * 512],
                                      wvt[:, (k % 2) * 512:(k % 2 + 1) * 512],
                                      ascv[:, :], 5)
                                assert T.n == vbase + vt_tern_plan[k]
                        evac(NP - 1)
                        assert T.n == vbase + vt_total

                    @block.tensor
                    def _(te):
                        for pn, (nm, off, t0, t1) in enumerate(passes):
                            if pn >= 1:
                                te.wait_ge(vts, vbase + vt_evac_plan[pn - 1])
                            for cc in range(CC):
                                k = pn * CC + cc
                                te.wait_ge(tq[k % 2].h,
                                           tq[k % 2].val(64 * (k // 2 + 1)))
                                te.wait_ge(xl[k % 2].h,
                                           xl[k % 2].val(16 * (k // 2 + 1)))
                                for mt in range(t0, t1):
                                    mi = te.matmul(
                                        pqv[:, (mt - t0) * 512:
                                            (mt - t0 + 1) * 512],
                                        xsl[:, (k % 2) * NOWN + (mt - t0) * P:
                                            (k % 2) * NOWN +
                                            (mt - t0 + 1) * P],
                                        wvq[:, (k % 2) * 512:
                                            (k % 2 + 1) * 512],
                                        start=(cc == 0), stop=(cc == CC - 1))
                                    if mt == t1 - 1:
                                        mi.then_inc(mm.h, 1)
                                        mm.inc(1)

                nq, nk = (NT + 7) // 8, (MT + 7) // 8
                VQ0 = nq + nk
                NCH = [(KL[i] * P + 511) // 512 for i in range(NT)]
                TBn = [(KL[i] + 7) // 8 for i in range(NT)]
                vst_end = [0] * NT
                vpte_end = [0] * NT
                vpav_end = [0] * NT
                cnt2 = VQ0
                for i in range(NT):
                    cnt2 += 2 * NCH[i] + 2
                    vst_end[i] = cnt2
                    cnt2 += TBn[i]
                    vpte_end[i] = cnt2
                    cnt2 += 2
                    vpav_end[i] = cnt2
                VTOT = cnt2
                mm_sc_s = [0] * NT
                mm_av_e = [0] * NT
                m2 = 0
                for i in range(NT):
                    mm_sc_s[i] = m2
                    m2 += NCH[i] + 1
                    mm_av_e[i] = m2
                evac_vop = [b + 1 for b in range(VQ0)]
                tse_end = [0] * NT
                tb_c = VQ0
                for i in range(NT):
                    for tx in range(TBn[i]):
                        evac_vop.append(vst_end[i] + tx + 1)
                    tb_c += TBn[i]
                    tse_end[i] = tb_c

                with nc.psum_tensor(f"asc{g}", [P, 4 * 512], F32) as psS, \
                        nc.psum_tensor(f"aav{g}", [P, 2 * 512], F32) as pav, \
                        nc.psum_tensor(f"atr{g}", [P, 16 * P], BF16) as ptr:
                    for hh in range(HG):
                        h = g * HG + hh
                        tsem = B.gsem("tsem")
                        mmh = B.gsem("mmh")
                        vts2, vnb2 = B.ticket("vector")
                        ats2, anb2 = B.ticket("scalar")
                        v2b, a2b = vnb2[0], anb2[0]
                        with nc.Block() as block:
                            @block.tensor
                            def _(te):
                                tb = 0
                                for src_, n_ in ((qtm, NT), (ktm, MT)):
                                    for i in range(n_):
                                        if tb >= 2 and i % 8 == 0:
                                            te.wait_ge(vts2,
                                                       v2b + evac_vop[tb - 2])
                                        ti = te.transpose(
                                            ptr[:, (tb % 2) * 8 * P +
                                                (i % 8) * P:
                                                (tb % 2) * 8 * P +
                                                (i % 8 + 1) * P],
                                            src_[:, i * HG * P + hh * P:
                                                 i * HG * P + (hh + 1) * P],
                                            idb[:, :])
                                        if i % 8 == 7 or i == n_ - 1:
                                            ti.then_inc(tsem.h, 1)
                                            tsem.inc(1)
                                            tb += 1
                                for i in range(NT):
                                    kvlen = KL[i] * P
                                    te.wait_ge(vts2, v2b + (vst_end[i - 1]
                                                            if i else VQ0))
                                    for c in range(NCH[i]):
                                        wc = min(512, kvlen - c * 512)
                                        te.matmul(
                                            psS[:, c * 512:c * 512 + wc],
                                            qfm[:, i * P:(i + 1) * P],
                                            kfm[:, c * 512:c * 512 + wc],
                                            start=True, stop=True
                                            ).then_inc(mmh.h, 1)
                                        mmh.inc(1)
                                    te.wait_ge(ats2, a2b + i + 1)
                                    for j in range(KL[i]):
                                        if tb >= 2 and j % 8 == 0:
                                            te.wait_ge(vts2,
                                                       v2b + evac_vop[tb - 2])
                                        ti = te.transpose(
                                            ptr[:, (tb % 2) * 8 * P +
                                                (j % 8) * P:
                                                (tb % 2) * 8 * P +
                                                (j % 8 + 1) * P],
                                            pS[:, (i % 2) * MAXKV + j * P:
                                               (i % 2) * MAXKV + (j + 1) * P],
                                            idb[:, :])
                                        if j % 8 == 7 or j == KL[i] - 1:
                                            ti.then_inc(tsem.h, 1)
                                            tsem.inc(1)
                                            tb += 1
                                    te.wait_ge(vts2, v2b + vpte_end[i])
                                    if i >= 2:
                                        te.wait_ge(vts2, v2b + vpav_end[i - 2])
                                    for j in range(KL[i]):
                                        mmi = te.matmul(
                                            pav[:, (i % 2) * 512:
                                                (i % 2) * 512 + 129],
                                            pT[:, (i % 2) * MAXKV + j * P:
                                               (i % 2) * MAXKV + (j + 1) * P],
                                            vbuf[:, j * HG * 129 + hh * 129:
                                                 j * HG * 129 +
                                                 (hh + 1) * 129],
                                            start=(j == 0),
                                            stop=(j == KL[i] - 1))
                                        if j == KL[i] - 1:
                                            mmi.then_inc(mmh.h, 1)
                                            mmh.inc(1)

                            @block.vector
                            def _(v):
                                T = B.TE(v, "vector")
                                tb = 0
                                for bq in range(nq):
                                    v.wait_ge(tsem.h, tsem.val(tb + 1))
                                    nb = min(8, NT - bq * 8)
                                    T.tensor_tensor(
                                        qfm[:, bq * 8 * P:bq * 8 * P + nb * P],
                                        ptr[:, (tb % 2) * 8 * P:
                                            (tb % 2) * 8 * P + nb * P],
                                        bcq[:, bq * 8 * P:bq * 8 * P + nb * P],
                                        ALU.mult)
                                    tb += 1
                                for bk in range(nk):
                                    v.wait_ge(tsem.h, tsem.val(tb + 1))
                                    nb = min(8, MT - bk * 8)
                                    T.tensor_tensor(
                                        kfm[:, bk * 8 * P:bk * 8 * P + nb * P],
                                        ptr[:, (tb % 2) * 8 * P:
                                            (tb % 2) * 8 * P + nb * P],
                                        bck[:, bk * 8 * P:bk * 8 * P + nb * P],
                                        ALU.mult)
                                    tb += 1
                                assert T.n == v2b + VQ0
                                for i in range(NT):
                                    kvlen = KL[i] * P
                                    for c in range(NCH[i]):
                                        wc = min(512, kvlen - c * 512)
                                        v.wait_ge(mmh.h,
                                                  mmh.val(mm_sc_s[i] + c + 1))
                                        if c == 0 and i >= 1:
                                            v.wait_ge(ats2, a2b + i)
                                        T.scalar_tensor_tensor(
                                            sst[:, c * 512:c * 512 + wc],
                                            psS[:, c * 512:c * 512 + wc],
                                            1.0,
                                            msk[:, soff[i] + c * 512:
                                                soff[i] + c * 512 + wc],
                                            ALU.mult, ALU.add)
                                        T.tensor_scalar(
                                            ssc[:, 0:wc],
                                            sst[:, c * 512:c * 512 + wc],
                                            1.0, None, ALU.mult, ALU.max,
                                            accum_out=rms[:, c:c + 1])
                                    T.reduce_max(nrm[:, 0:1],
                                                 rms[:, 0:NCH[i]], axis=AX)
                                    T.tensor_scalar(nrm[:, 1:2], nrm[:, 0:1],
                                                    -1.0, None, ALU.mult)
                                    assert T.n == v2b + vst_end[i]
                                    for tx in range(TBn[i]):
                                        v.wait_ge(tsem.h,
                                                  tsem.val(tse_end[i] -
                                                           TBn[i] + tx + 1))
                                        if tx == 0 and i >= 2:
                                            v.wait_ge(mmh.h,
                                                      mmh.val(mm_av_e[i - 2]))
                                        nb = min(8, KL[i] - tx * 8)
                                        tbg = tse_end[i] - TBn[i] + tx
                                        T.tensor_copy(
                                            pT[:, (i % 2) * MAXKV + tx * 8 * P:
                                               (i % 2) * MAXKV +
                                               tx * 8 * P + nb * P],
                                            ptr[:, (tbg % 2) * 8 * P:
                                                (tbg % 2) * 8 * P + nb * P])
                                    assert T.n == v2b + vpte_end[i]
                                    v.wait_ge(mmh.h, mmh.val(mm_av_e[i]))
                                    T.reciprocal(
                                        rcd[:, 0:1],
                                        pav[:, (i % 2) * 512 + P:
                                            (i % 2) * 512 + 129])
                                    T.tensor_scalar(
                                        attn_tm[:, i * C + h * P:
                                                i * C + (h + 1) * P],
                                        pav[:, (i % 2) * 512:
                                            (i % 2) * 512 + P],
                                        rcd[:, 0:1], None, ALU.mult)
                                    assert T.n == v2b + vpav_end[i]
                                assert T.n == v2b + VTOT

                            @block.scalar
                            def _(s):
                                T = B.TE(s, "scalar")
                                for i in range(NT):
                                    kvlen = KL[i] * P
                                    s.wait_ge(vts2, v2b + vst_end[i])
                                    if i >= 2:
                                        s.wait_ge(tsem.h,
                                                  tsem.val(tse_end[i - 2]))
                                    T.activation(
                                        pS[:, (i % 2) * MAXKV:
                                           (i % 2) * MAXKV + kvlen],
                                        sst[:, 0:kvlen], AF.Exp,
                                        bias=nrm[:, 1:2], scale=1.0)
                                assert T.n == a2b + NT

    def phase_qp(ffn_acc, attn_tm):
        OSL = C // 512
        with contextlib.ExitStack() as S:
            t = lambda n, sh, d: S.enter_context(nc.sbuf_tensor(n, sh, d))
            xqpT = t("xqpT", [P, CC * NOWN], BF16)
            sq = t("qpsq", [P, C], F32)
            sb_ = t("qpst", [P, 12], F32)
            scr = {"v": t("qpv", [P, C], F32), "i": t("qpi", [P, C], I32)}
            xqb = t("qpxq", [P, C], BF16)
            S2 = contextlib.ExitStack()
            ptq = S2.enter_context(nc.psum_tensor("qppt", [P, 8 * P], BF16))
            steps = []
            for i in range(NT):
                a_t = attn_tm[:, i * C:(i + 1) * C]

                def mk0(T):
                    T.memset(sb_[:, 0:1], 0.0)
                    T.memset(sb_[:, 1:2], 1.0)
                steps.append(("vector", mk0))
                steps += qscale_steps(a_t, C, sb_, sq, 0, 1, 4, 8, 9)

                def slabs(T, i=i):
                    T.tensor_copy(sb_[:, 2:3], sb_[:, 4:5])
                    T.memset(sb_[:, 3:4], 0.0)
                    T.tensor_tensor(dqp[:, i:i + 1], sb_[:, 8:9], wsc[:, 6:7],
                                    ALU.mult)
                steps.append(("vector", slabs))
                steps += q_steps(a_t, C, sb_[:, 2:3], sb_[:, 3:4], xqb[:, :],
                                 scr)
                for g0 in range(0, CC, 8):
                    nb = min(8, CC - g0)

                    def tr(T, g0=g0, nb=nb):
                        for kx in range(nb):
                            T.transpose(ptq[:, kx * P:(kx + 1) * P],
                                        xqb[:, (g0 + kx) * P:
                                            (g0 + kx + 1) * P],
                                        idb[:, :])
                    steps.append(("tensor", tr))

                    def evt(T, i=i, g0=g0, nb=nb):
                        T.tensor_copy(
                            bass.AP(xqpT, g0 * NOWN + i * P,
                                    [[CC * NOWN, P], [NOWN, nb], [1, P]]),
                            ptq[:, 0:nb * P])
                    steps.append(("vector", evt))
            B.chain("pqpq", steps)
            S2.close()

            wb = t("qpw", [P, 2 * 512], F32)
            wt = t("qpwt", [P, 2 * 512], BF16)
            wq = t("qpwq", [P, 2 * 512], BF16)
            asc = t("qpas", [P, 512], BF16)
            xrb = t("qpxr", [P, 4 * 512], F32)
            ob = t("qpo", [P, 4 * 512], F32)
            pp = S.enter_context(nc.psum_tensor("qpp", [P, 8 * 512], F32))
            ld = [B.gsem("ldA"), B.gsem("ldB")]
            tq = [B.gsem("tqA"), B.gsem("tqB")]
            xr = [B.gsem(f"xg{q}") for q in range(4)]
            so = [B.gsem(f"so{q}") for q in range(4)]
            mm = B.gsem("mm")
            vts, vnb = B.ticket("vector")
            vbase = vnb[0]
            NCH = OSL * CC
            vt_tern_plan = {}
            vt_evac_plan = {}
            cnt = 0
            for o in range(OSL):
                for cc in range(CC):
                    if cc == 2 and o >= 1:
                        cnt += 2 * NT
                        vt_evac_plan[o - 1] = cnt
                    cnt += 3
                    vt_tern_plan[o * CC + cc] = cnt
            cnt += 2 * NT
            vt_evac_plan[OSL - 1] = cnt
            vt_total = cnt

            with nc.Block() as block:
                @block.sync
                def _(s):
                    for k in range(min(2, NCH)):
                        o, cc = divmod(k, CC)
                        wload(s, wb, k % 2, w_proj, o * 512, cc * P,
                              ld[k % 2])
                    for k in range(NCH):
                        s.wait_ge(vts, vbase + vt_tern_plan[k])
                        if k >= 2:
                            s.wait_ge(mm.h, mm.val((k - 1) * NT))
                        wtrans(s, wq, (k % 2) * 512, wt, k % 2, tq[k % 2])
                        if k + 2 < NCH:
                            o2, c2 = divmod(k + 2, CC)
                            wload(s, wb, k % 2, w_proj, o2 * 512, c2 * P,
                                  ld[k % 2])

                @block.gpsimd
                def _(g):
                    for o in range(OSL):
                        for i in range(NT):
                            k2 = o * NT + i
                            if k2 >= 4:
                                op, oi = divmod(k2 - 4, NT)
                                g.wait_ge(vts, vbase + vt_evac_plan[op]
                                          - 2 * NT + 2 * (oi + 1))
                            g.dma_start(
                                out=xrb[:, (k2 % 4) * 512:(k2 % 4 + 1) * 512],
                                in_=x_all[NOWN + i * P:NOWN + (i + 1) * P,
                                          o * 512:(o + 1) * 512]
                                ).then_inc(xr[k2 % 4].h, 16)
                            xr[k2 % 4].inc(16)

                @block.vector
                def _(v):
                    T = B.TE(v, "vector")

                    def evacs(o):
                        v.wait_ge(mm.h, mm.val((o + 1) * CC * NT))
                        for i in range(NT):
                            k2 = o * NT + i
                            v.wait_ge(xr[k2 % 4].h,
                                      xr[k2 % 4].val(16 * (k2 // 4 + 1)))
                            if k2 >= 4:
                                v.wait_ge(so[k2 % 4].h,
                                          so[k2 % 4].val(16 * (k2 // 4)))
                            psl_ = pp[:, i * 512:(i + 1) * 512]
                            osl_ = ob[:, (k2 % 4) * 512:(k2 % 4 + 1) * 512]
                            T.scalar_tensor_tensor(
                                osl_, psl_, dqp[:, i:i + 1],
                                ffn_acc[:, i * C + o * 512:
                                        i * C + (o + 1) * 512],
                                ALU.mult, ALU.add)
                            T.tensor_tensor(
                                osl_, osl_,
                                xrb[:, (k2 % 4) * 512:(k2 % 4 + 1) * 512],
                                ALU.add)

                    for o in range(OSL):
                        for cc in range(CC):
                            k = o * CC + cc
                            if cc == 2 and o >= 1:
                                evacs(o - 1)
                                assert T.n == vbase + vt_evac_plan[o - 1]
                            v.wait_ge(ld[k % 2].h,
                                      ld[k % 2].val(64 * (k // 2 + 1)))
                            if k >= 2:
                                v.wait_ge(tq[k % 2].h,
                                          tq[k % 2].val(64 * (k // 2)))
                            tern3(T, wb[:, (k % 2) * 512:(k % 2 + 1) * 512],
                                  wt[:, (k % 2) * 512:(k % 2 + 1) * 512],
                                  asc[:, :], 7)
                            assert T.n == vbase + vt_tern_plan[k]
                    evacs(OSL - 1)
                    assert T.n == vbase + vt_total

                @block.tensor
                def _(te):
                    for o in range(OSL):
                        if o >= 1:
                            te.wait_ge(vts, vbase + vt_evac_plan[o - 1])
                        for cc in range(CC):
                            k = o * CC + cc
                            te.wait_ge(tq[k % 2].h,
                                       tq[k % 2].val(64 * (k // 2 + 1)))
                            for i in range(NT):
                                mi = te.matmul(
                                    pp[:, i * 512:(i + 1) * 512],
                                    xqpT[:, cc * NOWN + i * P:
                                         cc * NOWN + (i + 1) * P],
                                    wq[:, (k % 2) * 512:(k % 2 + 1) * 512],
                                    start=(cc == 0), stop=(cc == CC - 1))
                                if i == NT - 1:
                                    mi.then_inc(mm.h, NT)
                                    mm.inc(NT)

                @block.scalar
                def _(s):
                    for o in range(OSL):
                        for i in range(NT):
                            k2 = o * NT + i
                            s.wait_ge(vts, vbase + vt_evac_plan[o]
                                      - 2 * NT + 2 * (i + 1))
                            s.dma_start(out=out[i * P:(i + 1) * P,
                                                o * 512:(o + 1) * 512],
                                        in_=ob[:, (k2 % 4) * 512:
                                               (k2 % 4 + 1) * 512]
                                        ).then_inc(so[k2 % 4].h, 16)
                            so[k2 % 4].inc(16)
                    NK2 = OSL * NT
                    for sl in range(4):
                        s.wait_ge(so[sl].h,
                                  so[sl].val(16 * ((NK2 - 1 - sl) // 4 + 1)))

    LIM = globals().get("_PHASE_LIMIT", 99)
    MASK = globals().get("_PHASE_MASK", None)
    k_ = [0]

    def go(fn, *a):
        if (MASK is not None and k_[0] in MASK) or \
                (MASK is None and k_[0] < LIM):
            fn(*a)
        k_[0] += 1

    with nc.sbuf_tensor("xqf1T", [P, CC * NOWN], BF16) as xqf1T:
        go(weight_scale, "pw1", w1, F, C, 1e-8, 0)
        go(phase_a1, xqf1T)
        go(phase_f1, xqf1T)
    with (
        nc.sbuf_tensor("ffn_acc", [P, NT * C], BF16) as ffn_acc,
        nc.sbuf_tensor("attn_tm", [P, NT * C], BF16) as attn_tm,
    ):
        go(weight_scale, "pw2", w2, C, F, 1e-8, 2)
        go(phase_h)
        go(phase_f2, ffn_acc)
        go(weight_scale, "pwq", w_qkv, 3 * C, C, 1e-5, 4)
        with (
            nc.sbuf_tensor("bck", [P, TALL], F32) as bck,
            nc.sbuf_tensor("bcq", [P, NOWN], F32) as bcq,
        ):
            go(phase_a2, bck, bcq)
            go(phase_att, attn_tm, bck, bcq)
        if globals().get("_DEBUG_TAPS"):
            atm_d = nc.dram_tensor("atm_d", [P, NT * C], BF16,
                                   kind="ExternalOutput")
            fac_d = nc.dram_tensor("fac_d", [P, NT * C], BF16,
                                   kind="ExternalOutput")
            with nc.Block() as block:
                ts_ = B.gsem("tapx")

                @block.sync
                def _(s):
                    s.dma_start(out=atm_d[:, :], in_=attn_tm[:, :]
                                ).then_inc(ts_.h, 16)
                    ts_.inc(16)
                    s.dma_start(out=fac_d[:, :], in_=ffn_acc[:, :]
                                ).then_inc(ts_.h, 16)
                    ts_.inc(16)
                    s.wait_ge(ts_.h, ts_.val(32))
        go(weight_scale, "pwp", w_proj, C, C, 1e-5, 6)
        go(phase_qp, ffn_acc, attn_tm)

    if MASK is not None or LIM < 99:
        with nc.sbuf_tensor("fin", [P, C], F32) as fin, nc.Block() as block:
            fs = B.gsem("fin")

            @block.sync
            def _(s):
                for i in range(NT):
                    s.dma_start(out=fin[:, :],
                                in_=x_all[NOWN + i * P:NOWN + (i + 1) * P, :]
                                ).then_inc(fs.h, 16)
                    fs.inc(16)
                    s.wait_ge(fs.h, fs.val(32 * i + 16))
                    s.dma_start(out=out[i * P:(i + 1) * P, :], in_=fin[:, :]
                                ).then_inc(fs.h, 16)
                    fs.inc(16)
                    s.wait_ge(fs.h, fs.val(32 * i + 32))

    B.stack.close()
    return nc




def make_masks(cfg: Cfg, half: int) -> np.ndarray:
    m = np.zeros((P, cfg.SLAB_S), dtype=np.float32)
    for i in range(cfg.NT):
        kv = np.arange(cfg.KL[i] * P)[None, :]
        gkv = np.where(kv < cfg.NOWN, (1 - half) * cfg.NOWN + kv,
                       half * cfg.NOWN + (kv - cfg.NOWN))
        gq = half * cfg.NOWN + i * P + np.arange(P)[:, None]
        blk = np.where(gkv <= gq, 0.0, NEG).astype(np.float32)
        m[:, cfg.soff[i]:cfg.soff[i] + cfg.KL[i] * P] = blk
    return m.astype(ml_dtypes.bfloat16)


def stage_inputs(cfg: Cfg, x, wqkv, wproj, w1v, b1v, w2v, b2v):
    idb = np.eye(P).astype(ml_dtypes.bfloat16)
    idf = np.eye(P).astype(np.float32)
    in_maps = []
    nb = x.shape[0]
    for core in range(2 * nb):
        b, half = core // 2, core % 2
        partner = x[b, (1 - half) * cfg.NOWN:(2 - half) * cfg.NOWN]
        own = x[b, half * cfg.NOWN:(half + 1) * cfg.NOWN]
        x_all = np.ascontiguousarray(
            np.concatenate([partner, own], axis=0)).astype(np.float32)
        in_maps.append({
            "x_all": x_all,
            "w_qkv": wqkv, "w_proj": wproj,
            "w1": w1v, "b1": b1v.reshape(1, -1),
            "w2": w2v, "b2": b2v.reshape(1, -1),
            "masks": make_masks(cfg, half),
            "ident_b": idb, "ident_f": idf,
        })
    return in_maps


def kernel(L0_attn_x, L0_attn_w_qkv, L0_attn_w_proj, L0_ffn_w1, L0_ffn_b1,
           L0_ffn_w2, L0_ffn_b2):
    cfg = Cfg(C=2048, F=8192, H=16, NOWN=1024)
    x = np.ascontiguousarray(np.asarray(L0_attn_x, dtype=np.float32))
    nc = build(cfg)
    in_maps = stage_inputs(cfg, x,
                           np.asarray(L0_attn_w_qkv, dtype=np.float32),
                           np.asarray(L0_attn_w_proj, dtype=np.float32),
                           np.asarray(L0_ffn_w1, dtype=np.float32),
                           np.asarray(L0_ffn_b1, dtype=np.float32),
                           np.asarray(L0_ffn_w2, dtype=np.float32),
                           np.asarray(L0_ffn_b2, dtype=np.float32))
    res = run_bass_kernel_spmd(nc, in_maps, core_ids=list(range(len(in_maps))))
    B_, S, C = x.shape
    outv = np.empty((B_, S, C), dtype=np.float32)
    for core, r in enumerate(res.results):
        b, half = core // 2, core % 2
        outv[b, half * cfg.NOWN:(half + 1) * cfg.NOWN] = r["out"]
    return outv


# version 08 (final)
# speedup vs baseline: 1.0485x; 1.0168x over previous
import contextlib

import numpy as np
import ml_dtypes

import concourse.bass as bass
import concourse.mybir as mybir
from concourse.bass_utils import run_bass_kernel_spmd

F32 = mybir.dt.float32
BF16 = mybir.dt.bfloat16
I32 = mybir.dt.int32
AF = mybir.ActivationFunctionType
ALU = mybir.AluOpType
AX = mybir.AxisListType.X

NEG = -9.0e15
P = 128
LNEPS = 1.0e-5
ACTEPS = 1.0e-5

COMPUTE_OPS = {
    "tensor_scalar", "tensor_tensor", "scalar_tensor_tensor", "tensor_copy",
    "tensor_add", "tensor_sub", "tensor_mul", "tensor_max", "reduce_sum",
    "reduce_max", "reciprocal", "memset", "activation", "iota", "transpose",
    "matmul", "tensor_single_scalar",
}

ENG_COL = {"sync": 0, "vector": 1, "scalar": 2, "gpsimd": 3, "tensor": 4}


class TEng:

    def __init__(self, e, sem, nbox, fence_ap):
        object.__setattr__(self, "_e", e)
        object.__setattr__(self, "_sem", sem)
        object.__setattr__(self, "_n", nbox)
        object.__setattr__(self, "_fence", fence_ap)

    @property
    def n(self):
        return self._n[0]

    def __getattr__(self, nm):
        f = getattr(self._e, nm)
        if nm in COMPUTE_OPS:
            def w(*a, **k):
                if self._n[0]:
                    self._e.wait_ge(self._sem, self._n[0])
                inst = f(*a, **k)
                inst.then_inc(self._sem, 1)
                self._n[0] += 1
                return inst
            return w
        return f

    def fence(self):
        if self._n[0]:
            self._e.wait_ge(self._sem, self._n[0])
        self._e.memset(self._fence, 0.0).then_inc(self._sem, 1)
        self._n[0] += 1


class Cfg:
    def __init__(self, C=2048, F=8192, H=16, NOWN=1024):
        self.C, self.F, self.H, self.NOWN = C, F, H, NOWN
        self.D = P
        assert C == H * self.D and NOWN % P == 0 and F % 512 == 0 and C % 512 == 0
        self.CC = C // P
        self.FC = F // P
        self.NT = NOWN // P
        self.TALL = 2 * NOWN
        self.MT = self.TALL // P
        self.QOFF = self.MT - self.NT
        self.HG = min(4, H)
        self.NG = H // self.HG
        self.SCALE = float(self.D) ** -0.5
        self.KL = [self.QOFF + i + 1 for i in range(self.NT)]
        self.soff = [0] * self.NT
        for i in range(1, self.NT):
            self.soff[i] = self.soff[i - 1] + self.KL[i - 1] * P
        self.SLAB_S = self.soff[-1] + self.KL[-1] * P


class GSem:
    def __init__(self, h):
        self.h = h
        self.n = 0


class SemView:

    def __init__(self, g):
        self.g = g
        self.base = g.n

    @property
    def h(self):
        return self.g.h

    def inc(self, k=16):
        self.g.n += k

    def val(self, rel):
        return self.base + rel


class Builder:
    def __init__(self, cfg: Cfg):
        self.cfg = cfg
        self.nc = bass.Bass("TRN2", target_bir_lowering=False)
        self.stack = contextlib.ExitStack()
        self._semi = 0
        self.fnc = None
        self._gsems = {}
        self._tickets = {}

    def sem(self, tag=""):
        self._semi += 1
        return self.stack.enter_context(self.nc.semaphore(f"s{self._semi}{tag}"))

    def gsem(self, key):
        if key not in self._gsems:
            self._gsems[key] = GSem(self.sem(key))
        return SemView(self._gsems[key])

    def ticket(self, eng):
        if eng not in self._tickets:
            self._tickets[eng] = (self.sem("tk" + eng[0]), [0])
        return self._tickets[eng]

    def TE(self, e, eng):
        sem, nbox = self.ticket(eng)
        return TEng(e, sem, nbox,
                    self.fnc[0:1, ENG_COL[eng]:ENG_COL[eng] + 1])

    def sbuf(self, name, shape, dt):
        return self.stack.enter_context(self.nc.sbuf_tensor(name, shape, dt))

    def chain(self, name, steps):
        prev = [None]
        with self.nc.Block() as block:
            for eng, fn in steps:
                sem, nbox = self.ticket(eng)
                n_before = nbox[0]

                def sect(e, fn=fn, sem=sem, nbox=nbox, pv=prev[0], eng=eng):
                    T = TEng(e, sem, nbox,
                             self.fnc[0:1, ENG_COL[eng]:ENG_COL[eng] + 1])
                    if pv is not None and pv[0] is not sem:
                        e.wait_ge(pv[0], pv[1])
                    fn(T)
                getattr(block, eng)(sect)
                if nbox[0] > n_before:
                    prev[0] = (sem, nbox[0])


def build(cfg: Cfg):
    B = Builder(cfg)
    nc = B.nc
    C, F, H, NOWN = cfg.C, cfg.F, cfg.H, cfg.NOWN
    CC, FC, NT, MT, TALL = cfg.CC, cfg.FC, cfg.NT, cfg.MT, cfg.TALL
    HG, NG, QOFF = cfg.HG, cfg.NG, cfg.QOFF

    x_all = nc.dram_tensor("x_all", [TALL, C], F32, kind="ExternalInput")
    w_qkv = nc.dram_tensor("w_qkv", [3 * C, C], F32, kind="ExternalInput")
    w_proj = nc.dram_tensor("w_proj", [C, C], F32, kind="ExternalInput")
    w1 = nc.dram_tensor("w1", [F, C], F32, kind="ExternalInput")
    b1 = nc.dram_tensor("b1", [1, F], F32, kind="ExternalInput")
    w2 = nc.dram_tensor("w2", [C, F], F32, kind="ExternalInput")
    b2 = nc.dram_tensor("b2", [1, C], F32, kind="ExternalInput")
    masks = nc.dram_tensor("masks", [P, cfg.SLAB_S], BF16, kind="ExternalInput")
    ident_b = nc.dram_tensor("ident_b", [P, P], BF16, kind="ExternalInput")
    ident_f = nc.dram_tensor("ident_f", [P, P], F32, kind="ExternalInput")
    out = nc.dram_tensor("out", [NOWN, C], F32, kind="ExternalOutput")
    _tap = "ExternalOutput" if globals().get("_DEBUG_TAPS") else "Internal"
    xqa_d = nc.dram_tensor("xqa_d", [TALL, C], BF16, kind=_tap)
    xqh_d = nc.dram_tensor("xqh_d", [NOWN, F], BF16, kind=_tap)
    h_d = nc.dram_tensor("h_d", [NOWN, F], BF16, kind=_tap)

    B.fnc = B.sbuf("fnc", [1, 8], F32)
    idb = B.sbuf("idb", [P, P], BF16)
    idf = B.sbuf("idf", [P, P], F32)
    onesrow = B.sbuf("onesrow", [1, P], F32)
    wsc = B.sbuf("wsc", [P, 8], F32)
    aff_a = B.sbuf("aff_a", [P, NT], F32)
    aff_b = B.sbuf("aff_b", [P, NT], F32)
    dq1 = B.sbuf("dq1", [P, NT], F32)
    isc1 = B.sbuf("isc1", [P, NT], F32)
    sr1 = B.sbuf("sr1", [1, NT * P], F32)
    dq2 = B.sbuf("dq2", [P, NT], F32)
    isc2 = B.sbuf("isc2", [P, NT], F32)
    sr2 = B.sbuf("sr2", [1, NT * P], F32)
    dqv = B.sbuf("dqv", [P, MT], F32)
    dqp = B.sbuf("dqp", [P, NT], F32)
    OSL1 = F // 512
    hsum = B.sbuf("hsum", [P, NT * OSL1], F32)
    hsq = B.sbuf("hsq", [P, NT * OSL1], F32)
    hmx = B.sbuf("hmx", [P, NT * OSL1], F32)
    hmn = B.sbuf("hmn", [P, NT * OSL1], F32)

    with nc.Block() as block:
        ld0 = B.sem("p0")

        @block.sync
        def _(s):
            s.dma_start(out=idb[:, :], in_=ident_b[:, :]).then_inc(ld0, 16)
            s.dma_start(out=idf[:, :], in_=ident_f[:, :]).then_inc(ld0, 16)
            s.wait_ge(ld0, 32)

        @block.vector
        def _(v):
            v.memset(onesrow[:, :], 1.0)

    def weight_scale(tag, w_dram, rows, cols, eps, scol):
        RT = rows // P
        ld = [B.gsem("gldA"), B.gsem("gldB")]
        vts, vnb = B.ticket("vector")
        vbase = vnb[0]
        with (
            nc.sbuf_tensor(tag + "wb", [P, 2 * cols], F32) as wb,
            nc.sbuf_tensor(tag + "ab", [P, cols], F32) as ab,
            nc.sbuf_tensor(tag + "cs", [P, RT], F32) as cs,
        ):
            with nc.Block() as block:
                @block.gpsimd
                def _(g):
                    for t in range(RT):
                        if t >= 2:
                            g.wait_ge(vts, vbase + 2 * (t - 1))
                        g.dma_start(out=wb[:, (t % 2) * cols:(t % 2 + 1) * cols],
                                    in_=w_dram[t * P:(t + 1) * P, :]
                                    ).then_inc(ld[t % 2].h, 16)
                        ld[t % 2].inc(16)

                @block.vector
                def _(v):
                    T = B.TE(v, "vector")
                    for t in range(RT):
                        v.wait_ge(ld[t % 2].h, ld[t % 2].val(16 * (t // 2 + 1)))
                        src = wb[:, (t % 2) * cols:(t % 2 + 1) * cols]
                        T.scalar_tensor_tensor(ab[:, :], src, -1.0, src,
                                               ALU.mult, ALU.max)
                        T.tensor_scalar(ab[:, :], ab[:, :], 1.0, None,
                                        ALU.mult, ALU.add,
                                        accum_out=cs[:, t:t + 1])
                    assert T.n == vbase + 2 * RT

            with (
                nc.sbuf_tensor(tag + "sc", [P, 1], F32) as sc,
                nc.sbuf_tensor(tag + "r1", [1, 2], F32) as r1,
                nc.psum_tensor(tag + "pt", [1, P], F32) as pt,
                nc.psum_tensor(tag + "pb", [P, 2], F32) as pb,
            ):
                def s1(T):
                    T.reduce_sum(sc[:, 0:1], cs[:, :], axis=AX)

                def s2(T):
                    T.transpose(pt[0:1, :], sc[:, 0:1], idf[:, :])

                def s3(T):
                    T.reduce_sum(r1[0:1, 0:1], pt[0:1, :], axis=AX)
                    T.tensor_scalar(r1[0:1, 0:1], r1[0:1, 0:1],
                                    float(1.0 / (rows * cols)), float(eps),
                                    ALU.mult, ALU.max)
                    T.reciprocal(r1[0:1, 1:2], r1[0:1, 0:1])

                def s4(T):
                    T.matmul(pb[:, 0:2], onesrow[0:1, :], r1[0:1, 0:2],
                             start=True, stop=True)

                def s5(T):
                    T.tensor_copy(wsc[:, scol:scol + 2], pb[:, 0:2])

                B.chain(tag + "t", [("vector", s1), ("tensor", s2),
                                    ("vector", s3), ("tensor", s4),
                                    ("vector", s5)])

    def q_steps(src_ap, w, alpha, beta, xq_ap, scr):
        return [
            ("vector", lambda T: T.tensor_scalar(
                scr["v"][:, 0:w], src_ap, alpha, beta, ALU.mult, ALU.add)),
            ("vector", lambda T: T.tensor_copy(
                scr["i"][:, 0:w], scr["v"][:, 0:w])),
            ("vector", lambda T: T.tensor_copy(
                xq_ap, scr["i"][:, 0:w])),
        ]

    def ln_steps(src_ap, w, sb_, sq, cm, cr, t0, t1):
        def fin(T):
            T.tensor_scalar(sb_[:, cm:cm + 1], sb_[:, t0:t0 + 1], 1.0 / w,
                            None, ALU.mult)
            T.tensor_scalar(sb_[:, t1:t1 + 1], sb_[:, t1:t1 + 1], 1.0 / w,
                            None, ALU.mult)
            T.tensor_tensor(sb_[:, t0:t0 + 1], sb_[:, cm:cm + 1],
                            sb_[:, cm:cm + 1], ALU.mult)
            T.tensor_tensor(sb_[:, t1:t1 + 1], sb_[:, t1:t1 + 1],
                            sb_[:, t0:t0 + 1], ALU.subtract)
            T.tensor_scalar(sb_[:, t1:t1 + 1], sb_[:, t1:t1 + 1], 1.0, LNEPS,
                            ALU.mult, ALU.add)
        return [
            ("vector", lambda T: T.tensor_scalar(
                sq[:, 0:w], src_ap, 1.0, None, ALU.mult, ALU.add,
                accum_out=sb_[:, t0:t0 + 1])),
            ("scalar", lambda T: T.activation(
                sq[:, 0:w], src_ap, AF.Square, bias=0.0, scale=1.0,
                accum_out=sb_[:, t1:t1 + 1])),
            ("vector", fin),
            ("scalar", lambda T: T.activation(
                sb_[:, t0:t0 + 1], sb_[:, t1:t1 + 1], AF.Sqrt,
                bias=0.0, scale=1.0)),
            ("vector", lambda T: T.reciprocal(sb_[:, cr:cr + 1],
                                              sb_[:, t0:t0 + 1])),
        ]

    def qscale_steps(src_ap, w, sb_, sq, cm, cr, cs_, t0, t1):
        def fin(T):
            T.tensor_tensor(sb_[:, t0:t0 + 1], sb_[:, t0:t0 + 1],
                            sb_[:, cm:cm + 1], ALU.subtract)
            T.tensor_tensor(sb_[:, t1:t1 + 1], sb_[:, t1:t1 + 1],
                            sb_[:, cm:cm + 1], ALU.add)
            T.tensor_tensor(sb_[:, t0:t0 + 1], sb_[:, t0:t0 + 1],
                            sb_[:, t1:t1 + 1], ALU.max)
            T.tensor_tensor(sb_[:, t0:t0 + 1], sb_[:, t0:t0 + 1],
                            sb_[:, cr:cr + 1], ALU.mult)
            T.tensor_scalar(sb_[:, t0:t0 + 1], sb_[:, t0:t0 + 1], 1.0,
                            ACTEPS, ALU.mult, ALU.max)
            T.reciprocal(sb_[:, t1:t1 + 1], sb_[:, t0:t0 + 1])
            T.tensor_scalar(sb_[:, cs_:cs_ + 1], sb_[:, t1:t1 + 1], 127.0,
                            None, ALU.mult)
            T.tensor_scalar(sb_[:, t0:t0 + 1], sb_[:, t0:t0 + 1],
                            1.0 / 127.0, None, ALU.mult)
        return [
            ("vector", lambda T: T.tensor_scalar(
                sq[:, 0:w], src_ap, 1.0, None, ALU.mult, ALU.max,
                accum_out=sb_[:, t0:t0 + 1])),
            ("vector", lambda T: T.tensor_scalar(
                sq[:, 0:w], src_ap, -1.0, None, ALU.mult, ALU.max,
                accum_out=sb_[:, t1:t1 + 1])),
            ("vector", fin),
        ]

    def wload(eng, wr, slot, W, r0, c0, sem):
        for q in range(4):
            eng.dma_start(
                out=wr[:, slot * 512 + q * P:slot * 512 + (q + 1) * P],
                in_=W[r0 + q * P:r0 + (q + 1) * P, c0:c0 + P]
            ).then_inc(sem.h, 16)
            sem.inc(16)

    def wtrans(eng, wq_, dcol, wt, slot, sem):
        for q in range(4):
            eng.dma_start(
                out=wq_[:, dcol + q * P:dcol + (q + 1) * P],
                in_=wt[:, slot * 512 + q * P:slot * 512 + (q + 1) * P],
                transpose=True).then_inc(sem.h, 16)
            sem.inc(16)

    def tern3(T, src, dst, scr_a, rcol):
        T.tensor_scalar(scr_a, src, wsc[:, rcol:rcol + 1], 0.5,
                        ALU.mult, ALU.is_gt)
        T.tensor_scalar(dst, src, wsc[:, rcol:rcol + 1], -0.5,
                        ALU.mult, ALU.is_lt)
        T.tensor_tensor(dst, scr_a, dst, ALU.subtract)

    def phase_a1(xqf1T):
        with contextlib.ExitStack() as S:
            t = lambda n, sh, d: S.enter_context(nc.sbuf_tensor(n, sh, d))
            xb = t("a1x", [P, 2 * C], F32)
            xnb = t("a1xn", [P, C], F32)
            sq = t("a1sq", [P, C], F32)
            sb_ = t("a1st", [P, 12], F32)
            scr = {"v": t("a1v", [P, C], F32), "i": t("a1i", [P, C], I32)}
            xqb = t("a1xq", [P, C], BF16)
            pt = S.enter_context(nc.psum_tensor("a1pt", [P, 8 * P], BF16))
            ps = S.enter_context(nc.psum_tensor("a1ps", [1, 2 * P], F32))
            dmas = [B.gsem("dmA"), B.gsem("dmB")]
            steps = []

            def mk_dma(i):
                def f(T):
                    T._e.dma_start(
                        out=xb[:, (i % 2) * C:(i % 2 + 1) * C],
                        in_=x_all[NOWN + i * P:NOWN + (i + 1) * P, :]
                    ).then_inc(dmas[i % 2].h, 16)
                    dmas[i % 2].inc(16)
                return f

            steps.append(("sync", mk_dma(0)))
            for i in range(NT):
                if i + 1 < NT:
                    steps.append(("sync", mk_dma(i + 1)))
                steps.append(("vector",
                              lambda T, s_=dmas[i % 2], n=16 * (i // 2 + 1):
                              T._e.wait_ge(s_.h, s_.val(n))))
                x_t = xb[:, (i % 2) * C:(i % 2 + 1) * C]
                steps += ln_steps(x_t, C, sb_, sq, 0, 1, 8, 9)

                def mkxn(T, x_t=x_t):
                    T.tensor_tensor(sb_[:, 8:9], sb_[:, 0:1], sb_[:, 1:2],
                                    ALU.mult)
                    T.tensor_scalar(sb_[:, 8:9], sb_[:, 8:9], -1.0, None,
                                    ALU.mult)
                    T.tensor_scalar(xnb[:, :], x_t, sb_[:, 1:2], sb_[:, 8:9],
                                    ALU.mult, ALU.add)
                steps.append(("vector", mkxn))
                steps += ln_steps(xnb[:, :], C, sb_, sq, 2, 3, 8, 9)
                steps += qscale_steps(xnb[:, :], C, sb_, sq, 2, 3, 4, 8, 9)

                def slabs(T, i=i):
                    T.tensor_tensor(aff_a[:, i:i + 1], sb_[:, 3:4], sb_[:, 4:5],
                                    ALU.mult)
                    T.tensor_tensor(sb_[:, 9:10], sb_[:, 2:3],
                                    aff_a[:, i:i + 1], ALU.mult)
                    T.tensor_scalar(aff_b[:, i:i + 1], sb_[:, 9:10], -1.0,
                                    None, ALU.mult)
                    T.tensor_tensor(dq1[:, i:i + 1], sb_[:, 8:9], wsc[:, 0:1],
                                    ALU.mult)
                    T.tensor_tensor(isc1[:, i:i + 1], sb_[:, 4:5], wsc[:, 1:2],
                                    ALU.mult)
                steps.append(("vector", slabs))
                steps += q_steps(xnb[:, :], C, aff_a[:, i:i + 1],
                                 aff_b[:, i:i + 1], xqb[:, :], scr)
                for g0 in range(0, CC, 8):
                    nb = min(8, CC - g0)

                    def tr(T, g0=g0, nb=nb):
                        for k in range(nb):
                            T.transpose(pt[:, k * P:(k + 1) * P],
                                        xqb[:, (g0 + k) * P:(g0 + k + 1) * P],
                                        idb[:, :])
                    steps.append(("tensor", tr))

                    def ev(T, i=i, g0=g0, nb=nb):
                        T.tensor_copy(
                            bass.AP(xqf1T, g0 * NOWN + i * P,
                                    [[CC * NOWN, P], [NOWN, nb], [1, P]]),
                            pt[:, 0:nb * P])
                    steps.append(("vector", ev))
            for i in range(NT):
                steps.append(("tensor", lambda T, i=i: T.transpose(
                    ps[0:1, (i % 2) * P:(i % 2 + 1) * P],
                    isc1[:, i:i + 1], idf[:, :])))
                steps.append(("vector", lambda T, i=i: T.tensor_copy(
                    sr1[0:1, i * P:(i + 1) * P],
                    ps[0:1, (i % 2) * P:(i % 2 + 1) * P])))
            B.chain("pa1", steps)

    def phase_f1(xqf1T):
        OSL = F // 512
        NK = OSL * NT
        vt_tern_plan = {}
        vt_stat_plan = {}
        cnt = 0
        for o in range(OSL + 2):
            if o >= 2:
                for i in range(NT):
                    cnt += 3
                    vt_stat_plan[(o - 2) * NT + i] = cnt
            if o < OSL:
                for cc in range(CC):
                    cnt += 3
                    vt_tern_plan[o * CC + cc] = cnt
        vt_total = cnt
        with contextlib.ExitStack() as S:
            t = lambda n, sh, d: S.enter_context(nc.sbuf_tensor(n, sh, d))
            wb = t("f1wb", [P, 4 * 512], F32)
            wt = t("f1wt", [P, 4 * 512], BF16)
            asc = t("f1as", [P, 512], BF16)
            wq = t("f1wq", [P, 2 * CC * 512], BF16)
            hb = t("f1h", [P, 4 * 512], BF16)
            sqs = t("f1sq", [P, 4 * 512], F32)
            b1s = t("b1s", [1, F], F32)
            pp = S.enter_context(nc.psum_tensor("f1p", [P, 2 * 512], F32))
            ld = [B.gsem(f"ld{q}") for q in range(4)]
            tq = [B.gsem(f"tq{q}") for q in range(4)]
            so = [B.gsem(f"gso{q}") for q in range(4)]
            mm = B.gsem("mm")
            vts, vnb = B.ticket("vector")
            ats, anb = B.ticket("scalar")
            vbase, abase = vnb[0], anb[0]
            b1l = B.gsem("bld")
            NK3 = OSL * CC
            with nc.Block() as block:
                @block.sync
                def _(s):
                    s.dma_start(out=b1s[:, :], in_=b1[:, :]).then_inc(b1l.h, 16)
                    b1l.inc(16)
                    for k3 in range(min(4, NK3)):
                        o, cc = divmod(k3, CC)
                        wload(s, wb, k3 % 4, w1, o * 512, cc * P, ld[k3 % 4])
                    for k3 in range(NK3):
                        o, cc = divmod(k3, CC)
                        s.wait_ge(vts, vbase + vt_tern_plan[k3])
                        if cc == 0 and o >= 2:
                            s.wait_ge(mm.h, mm.val((o - 1) * NT))
                        wtrans(s, wq, ((o % 2) * CC + cc) * 512, wt,
                               k3 % 4, tq[k3 % 4])
                        if k3 + 4 < NK3:
                            o2, c2 = divmod(k3 + 4, CC)
                            wload(s, wb, k3 % 4, w1, o2 * 512, c2 * P,
                                  ld[k3 % 4])

                @block.vector
                def _(v):
                    T = B.TE(v, "vector")
                    for o in range(OSL + 2):
                        if o >= 2:
                            for i in range(NT):
                                k = (o - 2) * NT + i
                                v.wait_ge(ats, abase + 2 * k + 2)
                                hsl = hb[:, (k % 4) * 512:(k % 4 + 1) * 512]
                                slot = i * OSL + (o - 2)
                                T.tensor_scalar(sqs[:, 0:512], hsl, 1.0, None,
                                                ALU.mult, ALU.add,
                                                accum_out=hsum[:, slot:slot + 1])
                                T.tensor_scalar(sqs[:, 512:1024], hsl, 1.0,
                                                None, ALU.mult, ALU.max,
                                                accum_out=hmx[:, slot:slot + 1])
                                T.tensor_scalar(sqs[:, 1024:1536], hsl, -1.0,
                                                None, ALU.mult, ALU.max,
                                                accum_out=hmn[:, slot:slot + 1])
                                assert T.n == vbase + vt_stat_plan[k]
                        if o < OSL:
                            for cc in range(CC):
                                k3 = o * CC + cc
                                v.wait_ge(ld[k3 % 4].h,
                                          ld[k3 % 4].val(64 * (k3 // 4 + 1)))
                                if k3 >= 4:
                                    v.wait_ge(tq[k3 % 4].h,
                                              tq[k3 % 4].val(64 * (k3 // 4)))
                                tern3(T,
                                      wb[:, (k3 % 4) * 512:(k3 % 4 + 1) * 512],
                                      wt[:, (k3 % 4) * 512:(k3 % 4 + 1) * 512],
                                      asc[:, :], 1)
                                assert T.n == vbase + vt_tern_plan[k3]
                    assert T.n == vbase + vt_total

                @block.tensor
                def _(te):
                    te.wait_ge(b1l.h, b1l.val(16))
                    assert CC % 4 == 0
                    for o in range(OSL):
                        for sl in range(4):
                            klast = o * CC + CC - 4 + sl
                            te.wait_ge(tq[sl].h,
                                       tq[sl].val(64 * (klast // 4 + 1)))
                        for i in range(NT):
                            k = o * NT + i
                            if k >= 2:
                                te.wait_ge(ats, abase + 2 * (k - 2) + 1)
                            psl = pp[:, (k % 2) * 512:(k % 2 + 1) * 512]
                            for cc in range(CC):
                                te.matmul(
                                    psl,
                                    xqf1T[:, cc * NOWN + i * P:
                                          cc * NOWN + (i + 1) * P],
                                    wq[:, ((o % 2) * CC + cc) * 512:
                                       ((o % 2) * CC + cc + 1) * 512],
                                    start=(cc == 0), stop=False)
                            te.matmul(psl, sr1[0:1, i * P:(i + 1) * P],
                                      b1s[0:1, o * 512:(o + 1) * 512],
                                      start=False, stop=True).then_inc(mm.h, 1)
                            mm.inc(1)

                @block.scalar
                def _(s):
                    T = B.TE(s, "scalar")
                    for k in range(NK):
                        o, i = divmod(k, NT)
                        s.wait_ge(mm.h, mm.val(k + 1))
                        if k >= 4:
                            s.wait_ge(so[k % 4].h, so[k % 4].val(16 * (k // 4)))
                            s.wait_ge(vts, vbase + vt_stat_plan[k - 4])
                        psl = pp[:, (k % 2) * 512:(k % 2 + 1) * 512]
                        hsl = hb[:, (k % 4) * 512:(k % 4 + 1) * 512]
                        T.activation(hsl, psl, AF.Gelu, bias=0.0,
                                     scale=dq1[:, i:i + 1])
                        T.activation(sqs[:, 1536:2048], hsl, AF.Square,
                                     bias=0.0, scale=1.0,
                                     accum_out=hsq[:, i * OSL + o:
                                                   i * OSL + o + 1])
                    assert T.n == abase + 2 * NK

                @block.gpsimd
                def _(g):
                    for k in range(NK):
                        o, i = divmod(k, NT)
                        g.wait_ge(vts, vbase + vt_stat_plan[k])
                        g.dma_start(out=h_d[i * P:(i + 1) * P,
                                            o * 512:(o + 1) * 512],
                                    in_=hb[:, (k % 4) * 512:(k % 4 + 1) * 512]
                                    ).then_inc(so[k % 4].h, 16)
                        so[k % 4].inc(16)
                    for sl in range(4):
                        g.wait_ge(so[sl].h,
                                  so[sl].val(16 * ((NK - 1 - sl) // 4 + 1)))

    def phase_h():
        OSL = F // 512
        with contextlib.ExitStack() as S:
            t = lambda n, sh, d: S.enter_context(nc.sbuf_tensor(n, sh, d))
            hb = t("phh", [P, 2 * F], BF16)
            sb_ = t("phst", [P, 12], F32)
            scr = {"v": t("phv", [P, F], F32), "i": t("phi", [P, F], I32)}
            ps = S.enter_context(nc.psum_tensor("phps", [1, 2 * P], F32))
            dmas = [B.gsem("dmA"), B.gsem("dmB")]
            sto = [B.gsem("stA"), B.gsem("stB")]
            steps = []

            def mk_dma(i):
                def f(T):
                    if i >= 2:
                        T._e.wait_ge(sto[i % 2].h,
                                     sto[i % 2].val(16 * ((i - 2) // 2 + 1)))
                    T._e.dma_start(out=hb[:, (i % 2) * F:(i % 2 + 1) * F],
                                   in_=h_d[i * P:(i + 1) * P, :]
                                   ).then_inc(dmas[i % 2].h, 16)
                    dmas[i % 2].inc(16)
                return f

            steps.append(("sync", mk_dma(0)))
            for i in range(NT):
                if i + 1 < NT:
                    steps.append(("sync", mk_dma(i + 1)))
                steps.append(("vector",
                              lambda T, s_=dmas[i % 2], n=16 * (i // 2 + 1):
                              T._e.wait_ge(s_.h, s_.val(n))))
                h_t = hb[:, (i % 2) * F:(i % 2 + 1) * F]

                def agg(T, i=i):
                    T.reduce_sum(sb_[:, 8:9], hsum[:, i * OSL:(i + 1) * OSL],
                                 axis=AX)
                    T.reduce_sum(sb_[:, 9:10], hsq[:, i * OSL:(i + 1) * OSL],
                                 axis=AX)
                    T.reduce_max(sb_[:, 10:11], hmx[:, i * OSL:(i + 1) * OSL],
                                 axis=AX)
                    T.reduce_max(sb_[:, 11:12], hmn[:, i * OSL:(i + 1) * OSL],
                                 axis=AX)
                    T.tensor_scalar(sb_[:, 0:1], sb_[:, 8:9], 1.0 / F, None,
                                    ALU.mult)
                    T.tensor_scalar(sb_[:, 9:10], sb_[:, 9:10], 1.0 / F, None,
                                    ALU.mult)
                    T.tensor_tensor(sb_[:, 8:9], sb_[:, 0:1], sb_[:, 0:1],
                                    ALU.mult)
                    T.tensor_tensor(sb_[:, 9:10], sb_[:, 9:10], sb_[:, 8:9],
                                    ALU.subtract)
                    T.tensor_scalar(sb_[:, 9:10], sb_[:, 9:10], 1.0, LNEPS,
                                    ALU.mult, ALU.add)
                steps.append(("vector", agg))
                steps.append(("scalar", lambda T: T.activation(
                    sb_[:, 8:9], sb_[:, 9:10], AF.Sqrt, bias=0.0, scale=1.0)))

                def fin(T, i=i):
                    T.reciprocal(sb_[:, 1:2], sb_[:, 8:9])
                    T.tensor_tensor(sb_[:, 10:11], sb_[:, 10:11], sb_[:, 0:1],
                                    ALU.subtract)
                    T.tensor_tensor(sb_[:, 11:12], sb_[:, 11:12], sb_[:, 0:1],
                                    ALU.add)
                    T.tensor_tensor(sb_[:, 10:11], sb_[:, 10:11],
                                    sb_[:, 11:12], ALU.max)
                    T.tensor_tensor(sb_[:, 10:11], sb_[:, 10:11], sb_[:, 1:2],
                                    ALU.mult)
                    T.tensor_scalar(sb_[:, 10:11], sb_[:, 10:11], 1.0, ACTEPS,
                                    ALU.mult, ALU.max)
                    T.reciprocal(sb_[:, 11:12], sb_[:, 10:11])
                    T.tensor_scalar(sb_[:, 4:5], sb_[:, 11:12], 127.0, None,
                                    ALU.mult)
                    T.tensor_scalar(sb_[:, 10:11], sb_[:, 10:11], 1.0 / 127.0,
                                    None, ALU.mult)
                    T.tensor_tensor(sb_[:, 2:3], sb_[:, 1:2], sb_[:, 4:5],
                                    ALU.mult)
                    T.tensor_tensor(sb_[:, 3:4], sb_[:, 0:1], sb_[:, 2:3],
                                    ALU.mult)
                    T.tensor_scalar(sb_[:, 3:4], sb_[:, 3:4], -1.0, None,
                                    ALU.mult)
                    T.tensor_tensor(dq2[:, i:i + 1], sb_[:, 10:11],
                                    wsc[:, 2:3], ALU.mult)
                    T.tensor_tensor(isc2[:, i:i + 1], sb_[:, 4:5], wsc[:, 3:4],
                                    ALU.mult)
                steps.append(("vector", fin))
                steps += q_steps(h_t, F, sb_[:, 2:3], sb_[:, 3:4], h_t, scr)

                def sto_f(T, i=i):
                    T._e.dma_start(out=xqh_d[i * P:(i + 1) * P, :],
                                   in_=hb[:, (i % 2) * F:(i % 2 + 1) * F]
                                   ).then_inc(sto[i % 2].h, 16)
                    sto[i % 2].inc(16)
                steps.append(("scalar", sto_f))
            def wait_all_sto(T):
                for sl in range(2):
                    T._e.wait_ge(sto[sl].h,
                                 sto[sl].val(16 * ((NT - 1 - sl) // 2 + 1)))
            steps.append(("scalar", wait_all_sto))
            for i in range(NT):
                steps.append(("tensor", lambda T, i=i: T.transpose(
                    ps[0:1, (i % 2) * P:(i % 2 + 1) * P],
                    isc2[:, i:i + 1], idf[:, :])))
                steps.append(("vector", lambda T, i=i: T.tensor_copy(
                    sr2[0:1, i * P:(i + 1) * P],
                    ps[0:1, (i % 2) * P:(i % 2 + 1) * P])))
            B.chain("ph", steps)

    def phase_f2(ffn_acc):
        OSL = C // 512
        NCH = OSL * FC

        def mmv(k):
            o, fc = divmod(k, FC)
            return (o * (FC + 1) + fc + 1) * NT

        vt_tern_plan = {}
        vt_evac_plan = {}
        cnt = 0
        for o in range(OSL):
            for fc in range(FC):
                if fc == 2 and o >= 1:
                    cnt += NT
                    vt_evac_plan[o - 1] = cnt
                cnt += 3
                vt_tern_plan[o * FC + fc] = cnt
        cnt += NT
        vt_evac_plan[OSL - 1] = cnt
        vt_total = cnt

        with contextlib.ExitStack() as S:
            t = lambda n, sh, d: S.enter_context(nc.sbuf_tensor(n, sh, d))
            wb = t("f2wb", [P, 4 * 512], F32)
            wt = t("f2wt", [P, 4 * 512], BF16)
            wq = t("f2wq", [P, 4 * 512], BF16)
            asc = t("f2as", [P, 512], BF16)
            xb = t("f2xb", [P, 4 * NOWN], BF16)
            b2s = t("b2s", [1, C], F32)
            pp = S.enter_context(nc.psum_tensor("f2p", [P, 8 * 512], F32))
            ld = [B.gsem(f"ld{q}") for q in range(4)]
            tq = [B.gsem(f"tq{q}") for q in range(4)]
            xl = [B.gsem(f"xl{q}") for q in range(4)]
            mm = B.gsem("mm")
            vts, vnb = B.ticket("vector")
            vbase = vnb[0]
            b2l = B.gsem("bld")
            with nc.Block() as block:
                @block.sync
                def _(s):
                    s.dma_start(out=b2s[:, :], in_=b2[:, :]).then_inc(b2l.h, 16)
                    b2l.inc(16)
                    for k in range(min(4, NCH)):
                        o, fc = divmod(k, FC)
                        wload(s, wb, k % 4, w2, o * 512, fc * P, ld[k % 4])
                    for k in range(NCH):
                        o, fc = divmod(k, FC)
                        if k >= 4:
                            s.wait_ge(mm.h, mm.val(mmv(k - 4)))
                        s.dma_start(
                            out=xb[:, (k % 4) * NOWN:(k % 4 + 1) * NOWN],
                            in_=xqh_d[0:NOWN, fc * P:(fc + 1) * P],
                            transpose=True).then_inc(xl[k % 4].h, 16)
                        xl[k % 4].inc(16)
                        s.wait_ge(vts, vbase + vt_tern_plan[k])
                        wtrans(s, wq, (k % 4) * 512, wt, k % 4, tq[k % 4])
                        if k + 4 < NCH:
                            o2, f2c = divmod(k + 4, FC)
                            wload(s, wb, k % 4, w2, o2 * 512, f2c * P,
                                  ld[k % 4])

                @block.vector
                def _(v):
                    T = B.TE(v, "vector")

                    def evac(o):
                        v.wait_ge(mm.h, mm.val((o + 1) * (FC + 1) * NT))
                        for i in range(NT):
                            T.tensor_scalar(
                                ffn_acc[:, i * C + o * 512:
                                        i * C + (o + 1) * 512],
                                pp[:, i * 512:(i + 1) * 512], dq2[:, i:i + 1],
                                None, ALU.mult)
                    for o in range(OSL):
                        for fc in range(FC):
                            k = o * FC + fc
                            if fc == 2 and o >= 1:
                                evac(o - 1)
                                assert T.n == vbase + vt_evac_plan[o - 1]
                            v.wait_ge(ld[k % 4].h,
                                      ld[k % 4].val(64 * (k // 4 + 1)))
                            if k >= 4:
                                v.wait_ge(tq[k % 4].h,
                                          tq[k % 4].val(64 * (k // 4)))
                            tern3(T, wb[:, (k % 4) * 512:(k % 4 + 1) * 512],
                                  wt[:, (k % 4) * 512:(k % 4 + 1) * 512],
                                  asc[:, :], 3)
                            assert T.n == vbase + vt_tern_plan[k]
                    evac(OSL - 1)
                    assert T.n == vbase + vt_total

                @block.tensor
                def _(te):
                    te.wait_ge(b2l.h, b2l.val(16))
                    for o in range(OSL):
                        if o >= 1:
                            te.wait_ge(vts, vbase + vt_evac_plan[o - 1])
                        for fc in range(FC):
                            k = o * FC + fc
                            te.wait_ge(tq[k % 4].h,
                                       tq[k % 4].val(64 * (k // 4 + 1)))
                            te.wait_ge(xl[k % 4].h,
                                       xl[k % 4].val(16 * (k // 4 + 1)))
                            for i in range(NT):
                                mi = te.matmul(pp[:, i * 512:(i + 1) * 512],
                                               xb[:, (k % 4) * NOWN + i * P:
                                                  (k % 4) * NOWN + (i + 1) * P],
                                               wq[:, (k % 4) * 512:
                                                  (k % 4 + 1) * 512],
                                               start=(fc == 0), stop=False)
                                if i == NT - 1:
                                    mi.then_inc(mm.h, NT)
                                    mm.inc(NT)
                        for i in range(NT):
                            mi = te.matmul(pp[:, i * 512:(i + 1) * 512],
                                           sr2[0:1, i * P:(i + 1) * P],
                                           b2s[0:1, o * 512:(o + 1) * 512],
                                           start=False, stop=True)
                            if i == NT - 1:
                                mi.then_inc(mm.h, NT)
                                mm.inc(NT)

    def phase_a2(bck, bcq):
        with contextlib.ExitStack() as S:
            t = lambda n, sh, d: S.enter_context(nc.sbuf_tensor(n, sh, d))
            xb = t("a2x", [P, 2 * C], F32)
            sq = t("a2sq", [P, C], F32)
            sb_ = t("a2st", [P, 12], F32)
            scr = {"v": t("a2v", [P, C], F32), "i": t("a2i", [P, C], I32)}
            xqb = t("a2xq", [P, 2 * C], BF16)
            svr = t("a2sv", [1, MT * P], F32)
            ps = S.enter_context(nc.psum_tensor("a2ps", [1, 2 * P], F32))
            pb = S.enter_context(nc.psum_tensor("a2pb", [P, 2 * P], F32))
            dmas = [B.gsem("dmA"), B.gsem("dmB")]
            sto = [B.gsem("stA"), B.gsem("stB")]
            steps = []

            def mk_dma(t_):
                def f(T):
                    T._e.dma_start(out=xb[:, (t_ % 2) * C:(t_ % 2 + 1) * C],
                                   in_=x_all[t_ * P:(t_ + 1) * P, :]
                                   ).then_inc(dmas[t_ % 2].h, 16)
                    dmas[t_ % 2].inc(16)
                return f

            steps.append(("sync", mk_dma(0)))
            for t_ in range(MT):
                if t_ + 1 < MT:
                    steps.append(("sync", mk_dma(t_ + 1)))
                steps.append(("vector",
                              lambda T, s_=dmas[t_ % 2], n=16 * (t_ // 2 + 1):
                              T._e.wait_ge(s_.h, s_.val(n))))
                x_t = xb[:, (t_ % 2) * C:(t_ % 2 + 1) * C]
                steps += ln_steps(x_t, C, sb_, sq, 0, 1, 8, 9)
                steps += qscale_steps(x_t, C, sb_, sq, 0, 1, 4, 8, 9)

                def slabs(T, t_=t_):
                    T.tensor_tensor(sb_[:, 2:3], sb_[:, 1:2], sb_[:, 4:5],
                                    ALU.mult)
                    T.tensor_tensor(sb_[:, 3:4], sb_[:, 0:1], sb_[:, 2:3],
                                    ALU.mult)
                    T.tensor_scalar(sb_[:, 3:4], sb_[:, 3:4], -1.0, None,
                                    ALU.mult)
                    T.tensor_tensor(dqv[:, t_:t_ + 1], sb_[:, 8:9],
                                    wsc[:, 4:5], ALU.mult)
                steps.append(("vector", slabs))
                if t_ >= 2:
                    steps.append(("vector",
                                  lambda T, s_=sto[t_ % 2],
                                  n=16 * ((t_ - 2) // 2 + 1):
                                  T._e.wait_ge(s_.h, s_.val(n))))
                steps += q_steps(x_t, C, sb_[:, 2:3], sb_[:, 3:4],
                                 xqb[:, (t_ % 2) * C:(t_ % 2 + 1) * C], scr)

                def sto_f(T, t_=t_):
                    T._e.dma_start(out=xqa_d[t_ * P:(t_ + 1) * P, :],
                                   in_=xqb[:, (t_ % 2) * C:(t_ % 2 + 1) * C]
                                   ).then_inc(sto[t_ % 2].h, 16)
                    sto[t_ % 2].inc(16)
                steps.append(("scalar", sto_f))
            def wait_all_sto(T):
                for sl in range(2):
                    T._e.wait_ge(sto[sl].h,
                                 sto[sl].val(16 * ((MT - 1 - sl) // 2 + 1)))
            steps.append(("scalar", wait_all_sto))
            for t_ in range(MT):
                steps.append(("tensor", lambda T, t_=t_: T.transpose(
                    ps[0:1, (t_ % 2) * P:(t_ % 2 + 1) * P],
                    dqv[:, t_:t_ + 1], idf[:, :])))
                steps.append(("vector", lambda T, t_=t_: T.tensor_copy(
                    svr[0:1, t_ * P:(t_ + 1) * P],
                    ps[0:1, (t_ % 2) * P:(t_ % 2 + 1) * P])))
            for t_ in range(MT):
                def bc(T, t_=t_):
                    T.matmul(pb[:, (t_ % 2) * P:(t_ % 2 + 1) * P],
                             onesrow[0:1, :], svr[0:1, t_ * P:(t_ + 1) * P],
                             start=True, stop=True)
                steps.append(("tensor", bc))

                def bcev(T, t_=t_):
                    T.tensor_copy(bck[:, t_ * P:(t_ + 1) * P],
                                  pb[:, (t_ % 2) * P:(t_ % 2 + 1) * P])
                    if t_ >= QOFF:
                        T.tensor_scalar(
                            bcq[:, (t_ - QOFF) * P:(t_ - QOFF + 1) * P],
                            pb[:, (t_ % 2) * P:(t_ % 2 + 1) * P],
                            float(cfg.SCALE), None, ALU.mult)
                steps.append(("vector", bcev))
            B.chain("pa2", steps)

    def phase_att(attn_tm, bck, bcq):
        SLAB = cfg.SLAB_S
        KL, soff = cfg.KL, cfg.soff
        MAXKV = MT * P
        NP = 5
        with contextlib.ExitStack() as S:
            t = lambda n, sh, d: S.enter_context(nc.sbuf_tensor(n, sh, d))
            xsl = t("atxs", [P, 4 * NOWN], BF16)
            wvb = t("atwv", [P, 4 * 512], F32)
            wvt = t("atwt", [P, 4 * 512], BF16)
            wvq = t("atwvq", [P, 4 * 512], BF16)
            ascv = t("atas", [P, 512], BF16)
            vbuf = t("atvb", [P, MT * HG * 129], BF16)
            qtm = t("atqtm", [P, NT * HG * P], BF16)
            ktm = t("atktm", [P, MT * HG * P], BF16)
            qfm = t("atqfm", [P, NOWN], BF16)
            kfm = t("atkfm", [P, TALL], BF16)
            pT = t("atpT", [P, 2 * MT * P], BF16)
            msk = t("atms", [P, SLAB], BF16)
            sst = t("atss", [P, MAXKV], F32)
            pS = t("atps", [P, 2 * MAXKV], BF16)
            ssc = xsl
            rms = t("atrm", [P, 8], F32)
            nrm = t("atnr", [P, 2], F32)
            rcd = t("atrcd", [P, 2], F32)
            mld = B.gsem("mld")
            with nc.Block() as block:
                @block.sync
                def _(s):
                    NSL = (SLAB + 8191) // 8192
                    for k in range(NSL):
                        a, b_ = k * 8192, min(SLAB, (k + 1) * 8192)
                        s.dma_start(out=msk[:, a:b_], in_=masks[:, a:b_]
                                    ).then_inc(mld.h, 16)
                        mld.inc(16)
                    s.wait_ge(mld.h, mld.val(16 * NSL))

                @block.vector
                def _(v):
                    v.memset(bass.AP(vbuf, P, [[MT * HG * 129, P],
                                               [129, MT * HG], [1, 1]]), 1.0)

            for g in range(NG):
                qoff, koff, voff = g * HG * P, C + g * HG * P, 2 * C + g * HG * P
                passes = [("q", qoff, QOFF, MT),
                          ("kA", koff, 0, MT // 2), ("kB", koff, MT // 2, MT),
                          ("vA", voff, 0, MT // 2), ("vB", voff, MT // 2, MT)]
                xl = [B.gsem(f"xl{q}") for q in range(4)]
                wl = [B.gsem(f"wl{q}") for q in range(4)]
                tq = [B.gsem(f"tq{q}") for q in range(4)]
                mm = B.gsem("mm")
                vts, vnb = B.ticket("vector")
                vbase = vnb[0]

                def ev_ops(pn):
                    nm, off, t0, t1 = passes[pn]
                    return (t1 - t0) * (HG if nm.startswith("v") else 1)

                vt_tern_plan = {}
                vt_evac_plan = {}
                cnt = 0
                for pn in range(NP):
                    if pn >= 1:
                        cnt += ev_ops(pn - 1)
                        vt_evac_plan[pn - 1] = cnt
                    for cc in range(CC):
                        cnt += 3
                        vt_tern_plan[pn * CC + cc] = cnt
                cnt += ev_ops(NP - 1)
                vt_evac_plan[NP - 1] = cnt
                vt_total = cnt

                with nc.psum_tensor(f"atp{g}", [P, 8 * 512], F32) as pqv, \
                        nc.Block() as block:
                    @block.sync
                    def _(s):
                        for pn, (nm, off, t0, t1) in enumerate(passes):
                            for cc in range(CC):
                                k = pn * CC + cc
                                if k >= 4:
                                    s.wait_ge(mm.h, mm.val(k - 3))
                                s.dma_start(
                                    out=xsl[:, (k % 4) * NOWN:
                                            (k % 4) * NOWN + (t1 - t0) * P],
                                    in_=xqa_d[t0 * P:t1 * P,
                                              cc * P:(cc + 1) * P],
                                    transpose=True).then_inc(xl[k % 4].h, 16)
                                xl[k % 4].inc(16)
                                s.wait_ge(vts, vbase + vt_tern_plan[k])
                                wtrans(s, wvq, (k % 4) * 512, wvt, k % 4,
                                       tq[k % 4])

                    @block.scalar
                    def _(s):
                        NKQ = NP * CC
                        for k in range(min(4, NKQ)):
                            pn, cc = divmod(k, CC)
                            wload(s, wvb, k % 4, w_qkv, passes[pn][1],
                                  cc * P, wl[k % 4])
                        for k in range(NKQ):
                            if k + 4 < NKQ:
                                s.wait_ge(vts, vbase + vt_tern_plan[k])
                                pn2, cc2 = divmod(k + 4, CC)
                                wload(s, wvb, k % 4, w_qkv, passes[pn2][1],
                                      cc2 * P, wl[k % 4])

                    @block.vector
                    def _(v):
                        T = B.TE(v, "vector")

                        def evac(pn):
                            nm, off, t0, t1 = passes[pn]
                            v.wait_ge(mm.h, mm.val((pn + 1) * CC))
                            for mt in range(t0, t1):
                                psl_ = pqv[:, (mt - t0) * 512:
                                           (mt - t0 + 1) * 512]
                                if nm.startswith("v"):
                                    for hh in range(HG):
                                        T.tensor_scalar(
                                            vbuf[:, mt * HG * 129 + hh * 129:
                                                 mt * HG * 129 + hh * 129 + P],
                                            psl_[:, hh * P:(hh + 1) * P],
                                            dqv[:, mt:mt + 1], None, ALU.mult)
                                elif nm.startswith("k"):
                                    T.tensor_copy(
                                        ktm[:, mt * HG * P:(mt + 1) * HG * P],
                                        psl_)
                                else:
                                    T.tensor_copy(
                                        qtm[:, (mt - t0) * HG * P:
                                            (mt - t0 + 1) * HG * P], psl_)

                        for pn in range(NP):
                            if pn >= 1:
                                evac(pn - 1)
                                assert T.n == vbase + vt_evac_plan[pn - 1]
                            for cc in range(CC):
                                k = pn * CC + cc
                                v.wait_ge(wl[k % 4].h,
                                          wl[k % 4].val(64 * (k // 4 + 1)))
                                if k >= 4:
                                    v.wait_ge(tq[k % 4].h,
                                              tq[k % 4].val(64 * (k // 4)))
                                tern3(T,
                                      wvb[:, (k % 4) * 512:(k % 4 + 1) * 512],
                                      wvt[:, (k % 4) * 512:(k % 4 + 1) * 512],
                                      ascv[:, :], 5)
                                assert T.n == vbase + vt_tern_plan[k]
                        evac(NP - 1)
                        assert T.n == vbase + vt_total

                    @block.tensor
                    def _(te):
                        for pn, (nm, off, t0, t1) in enumerate(passes):
                            if pn >= 1:
                                te.wait_ge(vts, vbase + vt_evac_plan[pn - 1])
                            for cc in range(CC):
                                k = pn * CC + cc
                                te.wait_ge(tq[k % 4].h,
                                           tq[k % 4].val(64 * (k // 4 + 1)))
                                te.wait_ge(xl[k % 4].h,
                                           xl[k % 4].val(16 * (k // 4 + 1)))
                                for mt in range(t0, t1):
                                    mi = te.matmul(
                                        pqv[:, (mt - t0) * 512:
                                            (mt - t0 + 1) * 512],
                                        xsl[:, (k % 4) * NOWN + (mt - t0) * P:
                                            (k % 4) * NOWN +
                                            (mt - t0 + 1) * P],
                                        wvq[:, (k % 4) * 512:
                                            (k % 4 + 1) * 512],
                                        start=(cc == 0), stop=(cc == CC - 1))
                                    if mt == t1 - 1:
                                        mi.then_inc(mm.h, 1)
                                        mm.inc(1)

                nq, nk = (NT + 7) // 8, (MT + 7) // 8
                VQ0 = nq + nk
                NCH = [(KL[i] * P + 511) // 512 for i in range(NT)]
                TBn = [(KL[i] + 7) // 8 for i in range(NT)]
                vst_end = [0] * NT
                vpte_end = [0] * NT
                vpav_end = [0] * NT
                cnt2 = VQ0
                for i in range(NT):
                    cnt2 += 2 * NCH[i] + 2
                    vst_end[i] = cnt2
                    cnt2 += TBn[i]
                    vpte_end[i] = cnt2
                    cnt2 += 2
                    vpav_end[i] = cnt2
                VTOT = cnt2
                mm_sc_s = [0] * NT
                mm_av_e = [0] * NT
                m2 = 0
                for i in range(NT):
                    mm_sc_s[i] = m2
                    m2 += NCH[i] + 1
                    mm_av_e[i] = m2
                evac_vop = [b + 1 for b in range(VQ0)]
                tse_end = [0] * NT
                tb_c = VQ0
                for i in range(NT):
                    for tx in range(TBn[i]):
                        evac_vop.append(vst_end[i] + tx + 1)
                    tb_c += TBn[i]
                    tse_end[i] = tb_c

                with nc.psum_tensor(f"asc{g}", [P, 4 * 512], F32) as psS, \
                        nc.psum_tensor(f"aav{g}", [P, 2 * 512], F32) as pav, \
                        nc.psum_tensor(f"atr{g}", [P, 16 * P], BF16) as ptr:
                    for hh in range(HG):
                        h = g * HG + hh
                        tsem = B.gsem("tsem")
                        mmh = B.gsem("mmh")
                        vts2, vnb2 = B.ticket("vector")
                        ats2, anb2 = B.ticket("scalar")
                        v2b, a2b = vnb2[0], anb2[0]
                        with nc.Block() as block:
                            @block.tensor
                            def _(te):
                                tb = 0
                                for src_, n_ in ((qtm, NT), (ktm, MT)):
                                    for i in range(n_):
                                        if tb >= 2 and i % 8 == 0:
                                            te.wait_ge(vts2,
                                                       v2b + evac_vop[tb - 2])
                                        ti = te.transpose(
                                            ptr[:, (tb % 2) * 8 * P +
                                                (i % 8) * P:
                                                (tb % 2) * 8 * P +
                                                (i % 8 + 1) * P],
                                            src_[:, i * HG * P + hh * P:
                                                 i * HG * P + (hh + 1) * P],
                                            idb[:, :])
                                        if i % 8 == 7 or i == n_ - 1:
                                            ti.then_inc(tsem.h, 1)
                                            tsem.inc(1)
                                            tb += 1
                                for i in range(NT):
                                    kvlen = KL[i] * P
                                    te.wait_ge(vts2, v2b + (vst_end[i - 1]
                                                            if i else VQ0))
                                    for c in range(NCH[i]):
                                        wc = min(512, kvlen - c * 512)
                                        te.matmul(
                                            psS[:, c * 512:c * 512 + wc],
                                            qfm[:, i * P:(i + 1) * P],
                                            kfm[:, c * 512:c * 512 + wc],
                                            start=True, stop=True
                                            ).then_inc(mmh.h, 1)
                                        mmh.inc(1)
                                    te.wait_ge(ats2, a2b + i + 1)
                                    for j in range(KL[i]):
                                        if tb >= 2 and j % 8 == 0:
                                            te.wait_ge(vts2,
                                                       v2b + evac_vop[tb - 2])
                                        ti = te.transpose(
                                            ptr[:, (tb % 2) * 8 * P +
                                                (j % 8) * P:
                                                (tb % 2) * 8 * P +
                                                (j % 8 + 1) * P],
                                            pS[:, (i % 2) * MAXKV + j * P:
                                               (i % 2) * MAXKV + (j + 1) * P],
                                            idb[:, :])
                                        if j % 8 == 7 or j == KL[i] - 1:
                                            ti.then_inc(tsem.h, 1)
                                            tsem.inc(1)
                                            tb += 1
                                    te.wait_ge(vts2, v2b + vpte_end[i])
                                    if i >= 2:
                                        te.wait_ge(vts2, v2b + vpav_end[i - 2])
                                    for j in range(KL[i]):
                                        mmi = te.matmul(
                                            pav[:, (i % 2) * 512:
                                                (i % 2) * 512 + 129],
                                            pT[:, (i % 2) * MAXKV + j * P:
                                               (i % 2) * MAXKV + (j + 1) * P],
                                            vbuf[:, j * HG * 129 + hh * 129:
                                                 j * HG * 129 +
                                                 (hh + 1) * 129],
                                            start=(j == 0),
                                            stop=(j == KL[i] - 1))
                                        if j == KL[i] - 1:
                                            mmi.then_inc(mmh.h, 1)
                                            mmh.inc(1)

                            @block.vector
                            def _(v):
                                T = B.TE(v, "vector")
                                tb = 0
                                for bq in range(nq):
                                    v.wait_ge(tsem.h, tsem.val(tb + 1))
                                    nb = min(8, NT - bq * 8)
                                    T.tensor_tensor(
                                        qfm[:, bq * 8 * P:bq * 8 * P + nb * P],
                                        ptr[:, (tb % 2) * 8 * P:
                                            (tb % 2) * 8 * P + nb * P],
                                        bcq[:, bq * 8 * P:bq * 8 * P + nb * P],
                                        ALU.mult)
                                    tb += 1
                                for bk in range(nk):
                                    v.wait_ge(tsem.h, tsem.val(tb + 1))
                                    nb = min(8, MT - bk * 8)
                                    T.tensor_tensor(
                                        kfm[:, bk * 8 * P:bk * 8 * P + nb * P],
                                        ptr[:, (tb % 2) * 8 * P:
                                            (tb % 2) * 8 * P + nb * P],
                                        bck[:, bk * 8 * P:bk * 8 * P + nb * P],
                                        ALU.mult)
                                    tb += 1
                                assert T.n == v2b + VQ0
                                for i in range(NT):
                                    kvlen = KL[i] * P
                                    for c in range(NCH[i]):
                                        wc = min(512, kvlen - c * 512)
                                        v.wait_ge(mmh.h,
                                                  mmh.val(mm_sc_s[i] + c + 1))
                                        if c == 0 and i >= 1:
                                            v.wait_ge(ats2, a2b + i)
                                        T.scalar_tensor_tensor(
                                            sst[:, c * 512:c * 512 + wc],
                                            psS[:, c * 512:c * 512 + wc],
                                            1.0,
                                            msk[:, soff[i] + c * 512:
                                                soff[i] + c * 512 + wc],
                                            ALU.mult, ALU.add)
                                        T.tensor_scalar(
                                            ssc[:, 0:wc],
                                            sst[:, c * 512:c * 512 + wc],
                                            1.0, None, ALU.mult, ALU.max,
                                            accum_out=rms[:, c:c + 1])
                                    T.reduce_max(nrm[:, 0:1],
                                                 rms[:, 0:NCH[i]], axis=AX)
                                    T.tensor_scalar(nrm[:, 1:2], nrm[:, 0:1],
                                                    -1.0, None, ALU.mult)
                                    assert T.n == v2b + vst_end[i]
                                    for tx in range(TBn[i]):
                                        v.wait_ge(tsem.h,
                                                  tsem.val(tse_end[i] -
                                                           TBn[i] + tx + 1))
                                        if tx == 0 and i >= 2:
                                            v.wait_ge(mmh.h,
                                                      mmh.val(mm_av_e[i - 2]))
                                        nb = min(8, KL[i] - tx * 8)
                                        tbg = tse_end[i] - TBn[i] + tx
                                        T.tensor_copy(
                                            pT[:, (i % 2) * MAXKV + tx * 8 * P:
                                               (i % 2) * MAXKV +
                                               tx * 8 * P + nb * P],
                                            ptr[:, (tbg % 2) * 8 * P:
                                                (tbg % 2) * 8 * P + nb * P])
                                    assert T.n == v2b + vpte_end[i]
                                    v.wait_ge(mmh.h, mmh.val(mm_av_e[i]))
                                    T.reciprocal(
                                        rcd[:, 0:1],
                                        pav[:, (i % 2) * 512 + P:
                                            (i % 2) * 512 + 129])
                                    T.tensor_scalar(
                                        attn_tm[:, i * C + h * P:
                                                i * C + (h + 1) * P],
                                        pav[:, (i % 2) * 512:
                                            (i % 2) * 512 + P],
                                        rcd[:, 0:1], None, ALU.mult)
                                    assert T.n == v2b + vpav_end[i]
                                assert T.n == v2b + VTOT

                            @block.scalar
                            def _(s):
                                T = B.TE(s, "scalar")
                                for i in range(NT):
                                    kvlen = KL[i] * P
                                    s.wait_ge(vts2, v2b + vst_end[i])
                                    if i >= 2:
                                        s.wait_ge(tsem.h,
                                                  tsem.val(tse_end[i - 2]))
                                    T.activation(
                                        pS[:, (i % 2) * MAXKV:
                                           (i % 2) * MAXKV + kvlen],
                                        sst[:, 0:kvlen], AF.Exp,
                                        bias=nrm[:, 1:2], scale=1.0)
                                assert T.n == a2b + NT

    def phase_qp(ffn_acc, attn_tm):
        OSL = C // 512
        with contextlib.ExitStack() as S:
            t = lambda n, sh, d: S.enter_context(nc.sbuf_tensor(n, sh, d))
            xqpT = t("xqpT", [P, CC * NOWN], BF16)
            sq = t("qpsq", [P, C], F32)
            sb_ = t("qpst", [P, 12], F32)
            scr = {"v": t("qpv", [P, C], F32), "i": t("qpi", [P, C], I32)}
            xqb = t("qpxq", [P, C], BF16)
            S2 = contextlib.ExitStack()
            ptq = S2.enter_context(nc.psum_tensor("qppt", [P, 8 * P], BF16))
            steps = []
            for i in range(NT):
                a_t = attn_tm[:, i * C:(i + 1) * C]

                def mk0(T):
                    T.memset(sb_[:, 0:1], 0.0)
                    T.memset(sb_[:, 1:2], 1.0)
                steps.append(("vector", mk0))
                steps += qscale_steps(a_t, C, sb_, sq, 0, 1, 4, 8, 9)

                def slabs(T, i=i):
                    T.tensor_copy(sb_[:, 2:3], sb_[:, 4:5])
                    T.memset(sb_[:, 3:4], 0.0)
                    T.tensor_tensor(dqp[:, i:i + 1], sb_[:, 8:9], wsc[:, 6:7],
                                    ALU.mult)
                steps.append(("vector", slabs))
                steps += q_steps(a_t, C, sb_[:, 2:3], sb_[:, 3:4], xqb[:, :],
                                 scr)
                for g0 in range(0, CC, 8):
                    nb = min(8, CC - g0)

                    def tr(T, g0=g0, nb=nb):
                        for kx in range(nb):
                            T.transpose(ptq[:, kx * P:(kx + 1) * P],
                                        xqb[:, (g0 + kx) * P:
                                            (g0 + kx + 1) * P],
                                        idb[:, :])
                    steps.append(("tensor", tr))

                    def evt(T, i=i, g0=g0, nb=nb):
                        T.tensor_copy(
                            bass.AP(xqpT, g0 * NOWN + i * P,
                                    [[CC * NOWN, P], [NOWN, nb], [1, P]]),
                            ptq[:, 0:nb * P])
                    steps.append(("vector", evt))
            B.chain("pqpq", steps)
            S2.close()

            wb = t("qpw", [P, 2 * 512], F32)
            wt = t("qpwt", [P, 2 * 512], BF16)
            wq = t("qpwq", [P, 2 * 512], BF16)
            asc = t("qpas", [P, 512], BF16)
            xrb = t("qpxr", [P, 4 * 512], F32)
            ob = t("qpo", [P, 4 * 512], F32)
            pp = S.enter_context(nc.psum_tensor("qpp", [P, 8 * 512], F32))
            ld = [B.gsem("ldA"), B.gsem("ldB")]
            tq = [B.gsem("tqA"), B.gsem("tqB")]
            xr = [B.gsem(f"xg{q}") for q in range(4)]
            so = [B.gsem(f"so{q}") for q in range(4)]
            mm = B.gsem("mm")
            vts, vnb = B.ticket("vector")
            vbase = vnb[0]
            NCH = OSL * CC
            vt_tern_plan = {}
            vt_evac_plan = {}
            cnt = 0
            for o in range(OSL):
                for cc in range(CC):
                    if cc == 2 and o >= 1:
                        cnt += 2 * NT
                        vt_evac_plan[o - 1] = cnt
                    cnt += 3
                    vt_tern_plan[o * CC + cc] = cnt
            cnt += 2 * NT
            vt_evac_plan[OSL - 1] = cnt
            vt_total = cnt

            with nc.Block() as block:
                @block.sync
                def _(s):
                    for k in range(min(2, NCH)):
                        o, cc = divmod(k, CC)
                        wload(s, wb, k % 2, w_proj, o * 512, cc * P,
                              ld[k % 2])
                    for k in range(NCH):
                        s.wait_ge(vts, vbase + vt_tern_plan[k])
                        if k >= 2:
                            s.wait_ge(mm.h, mm.val((k - 1) * NT))
                        wtrans(s, wq, (k % 2) * 512, wt, k % 2, tq[k % 2])
                        if k + 2 < NCH:
                            o2, c2 = divmod(k + 2, CC)
                            wload(s, wb, k % 2, w_proj, o2 * 512, c2 * P,
                                  ld[k % 2])

                @block.gpsimd
                def _(g):
                    for o in range(OSL):
                        for i in range(NT):
                            k2 = o * NT + i
                            if k2 >= 4:
                                op, oi = divmod(k2 - 4, NT)
                                g.wait_ge(vts, vbase + vt_evac_plan[op]
                                          - 2 * NT + 2 * (oi + 1))
                            g.dma_start(
                                out=xrb[:, (k2 % 4) * 512:(k2 % 4 + 1) * 512],
                                in_=x_all[NOWN + i * P:NOWN + (i + 1) * P,
                                          o * 512:(o + 1) * 512]
                                ).then_inc(xr[k2 % 4].h, 16)
                            xr[k2 % 4].inc(16)

                @block.vector
                def _(v):
                    T = B.TE(v, "vector")

                    def evacs(o):
                        v.wait_ge(mm.h, mm.val((o + 1) * CC * NT))
                        for i in range(NT):
                            k2 = o * NT + i
                            v.wait_ge(xr[k2 % 4].h,
                                      xr[k2 % 4].val(16 * (k2 // 4 + 1)))
                            if k2 >= 4:
                                v.wait_ge(so[k2 % 4].h,
                                          so[k2 % 4].val(16 * (k2 // 4)))
                            psl_ = pp[:, i * 512:(i + 1) * 512]
                            osl_ = ob[:, (k2 % 4) * 512:(k2 % 4 + 1) * 512]
                            T.scalar_tensor_tensor(
                                osl_, psl_, dqp[:, i:i + 1],
                                ffn_acc[:, i * C + o * 512:
                                        i * C + (o + 1) * 512],
                                ALU.mult, ALU.add)
                            T.tensor_tensor(
                                osl_, osl_,
                                xrb[:, (k2 % 4) * 512:(k2 % 4 + 1) * 512],
                                ALU.add)

                    for o in range(OSL):
                        for cc in range(CC):
                            k = o * CC + cc
                            if cc == 2 and o >= 1:
                                evacs(o - 1)
                                assert T.n == vbase + vt_evac_plan[o - 1]
                            v.wait_ge(ld[k % 2].h,
                                      ld[k % 2].val(64 * (k // 2 + 1)))
                            if k >= 2:
                                v.wait_ge(tq[k % 2].h,
                                          tq[k % 2].val(64 * (k // 2)))
                            tern3(T, wb[:, (k % 2) * 512:(k % 2 + 1) * 512],
                                  wt[:, (k % 2) * 512:(k % 2 + 1) * 512],
                                  asc[:, :], 7)
                            assert T.n == vbase + vt_tern_plan[k]
                    evacs(OSL - 1)
                    assert T.n == vbase + vt_total

                @block.tensor
                def _(te):
                    for o in range(OSL):
                        if o >= 1:
                            te.wait_ge(vts, vbase + vt_evac_plan[o - 1])
                        for cc in range(CC):
                            k = o * CC + cc
                            te.wait_ge(tq[k % 2].h,
                                       tq[k % 2].val(64 * (k // 2 + 1)))
                            for i in range(NT):
                                mi = te.matmul(
                                    pp[:, i * 512:(i + 1) * 512],
                                    xqpT[:, cc * NOWN + i * P:
                                         cc * NOWN + (i + 1) * P],
                                    wq[:, (k % 2) * 512:(k % 2 + 1) * 512],
                                    start=(cc == 0), stop=(cc == CC - 1))
                                if i == NT - 1:
                                    mi.then_inc(mm.h, NT)
                                    mm.inc(NT)

                @block.scalar
                def _(s):
                    for o in range(OSL):
                        for i in range(NT):
                            k2 = o * NT + i
                            s.wait_ge(vts, vbase + vt_evac_plan[o]
                                      - 2 * NT + 2 * (i + 1))
                            s.dma_start(out=out[i * P:(i + 1) * P,
                                                o * 512:(o + 1) * 512],
                                        in_=ob[:, (k2 % 4) * 512:
                                               (k2 % 4 + 1) * 512]
                                        ).then_inc(so[k2 % 4].h, 16)
                            so[k2 % 4].inc(16)
                    NK2 = OSL * NT
                    for sl in range(4):
                        s.wait_ge(so[sl].h,
                                  so[sl].val(16 * ((NK2 - 1 - sl) // 4 + 1)))

    LIM = globals().get("_PHASE_LIMIT", 99)
    MASK = globals().get("_PHASE_MASK", None)
    k_ = [0]

    def go(fn, *a):
        if (MASK is not None and k_[0] in MASK) or \
                (MASK is None and k_[0] < LIM):
            fn(*a)
        k_[0] += 1

    with nc.sbuf_tensor("xqf1T", [P, CC * NOWN], BF16) as xqf1T:
        go(weight_scale, "pw1", w1, F, C, 1e-8, 0)
        go(phase_a1, xqf1T)
        go(phase_f1, xqf1T)
    with (
        nc.sbuf_tensor("ffn_acc", [P, NT * C], BF16) as ffn_acc,
        nc.sbuf_tensor("attn_tm", [P, NT * C], BF16) as attn_tm,
    ):
        go(weight_scale, "pw2", w2, C, F, 1e-8, 2)
        go(phase_h)
        go(phase_f2, ffn_acc)
        go(weight_scale, "pwq", w_qkv, 3 * C, C, 1e-5, 4)
        with (
            nc.sbuf_tensor("bck", [P, TALL], F32) as bck,
            nc.sbuf_tensor("bcq", [P, NOWN], F32) as bcq,
        ):
            go(phase_a2, bck, bcq)
            go(phase_att, attn_tm, bck, bcq)
        if globals().get("_DEBUG_TAPS"):
            atm_d = nc.dram_tensor("atm_d", [P, NT * C], BF16,
                                   kind="ExternalOutput")
            fac_d = nc.dram_tensor("fac_d", [P, NT * C], BF16,
                                   kind="ExternalOutput")
            with nc.Block() as block:
                ts_ = B.gsem("tapx")

                @block.sync
                def _(s):
                    s.dma_start(out=atm_d[:, :], in_=attn_tm[:, :]
                                ).then_inc(ts_.h, 16)
                    ts_.inc(16)
                    s.dma_start(out=fac_d[:, :], in_=ffn_acc[:, :]
                                ).then_inc(ts_.h, 16)
                    ts_.inc(16)
                    s.wait_ge(ts_.h, ts_.val(32))
        go(weight_scale, "pwp", w_proj, C, C, 1e-5, 6)
        go(phase_qp, ffn_acc, attn_tm)

    if MASK is not None or LIM < 99:
        with nc.sbuf_tensor("fin", [P, C], F32) as fin, nc.Block() as block:
            fs = B.gsem("fin")

            @block.sync
            def _(s):
                for i in range(NT):
                    s.dma_start(out=fin[:, :],
                                in_=x_all[NOWN + i * P:NOWN + (i + 1) * P, :]
                                ).then_inc(fs.h, 16)
                    fs.inc(16)
                    s.wait_ge(fs.h, fs.val(32 * i + 16))
                    s.dma_start(out=out[i * P:(i + 1) * P, :], in_=fin[:, :]
                                ).then_inc(fs.h, 16)
                    fs.inc(16)
                    s.wait_ge(fs.h, fs.val(32 * i + 32))

    B.stack.close()
    return nc




def make_masks(cfg: Cfg, half: int) -> np.ndarray:
    m = np.zeros((P, cfg.SLAB_S), dtype=np.float32)
    for i in range(cfg.NT):
        kv = np.arange(cfg.KL[i] * P)[None, :]
        gkv = np.where(kv < cfg.NOWN, (1 - half) * cfg.NOWN + kv,
                       half * cfg.NOWN + (kv - cfg.NOWN))
        gq = half * cfg.NOWN + i * P + np.arange(P)[:, None]
        blk = np.where(gkv <= gq, 0.0, NEG).astype(np.float32)
        m[:, cfg.soff[i]:cfg.soff[i] + cfg.KL[i] * P] = blk
    return m.astype(ml_dtypes.bfloat16)


def stage_inputs(cfg: Cfg, x, wqkv, wproj, w1v, b1v, w2v, b2v):
    idb = np.eye(P).astype(ml_dtypes.bfloat16)
    idf = np.eye(P).astype(np.float32)
    in_maps = []
    nb = x.shape[0]
    for core in range(2 * nb):
        b, half = core // 2, core % 2
        partner = x[b, (1 - half) * cfg.NOWN:(2 - half) * cfg.NOWN]
        own = x[b, half * cfg.NOWN:(half + 1) * cfg.NOWN]
        x_all = np.ascontiguousarray(
            np.concatenate([partner, own], axis=0)).astype(np.float32)
        in_maps.append({
            "x_all": x_all,
            "w_qkv": wqkv, "w_proj": wproj,
            "w1": w1v, "b1": b1v.reshape(1, -1),
            "w2": w2v, "b2": b2v.reshape(1, -1),
            "masks": make_masks(cfg, half),
            "ident_b": idb, "ident_f": idf,
        })
    return in_maps


def kernel(L0_attn_x, L0_attn_w_qkv, L0_attn_w_proj, L0_ffn_w1, L0_ffn_b1,
           L0_ffn_w2, L0_ffn_b2):
    cfg = Cfg(C=2048, F=8192, H=16, NOWN=1024)
    x = np.ascontiguousarray(np.asarray(L0_attn_x, dtype=np.float32))
    nc = build(cfg)
    in_maps = stage_inputs(cfg, x,
                           np.asarray(L0_attn_w_qkv, dtype=np.float32),
                           np.asarray(L0_attn_w_proj, dtype=np.float32),
                           np.asarray(L0_ffn_w1, dtype=np.float32),
                           np.asarray(L0_ffn_b1, dtype=np.float32),
                           np.asarray(L0_ffn_w2, dtype=np.float32),
                           np.asarray(L0_ffn_b2, dtype=np.float32))
    res = run_bass_kernel_spmd(nc, in_maps, core_ids=list(range(len(in_maps))))
    B_, S, C = x.shape
    outv = np.empty((B_, S, C), dtype=np.float32)
    for core, r in enumerate(res.results):
        b, half = core // 2, core % 2
        outv[b, half * cfg.NOWN:(half + 1) * cfg.NOWN] = r["out"]
    return outv
